# version 12
# baseline (speedup 1.0000x reference)
import numpy as np
import ml_dtypes
from contextlib import ExitStack

import concourse.bass as bass
import concourse.mybir as mybir
from concourse.bass_core_rust import IndirectOffsetOnAxis
from concourse.bass_utils import run_bass_kernel_spmd

F32 = mybir.dt.float32
BF16 = mybir.dt.bfloat16
I32 = mybir.dt.int32
AF = mybir.ActivationFunctionType
ALU = mybir.AluOpType
AX = mybir.AxisListType
BF16NP = ml_dtypes.bfloat16

NCORES = 8
T = 2048
D = 1024
DT = D // 128
NT = T // 128
NH = 16
HD = 64
CTX = 4096
F0 = 4096
NF = F0 // 128
H1P = 5504
HT1 = H1P // 128
CAP = 512
NSLOT = 4096
LNEPS = 1e-5
GELU_C = 0.044715
GELU_S = 1.5957691216

_UID = [0]


def _sb(nc, name, shape, dtype):
    _UID[0] += 1
    return nc.sbuf_tensor(f"{name}_u{_UID[0]}", shape, dtype)


def _ps(nc, name, shape, dtype):
    _UID[0] += 1
    return nc.psum_tensor(f"{name}_u{_UID[0]}", shape, dtype)


ENGINES = ("sync", "gpsimd", "tensor", "vector", "scalar")


class Prog:

    def __init__(self, nc):
        self.nc = nc
        self.ops = {e: [] for e in ENGINES}
        self.cnt = {}
        self.sems = {}

    def sem(self, st, name):
        self.sems[name] = st.enter_context(self.nc.semaphore(name))
        self.cnt[name] = 0

    def dsem(self, name):
        if name not in self.sems:
            self.sems[name] = self._st.enter_context(self.nc.semaphore(name))
            self.cnt[name] = 0
        return name

    STRICT = {"vector": "v", "scalar": "s"}

    def do(self, eng, fn, inc=None):
        if inc is None and eng in self.STRICT:
            inc = (self.STRICT[eng], 1)
        if inc is None:
            self.ops[eng].append(fn)
            return None
        sname, n = inc
        val = self.cnt[sname] = self.cnt[sname] + n
        sem = self.sems[sname]
        self.ops[eng].append(lambda e, fn=fn, sem=sem, n=n: fn(e).then_inc(sem, n))
        if eng in self.STRICT and sname == self.STRICT[eng]:
            self.wait(eng, sname, val)
        return val

    def wait(self, eng, sname, val):
        if val is None:
            return
        sem = self.sems[sname]
        self.ops[eng].append(lambda e, sem=sem, val=val: e.wait_ge(sem, val))

    def barrier(self):
        names = {"sync": "b_sy", "gpsimd": "b_g", "tensor": "b_t",
                 "vector": "b_v", "scalar": "b_s"}
        vals = {}
        for eng, sn in names.items():
            vals[eng] = self.do(eng, lambda e: e.nop(), inc=(sn, 1))
        for eng in ENGINES:
            for other, sn in names.items():
                if other != eng:
                    self.wait(eng, sn, vals[other])

    def run(self, block):
        for eng in ENGINES:
            ops = self.ops[eng]

            def body(e, ops=ops):
                for f in ops:
                    f(e)
            getattr(block, eng)(body)


def mk_prog(nc, st):
    p = Prog(nc)
    p._st = st
    for s in ["hdma", "gdma", "t", "v", "s", "g",
              "b_sy", "b_g", "b_t", "b_v", "b_s"]:
        p.sem(st, s)
    return p


def emit_ln_T(p, xn, ps, onesP, sq, mrow, vrow, mb, rb, epsc):
    CH = 512
    lastv = None
    sqslot_t = [None, None]
    p.wait("scalar", "v", p.cnt["v"])
    for tc in range(T // CH):
        tv = None
        for dd in range(DT):
            xs = xn[:, dd * T + tc * CH: dd * T + tc * CH + CH]
            ssl = sq[:, (dd % 2) * CH:(dd % 2) * CH + CH]
            p.wait("scalar", "t", sqslot_t[dd % 2])
            vv = p.do("scalar", lambda e, xs=xs, ssl=ssl: e.activation(
                out=ssl, in_=xs, func=AF.Square, bias=0.0, scale=1.0), inc=("s", 1))
            p.wait("tensor", "s", vv)
            p.do("tensor", lambda e, xs=xs, dd=dd: e.matmul(
                ps[0][0:1, :], onesP[:, 0:1], xs, start=(dd == 0), stop=(dd == DT - 1)))
            tv = p.do("tensor", lambda e, dd=dd, ssl=ssl: e.matmul(
                ps[1][0:1, :], onesP[:, 0:1], ssl, start=(dd == 0), stop=(dd == DT - 1)),
                inc=("t", 1))
            sqslot_t[dd % 2] = tv
        p.wait("vector", "t", tv)
        p.do("vector", lambda e: e.tensor_scalar(
            out=mrow[0:1, :], in0=ps[0][0:1, :], scalar1=1.0 / D, scalar2=None, op0=ALU.mult))
        p.do("vector", lambda e: e.tensor_scalar(
            out=vrow[0:1, :], in0=ps[1][0:1, :], scalar1=1.0 / D, scalar2=None, op0=ALU.mult))
        p.do("vector", lambda e: e.tensor_mul(out=sq[0:1, 0:512], in0=mrow[0:1, :],
                                              in1=mrow[0:1, :]))
        sv = p.do("vector", lambda e: e.tensor_sub(
            out=vrow[0:1, :], in0=vrow[0:1, :], in1=sq[0:1, 0:512]), inc=("v", 1))
        p.wait("scalar", "v", sv)
        p.do("scalar", lambda e: e.activation(out=vrow[0:1, :], in_=vrow[0:1, :],
                                              func=AF.Ln, bias=epsc[0:1, :], scale=1.0))
        ssv = p.do("scalar", lambda e: e.activation(out=vrow[0:1, :], in_=vrow[0:1, :],
                                                    func=AF.Exp, bias=0.0, scale=-0.5),
                   inc=("s", 1))
        p.wait("tensor", "s", ssv)
        p.do("tensor", lambda e: e.matmul(ps[2][:], onesP[0:1, :], mrow[0:1, :],
                                          start=True, stop=True))
        tv = p.do("tensor", lambda e: e.matmul(ps[3][:], onesP[0:1, :], vrow[0:1, :],
                                               start=True, stop=True), inc=("t", 1))
        p.wait("vector", "t", tv)
        p.do("vector", lambda e: e.tensor_copy(out=mb[:], in_=ps[2][:]))
        cv = p.do("vector", lambda e: e.tensor_copy(out=rb[:], in_=ps[3][:]), inc=("v", 1))
        for dd in range(DT):
            xs = xn[:, dd * T + tc * CH: dd * T + tc * CH + CH]
            p.do("vector", lambda e, xs=xs: e.tensor_sub(out=xs, in0=xs, in1=mb[:]))
            lastv = p.do("vector", lambda e, xs=xs: e.tensor_mul(out=xs, in0=xs, in1=rb[:]),
                         inc=("v", 1))
        p.wait("tensor", "v", cv)
    return lastv



IO_ATTN = ["xT", "xprefT", "pbias", "wq", "wk", "wv", "wo", "bq", "bk",
           "bvr", "bo", "triu", "yT"]


def emit_attn(nc, p, io, pref_wait=None, use_prefix=True):
    (xT, xprefT, pbias, wq, wk, wv, wo, bq, bk, bvr, bo, triu, yT) = (
        io[k] for k in IO_ATTN)
    with ExitStack() as st:
        en = st.enter_context
        xbf = en(_sb(nc, "xbf", [128, DT * T], BF16))
        OT = en(_sb(nc, "OT", [128, DT * T], BF16))
        bq_sb = en(_sb(nc, "bq_sb", [128, DT], F32))
        bk_sb = en(_sb(nc, "bk_sb", [128, DT], F32))
        bo_sb = en(_sb(nc, "bo_sb", [128, DT], F32))
        pb_sb = en(_sb(nc, "pb_sb", [128, 2], F32))
        triu_sb = en(_sb(nc, "triu_sb", [128, 128], BF16))
        onesP = en(_sb(nc, "onesP", [128, 128], F32))
        epsc = en(_sb(nc, "epsc", [128, 1], F32))
        scrA = en(_sb(nc, "scrA", [128, 3072], F32))
        bcb = scrA[0:64, 0:2048]
        rr = scrA[64:65, 0:2048]
        bvrow = scrA[64:65, 2048:3072]
        mrow = scrA[0:1, 2048:2560]
        vrow = scrA[0:1, 2560:3072]

        ps_s = en(_ps(nc, "ps_s", [128, 2048], F32))
        ps_o = [en(_ps(nc, f"ps_o{i}", [128, 512], F32)) for i in range(4)]

        vc = p.do("gpsimd", lambda e: e.memset(onesP[:], 1.0), inc=("g", 1))
        vc = p.do("gpsimd", lambda e: e.memset(epsc[:], LNEPS), inc=("g", 1))
        vc = p.do("gpsimd", lambda e: e.memset(pb_sb[:, 1:2], 0.0), inc=("g", 1))
        p.dsem("c_d"); p.dsem("xbf_d")
        vsm = None
        for dst, src in [(bq_sb[:], bq), (bk_sb[:], bk), (bo_sb[:], bo),
                         (triu_sb[:], triu), (pb_sb[:, 0:1], pbias), (bvrow, bvr)]:
            vsm = p.do("sync", lambda e, dst=dst, src=src: e.dma_start(out=dst, in_=src),
                       inc=("c_d", 16))
        vxc = None
        for k in range(DT):
            vxc = p.do("gpsimd", lambda e, k=k: e.dma_start(
                out=xbf[:, k * T:(k + 1) * T], in_=xT[k * 128:(k + 1) * 128, :]),
                inc=("xbf_d", 16))
        for eng in ("tensor", "vector", "scalar"):
            p.wait(eng, "xbf_d", vxc)
            p.wait(eng, "c_d", vsm)
            p.wait(eng, "g", vc)

        with ExitStack() as st2:
            en2 = st2.enter_context
            xpbf = en2(_sb(nc, "xpbf", [128, DT * T], BF16))
            KT = en2(_sb(nc, "KT", [128, 4 * CTX], BF16))
            Vaug = en2(_sb(nc, "Vaug", [128, 32 * 520], BF16))
            QT = en2(_sb(nc, "QT", [128, 4 * T], BF16))
            PT = en2(_sb(nc, "PT", [128, 3 * 1024], BF16))
            wring = en2(_sb(nc, "wring", [128, 2 * DT * 128], BF16))
            wvbuf = en2(_sb(nc, "wvbuf", [128, DT * 512], BF16))
            bvb = PT[:, 0:512]

            p.dsem("xpbf_d")
            if use_prefix:
                if pref_wait is not None:
                    p.wait("gpsimd", pref_wait[0], pref_wait[1])
                vpx = None
                for k in range(DT):
                    vpx = p.do("gpsimd", lambda e, k=k: e.dma_start(
                        out=xpbf[:, k * T:(k + 1) * T], in_=xprefT[k * 128:(k + 1) * 128, :]),
                        inc=("xpbf_d", 16))
                p.wait("tensor", "xpbf_d", vpx)

            RING = 2
            ring_free = [None] * RING
            ring_loaded = [None] * RING
            ring_pos = [0]

            for _s in range(RING):
                p.dsem(f"wr{_s}")

            def qk_panel(src, m):
                sl = ring_pos[0] % RING
                ring_pos[0] += 1
                p.wait("gpsimd", "t", ring_free[sl])
                dst = wring[:, sl * DT * 128:(sl + 1) * DT * 128].rearrange(
                    "p (k c) -> p k c", k=DT)
                ring_loaded[sl] = p.do("gpsimd", lambda e, m=m, dst=dst, src=src: e.dma_start(
                    out=dst, in_=src[m].rearrange("k p c -> p k c")), inc=(f"wr{sl}", 16))
                return sl

            bank_free = {i: None for i in range(4)}
            ps_s_free_s = None
            ps_s_free_v = None

            def ps_s_free_s_asv():
                return ps_s_free_v
            wv_used = None

            for hh in range(2):
                p.wait("tensor", "v", bank_free[0])
                tv = p.do("tensor", lambda e, hh=hh: e.matmul(
                    ps_o[0][:], onesP[64:65, :], bvrow[0:1, hh * 512:(hh + 1) * 512],
                    start=True, stop=True), inc=("t", 1))
                p.wait("vector", "t", tv)
                p.wait("vector", "s", p.cnt["s"])
                vbv = p.do("vector", lambda e: e.tensor_copy(out=bvb, in_=ps_o[0][:]),
                           inc=("v", 1))
                bank_free[0] = vbv

                p.dsem("wv_d")
                p.wait("gpsimd", "t", wv_used)
                wvl = None
                for k in range(DT):
                    wvl = p.do("gpsimd", lambda e, k=k, hh=hh: e.dma_start(
                        out=wvbuf[:, k * 512:(k + 1) * 512],
                        in_=wv[k, :, hh * 512:(hh + 1) * 512]), inc=("wv_d", 16))

                pbank = [ps_o[i][:] for i in range(4)] + \
                    [ps_s[:, i * 512:(i + 1) * 512] for i in range(4)]
                pb_free = {i: bank_free.get(i) for i in range(4)}
                pb_free.update({i: ps_s_free_s_asv() for i in range(4, 8)})
                p.wait("tensor", "s", ps_s_free_s)
                rot = [0]
                for which in ("k", "q"):
                    tcs = (list(range(8)) if use_prefix else [4, 5, 6, 7]) \
                        if which == "k" else list(range(4))
                    wsrc = wk if which == "k" else wq
                    bias_sb = bk_sb if which == "k" else bq_sb
                    for mm in range(4):
                        mg = hh * 4 + mm
                        sl = qk_panel(wsrc, mg)
                        p.wait("tensor", f"wr{sl}", ring_loaded[sl])
                        for tc in tcs:
                            bank = rot[0] % 8
                            rot[0] += 1
                            p.wait("tensor", "v", pb_free[bank])
                            if which == "k":
                                src = xpbf if tc < 4 else xbf
                                stc = tc % 4
                            else:
                                src, stc = xbf, tc
                            tval = None
                            for k in range(DT):
                                tval = p.do("tensor",
                                    lambda e, sl=sl, k=k, bank=bank, src=src, stc=stc: e.matmul(
                                        pbank[bank],
                                        wring[:, sl * DT * 128 + k * 128: sl * DT * 128 + (k + 1) * 128],
                                        src[:, k * T + stc * 512: k * T + stc * 512 + 512],
                                        start=(k == 0), stop=(k == DT - 1)), inc=("t", 1))
                            p.wait("vector", "t", tval)
                            if which == "k":
                                dsl = KT[:, mm * CTX + tc * 512: mm * CTX + tc * 512 + 512]
                            else:
                                dsl = QT[:, mm * T + tc * 512: mm * T + tc * 512 + 512]
                            pb_free[bank] = p.do("vector",
                                lambda e, dsl=dsl, bank=bank, bias_sb=bias_sb, mg=mg:
                                    e.tensor_scalar(out=dsl, in0=pbank[bank],
                                                    scalar1=bias_sb[:, mg:mg + 1],
                                                    scalar2=None, op0=ALU.add), inc=("v", 1))
                        ring_free[sl] = p.cnt["t"]

                p.wait("tensor", "wv_d", wvl)
                for tt in (range(32) if use_prefix else range(16, 32)):
                    bank = rot[0] % 8
                    rot[0] += 1
                    src = xpbf if tt < 16 else xbf
                    ttl = tt % 16
                    p.wait("tensor", "v", pb_free[bank])
                    tval = None
                    for k in range(DT):
                        tval = p.do("tensor",
                            lambda e, k=k, bank=bank, src=src, ttl=ttl: e.matmul(
                                pbank[bank],
                                src[:, k * T + ttl * 128: k * T + ttl * 128 + 128],
                                wvbuf[:, k * 512:(k + 1) * 512],
                                start=(k == 0), stop=(k == DT - 1)), inc=("t", 1))
                    p.wait("vector", "t", tval)
                    p.wait("vector", "v", vbv)
                    dst3 = Vaug[:, tt * 520: tt * 520 + 520].rearrange(
                        "p (h d) -> p h d", h=8)[:, :, 0:64]
                    src3 = pbank[bank].rearrange("p (h d) -> p h d", h=8)
                    bv3 = bvb.rearrange("p (h d) -> p h d", h=8)
                    p.do("vector", lambda e, dst3=dst3, src3=src3, bv3=bv3: e.tensor_tensor(
                        out=dst3, in0=src3, in1=bv3, op=ALU.add))
                    ones3 = Vaug[:, tt * 520: tt * 520 + 520].rearrange(
                        "p (h d) -> p h d", h=8)[:, :, 64:65]
                    pb_free[bank] = p.do("vector", lambda e, ones3=ones3: e.memset(ones3, 1.0),
                                         inc=("v", 2))
                for i in range(4):
                    bank_free[i] = pb_free[i]
                ps_s_free_v = max(v for v in [pb_free[j] for j in range(4, 8)]
                                  if v is not None) if any(
                    pb_free[j] is not None for j in range(4, 8)) else ps_s_free_v
                wv_used = p.cnt["t"]

                p.wait("scalar", "v", p.cnt["v"])
                for qc in range(4):
                    for grp in range(2):
                        kts = (list(range(16)) if use_prefix else []) + \
                            [16 + j for j in range(4 * qc + 4)]
                        for i in range(4):
                            p.wait("tensor", "v", bank_free[i])
                        pvv = None
                        half_free_s = {0: None, 1: None}
                        units = [(ki, kt, hf) for ki, kt in enumerate(kts)
                                 for hf in range(2)]
                        nu = len(units)

                        def emit_pv(u, final):
                            nonlocal pvv
                            (ki, kt, hf), slot, sv, fix = u
                            p.wait("tensor", "s", sv)
                            if fix is not None:
                                p.wait("tensor", "v", fix)
                            for j, hi in enumerate((hf * 2, hf * 2 + 1)):
                                h8 = grp * 4 + hi
                                last = (ki == len(kts) - 1)
                                incv = ("t", 1) if (final and j == 1) else None
                                pv_ = p.do("tensor",
                                    lambda e, hi=hi, h8=h8, kt=kt, ki=ki, last=last,
                                           slot=slot, j=j: e.matmul(
                                        ps_o[hi][0:65, :],
                                        Vaug[:, kt * 520 + h8 * 65: kt * 520 + h8 * 65 + 65],
                                        PT[:, slot * 1024 + j * 512: slot * 1024 + j * 512 + 512],
                                        start=(ki == 0), stop=last), inc=incv)
                                if pv_ is not None:
                                    pvv = pv_

                        from collections import deque
                        pending = deque()
                        for ui, (ki, kt, hf) in enumerate(units):
                            slot = ui % 3
                            bias_ap = pb_sb[:, 0:1] if kt < 16 else pb_sb[:, 1:2]
                            p.wait("tensor", "s", half_free_s[hf])
                            p.wait("tensor", "v", ps_s_free_v)
                            stv = None
                            for hi in (hf * 2, hf * 2 + 1):
                                h8 = grp * 4 + hi
                                mt = h8 // 2
                                r0 = (h8 % 2) * 64
                                stv = p.do("tensor",
                                    lambda e, mt=mt, r0=r0, kt=kt, qc=qc, hi=hi: e.matmul(
                                        ps_s[:, hi * 512:(hi + 1) * 512],
                                        KT[r0:r0 + 64, mt * CTX + kt * 128: mt * CTX + kt * 128 + 128],
                                        QT[r0:r0 + 64, mt * T + qc * 512: mt * T + qc * 512 + 512],
                                        start=True, stop=True), inc=("t", 1))
                            if len(pending) >= 2:
                                emit_pv(pending.popleft(), final=False)
                            p.wait("scalar", "t", stv)
                            sv = p.do("scalar",
                                lambda e, bias_ap=bias_ap, hf=hf, slot=slot: e.activation(
                                    out=PT[:, slot * 1024:(slot + 1) * 1024],
                                    in_=ps_s[:, hf * 1024:(hf + 1) * 1024],
                                    func=AF.Exp, bias=bias_ap, scale=0.125), inc=("s", 1))
                            half_free_s[hf] = sv
                            fix = None
                            if kt >= 16:
                                ko = kt - 16
                                p.wait("vector", "s", sv)
                                for j in range(2):
                                    for qsub in range(4):
                                        qj = 4 * qc + qsub
                                        sl_ = PT[:, slot * 1024 + j * 512 + qsub * 128:
                                                 slot * 1024 + j * 512 + qsub * 128 + 128]
                                        if qj < ko:
                                            fix = p.do("vector", lambda e, sl_=sl_: e.memset(
                                                sl_, 0.0), inc=("v", 1))
                                        elif qj == ko:
                                            fix = p.do("vector", lambda e, sl_=sl_: e.tensor_mul(
                                                out=sl_, in0=sl_, in1=triu_sb[:]), inc=("v", 1))
                            pending.append(((ki, kt, hf), slot, sv, fix))
                        while pending:
                            emit_pv(pending.popleft(), final=(len(pending) == 0))
                        ps_s_free_s = half_free_s[1]
                        p.wait("vector", "t", pvv)
                        rv = None
                        for hi in range(4):
                            rv = p.do("vector", lambda e, hi=hi: e.reciprocal(
                                out=rr[0:1, hi * 512:(hi + 1) * 512], in_=ps_o[hi][64:65, :]),
                                inc=("v", 1))
                        p.wait("tensor", "v", rv)
                        p.wait("tensor", "s", ps_s_free_s)
                        btv = None
                        for hi in range(4):
                            btv = p.do("tensor", lambda e, hi=hi: e.matmul(
                                ps_s[0:64, hi * 512:(hi + 1) * 512], onesP[64:65, 0:64],
                                rr[0:1, hi * 512:(hi + 1) * 512], start=True, stop=True),
                                inc=("t", 1))
                        p.wait("vector", "t", btv)
                        cv = p.do("vector", lambda e: e.tensor_copy(
                            out=bcb[:], in_=ps_s[0:64, :]), inc=("v", 1))
                        ps_s_free_v = cv
                        bvlast = None
                        for hi in range(4):
                            h8 = grp * 4 + hi
                            hg = hh * 8 + h8
                            kc = hg // 2
                            r0 = (hg % 2) * 64
                            dst = OT[r0:r0 + 64, kc * T + qc * 512: kc * T + qc * 512 + 512]
                            bvlast = p.do("vector", lambda e, dst=dst, hi=hi: e.tensor_tensor(
                                out=dst, in0=ps_o[hi][0:64, :],
                                in1=bcb[0:64, hi * 512:(hi + 1) * 512], op=ALU.mult),
                                inc=("v", 1))
                        for i in range(4):
                            bank_free[i] = bvlast

            p.barrier()

        with ExitStack() as st3:
            en3 = st3.enter_context
            xn = en3(_sb(nc, "xn", [128, DT * T], F32))
            woring = en3(_sb(nc, "woring", [128, 4 * DT * 128], BF16))
            sq = en3(_sb(nc, "sq", [128, 1024], F32))
            mb = en3(_sb(nc, "mb", [128, 512], F32))
            rb = en3(_sb(nc, "rb", [128, 512], F32))

            p.dsem("xn_d")
            vxn = None
            for k in range(DT):
                vxn = p.do("sync", lambda e, k=k: e.dma_start(
                    out=xn[:, k * T:(k + 1) * T], in_=xT[k * 128:(k + 1) * 128, :]),
                    inc=("xn_d", 16))
            p.wait("vector", "xn_d", vxn)

            RING2 = 4
            r2_free = [None] * RING2
            r2_loaded = [None] * RING2
            r2_pos = [0]

            for _s in range(RING2):
                p.dsem(f"wo{_s}")

            def wo_panel(m):
                sl = r2_pos[0] % RING2
                r2_pos[0] += 1
                p.wait("gpsimd", "t", r2_free[sl])
                dst = woring[:, sl * DT * 128:(sl + 1) * DT * 128].rearrange(
                    "p (k c) -> p k c", k=DT)
                r2_loaded[sl] = p.do("gpsimd", lambda e, m=m, dst=dst: e.dma_start(
                    out=dst, in_=wo[m].rearrange("k p c -> p k c")), inc=(f"wo{sl}", 16))
                return sl

            bank_free2 = {i: None for i in range(4)}
            for m in range(DT):
                sl = wo_panel(m)
                p.wait("tensor", f"wo{sl}", r2_loaded[sl])
                for tc in range(4):
                    bank = tc % 4
                    p.wait("tensor", "v", bank_free2[bank])
                    tval = None
                    for kc in range(DT):
                        tval = p.do("tensor",
                            lambda e, sl=sl, kc=kc, bank=bank, tc=tc: e.matmul(
                                ps_o[bank][:],
                                woring[:, sl * DT * 128 + kc * 128: sl * DT * 128 + (kc + 1) * 128],
                                OT[:, kc * T + tc * 512: kc * T + tc * 512 + 512],
                                start=(kc == 0), stop=(kc == DT - 1)), inc=("t", 1))
                    p.wait("vector", "t", tval)
                    dsl = xn[:, m * T + tc * 512: m * T + tc * 512 + 512]
                    bank_free2[bank] = p.do("vector",
                        lambda e, dsl=dsl, bank=bank, m=m: e.scalar_tensor_tensor(
                            out=dsl, in0=ps_o[bank][:], scalar=bo_sb[:, m:m + 1], in1=dsl,
                            op0=ALU.add, op1=ALU.add), inc=("v", 1))
                r2_free[sl] = p.cnt["t"]

            for i in range(4):
                p.wait("tensor", "v", bank_free2[i])
            lnv = emit_ln_T(p, xn, ps_o, onesP, sq, mrow, vrow, mb, rb, epsc)

            p.dsem("out_d")
            p.wait("sync", "v", lnv)
            vout = None
            for k in range(DT):
                vout = p.do("sync", lambda e, k=k: e.dma_start(
                    out=yT[k * 128:(k + 1) * 128, :], in_=xn[:, k * T:(k + 1) * T]),
                    inc=("out_d", 16))
            p.wait("sync", "out_d", vout)

            p.barrier()


def build_attn():
    nc = bass.Bass("TRN2", target_bir_lowering=False)
    xT = nc.dram_tensor("xT", [D, T], F32, kind="ExternalInput").ap()
    xprefT = nc.dram_tensor("xprefT", [D, T], F32, kind="ExternalInput").ap()
    pbias = nc.dram_tensor("pbias", [128, 1], F32, kind="ExternalInput").ap()
    wq = nc.dram_tensor("wq", [DT, DT, 128, 128], F32, kind="ExternalInput").ap()
    wk = nc.dram_tensor("wk", [DT, DT, 128, 128], F32, kind="ExternalInput").ap()
    wv = nc.dram_tensor("wv", [DT, 128, D], F32, kind="ExternalInput").ap()
    wo = nc.dram_tensor("wo", [DT, DT, 128, 128], F32, kind="ExternalInput").ap()
    bq = nc.dram_tensor("bq", [128, DT], F32, kind="ExternalInput").ap()
    bk = nc.dram_tensor("bk", [128, DT], F32, kind="ExternalInput").ap()
    bvr = nc.dram_tensor("bvr", [1, D], F32, kind="ExternalInput").ap()
    bo = nc.dram_tensor("bo", [128, DT], F32, kind="ExternalInput").ap()
    triu = nc.dram_tensor("triu", [128, 128], BF16, kind="ExternalInput").ap()
    yT = nc.dram_tensor("yT", [D, T], F32, kind="ExternalOutput").ap()

    io = {k: v for k, v in zip(IO_ATTN, [xT, xprefT, pbias, wq, wk, wv, wo,
                                        bq, bk, bvr, bo, triu, yT])}
    with ExitStack() as st:
        p = mk_prog(nc, st)
        emit_attn(nc, p, io)
        block = st.enter_context(nc.Block())
        p.run(block)
    return nc



IO_FFN0 = ["xT", "wr", "br", "w1", "b1", "w2", "b2", "id8", "idF", "sel8", "yT"]


def emit_ffn0(nc, p, io):
    (xT, wr, br, w1, b1, w2, b2, id8, idF, sel8, yT) = (io[k] for k in IO_FFN0)
    with ExitStack() as st:
        en = st.enter_context
        xn = en(_sb(nc, "xn", [128, DT * T], F32))
        xbf = en(_sb(nc, "xbf", [128, DT * T], BF16))
        hbuf = en(_sb(nc, "hbuf", [128, 16 * T], BF16))
        gTm = en(_sb(nc, "gTm", [8, T], F32))
        gbc = en(_sb(nc, "gbc", [128, T], F32))
        prb = en(_sb(nc, "prb", [128, NT * 8], F32))
        keep = en(_sb(nc, "keep", [128, NT * 8], F32))
        scr8 = en(_sb(nc, "scr8", [128, 8], F32))
        colA = en(_sb(nc, "colA", [128, 4], F32))
        onesP = en(_sb(nc, "onesP", [128, 128], F32))
        id8_sb = en(_sb(nc, "id8_sb", [8, 8], F32))
        idF_sb = en(_sb(nc, "idF_sb", [128, 128], F32))
        sel8_sb = en(_sb(nc, "sel8_sb", [8, 8 * 128], F32))
        br_sb = en(_sb(nc, "br_sb", [8, 1], F32))
        b1_sb = en(_sb(nc, "b1_sb", [128, NF], F32))
        b2_sb = en(_sb(nc, "b2_sb", [128, DT], F32))
        wrbuf = en(_sb(nc, "wrbuf", [128, DT * 8], F32))
        w1ring = en(_sb(nc, "w1ring", [128, 4 * DT * 128], BF16))
        w2ring = en(_sb(nc, "w2ring", [128, 2 * 16 * 128], BF16))
        tmp = en(_sb(nc, "tmp", [128, 512], F32))
        sq = en(_sb(nc, "sq", [128, 1024], F32))
        mrow = tmp[0:1, 0:512]
        vrow = gbc[0:1, 0:512]
        mb = gbc[:, 512:1024]
        rb = gbc[:, 1024:1536]

        epsc = en(_sb(nc, "epsc", [128, 1], F32))
        ps_s = en(_ps(nc, "ps_s", [128, 2048], F32))
        ps_A = en(_ps(nc, "ps_A", [128, 2048], F32))
        ps_o = [ps_A[:, i * 512:(i + 1) * 512] for i in range(4)]

        vc = p.do("gpsimd", lambda e: e.memset(onesP[:], 1.0), inc=("g", 1))
        vc = p.do("gpsimd", lambda e: e.memset(epsc[:], LNEPS), inc=("g", 1))
        p.dsem("c_d"); p.dsem("xn_d"); p.dsem("xbf_d")
        vsm = None
        for dst, src in [(br_sb[:], br), (wrbuf[:], wr), (id8_sb[:], id8), (idF_sb[:], idF),
                         (sel8_sb[:], sel8)]:
            vsm = p.do("sync", lambda e, dst=dst, src=src: e.dma_start(out=dst, in_=src),
                       inc=("c_d", 16))
        vx = None
        for k in range(DT):
            vx = p.do("sync", lambda e, k=k: e.dma_start(
                out=xn[:, k * T:(k + 1) * T], in_=xT[k * 128:(k + 1) * 128, :]), inc=("xn_d", 16))
            p.do("gpsimd", lambda e, k=k: e.dma_start(
                out=xbf[:, k * T:(k + 1) * T], in_=xT[k * 128:(k + 1) * 128, :]), inc=("xbf_d", 16))
        vxb = p.cnt["xbf_d"]
        for eng in ("tensor", "vector", "scalar"):
            p.wait(eng, "c_d", vsm)
            p.wait(eng, "xn_d", vx)
            p.wait(eng, "xbf_d", vxb)
            p.wait(eng, "g", vc)

        tv = None
        for tc in range(4):
            for k in range(DT):
                tv = p.do("tensor", lambda e, k=k, tc=tc: e.matmul(
                    ps_o[tc][0:8, :], wrbuf[:, k * 8:(k + 1) * 8],
                    xn[:, k * T + tc * 512: k * T + tc * 512 + 512],
                    start=(k == 0), stop=(k == DT - 1)), inc=("t", 1))
        p.wait("vector", "t", tv)
        lv = None
        for tc in range(4):
            lv = p.do("vector", lambda e, tc=tc: e.tensor_scalar(
                out=gTm[0:8, tc * 512:(tc + 1) * 512], in0=ps_o[tc][0:8, :],
                scalar1=br_sb[:], scalar2=None, op0=ALU.add), inc=("v", 1))
        p.wait("tensor", "v", lv)
        tp_free = [None] * 4
        for tt in range(NT):
            p.wait("tensor", "v", tp_free[tt % 4])
            tv = p.do("tensor", lambda e, tt=tt: e.matmul(
                ps_o[tt % 4][0:128, 0:8], gTm[0:8, tt * 128:(tt + 1) * 128], id8_sb[:],
                start=True, stop=True), inc=("t", 1))
            p.wait("vector", "t", tv)
            lv = tp_free[tt % 4] = p.do("vector", lambda e, tt=tt: e.tensor_copy(
                out=prb[:, tt * 8:(tt + 1) * 8], in_=ps_o[tt % 4][0:128, 0:8]), inc=("v", 1))
        gv = None
        for tt in range(NT):
            sl_ = prb[:, tt * 8:(tt + 1) * 8]
            kl = keep[:, tt * 8:(tt + 1) * 8]
            _rv = p.do("vector", lambda e, sl_=sl_: e.reduce_max(
                out=colA[:, 0:1], in_=sl_, axis=AX.X, negate=True), inc=("v", 1))
            p.wait("vector", "v", _rv)
            vv = p.do("vector", lambda e, sl_=sl_: e.tensor_scalar(
                out=sl_, in0=sl_, scalar1=colA[:, 0:1], scalar2=None, op0=ALU.add), inc=("v", 1))
            p.wait("scalar", "v", vv)
            sv = p.do("scalar", lambda e, sl_=sl_: e.activation(
                out=sl_, in_=sl_, func=AF.Exp, bias=0.0, scale=1.0), inc=("s", 1))
            p.wait("vector", "s", sv)
            _rv = p.do("vector", lambda e, sl_=sl_: e.reduce_sum(out=colA[:, 1:2], in_=sl_, axis=AX.X), inc=("v", 1))
            p.wait("vector", "v", _rv)
            _rv = p.do("vector", lambda e: e.reciprocal(out=colA[:, 2:3], in_=colA[:, 1:2]),
                       inc=("v", 1))
            p.wait("vector", "v", _rv)
            p.do("vector", lambda e, sl_=sl_: e.tensor_scalar(
                out=sl_, in0=sl_, scalar1=colA[:, 2:3], scalar2=None, op0=ALU.mult))
            for e_ in range(8):
                p.do("vector", lambda e, sl_=sl_, e_=e_: e.tensor_scalar(
                    out=scr8[:], in0=sl_, scalar1=sl_[:, e_:e_ + 1], scalar2=None, op0=ALU.is_gt))
                p.do("vector", lambda e, sl_=sl_: e.tensor_mul(out=scr8[:], in0=scr8[:], in1=sl_))
                _rv = p.do("vector", lambda e, kl=kl, e_=e_: e.reduce_sum(
                    out=kl[:, e_:e_ + 1], in_=scr8[:], axis=AX.X), inc=("v", 1))
                p.wait("vector", "v", _rv)
            p.do("vector", lambda e, kl=kl, sl_=sl_: e.tensor_add(out=kl, in0=kl, in1=sl_))
            p.do("vector", lambda e, kl=kl: e.tensor_scalar(
                out=kl, in0=kl, scalar1=0.8, scalar2=None, op0=ALU.is_lt))
            _rv = p.do("vector", lambda e, sl_=sl_: e.reduce_max(out=colA[:, 3:4], in_=sl_, axis=AX.X), inc=("v", 1))
            p.wait("vector", "v", _rv)
            p.do("vector", lambda e, sl_=sl_: e.tensor_scalar(
                out=scr8[:], in0=sl_, scalar1=colA[:, 3:4], scalar2=None, op0=ALU.is_ge))
            p.do("vector", lambda e, kl=kl: e.tensor_max(out=kl, in0=kl, in1=scr8[:]))
            gv = p.do("vector", lambda e, kl=kl, sl_=sl_: e.tensor_mul(out=kl, in0=kl, in1=sl_),
                      inc=("v", 1))
        p.wait("tensor", "v", gv)
        tp_free = [None] * 4
        for tt in range(NT):
            p.wait("tensor", "v", tp_free[tt % 4])
            tv = p.do("tensor", lambda e, tt=tt: e.matmul(
                ps_o[tt % 4][0:8, 0:128], keep[:, tt * 8:(tt + 1) * 8], idF_sb[:],
                start=True, stop=True), inc=("t", 1))
            p.wait("vector", "t", tv)
            gv = tp_free[tt % 4] = p.do("vector", lambda e, tt=tt: e.tensor_copy(
                out=gTm[0:8, tt * 128:(tt + 1) * 128], in_=ps_o[tt % 4][0:8, 0:128]),
                inc=("v", 1))

        RING = 4
        r_free = [None] * RING
        r_loaded = [None] * RING
        r_pos = [0]

        for _s in range(RING):
            p.dsem(f"w1r{_s}")

        def w1_panel(e_, f):
            sl = r_pos[0] % RING
            r_pos[0] += 1
            p.wait("gpsimd", "t", r_free[sl])
            dst = w1ring[:, sl * DT * 128:(sl + 1) * DT * 128].rearrange(
                "p (k c) -> p k c", k=DT)
            r_loaded[sl] = p.do("gpsimd", lambda e, e_=e_, f=f, dst=dst: e.dma_start(
                out=dst, in_=w1[e_, f].rearrange("k p c -> p k c")), inc=(f"w1r{sl}", 16))
            return sl

        r2_free = [None, None]
        r2_loaded = [None, None]
        r2_pos = [0]

        for _s in range(2):
            p.dsem(f"w2r{_s}")

        def w2_panel(e_, m, fh):
            sl = r2_pos[0] % 2
            r2_pos[0] += 1
            p.wait("gpsimd", "t", r2_free[sl])
            dst = w2ring[:, sl * 16 * 128:(sl + 1) * 16 * 128].rearrange(
                "p (k c) -> p k c", k=16)
            r2_loaded[sl] = p.do("gpsimd", lambda e, e_=e_, m=m, fh=fh, dst=dst: e.dma_start(
                out=dst, in_=w2[e_, m, fh * 16:(fh + 1) * 16].rearrange("k p c -> p k c")),
                inc=(f"w2r{sl}", 16))
            return sl

        vb12 = None
        sets = {"S": ps_s, "A": ps_A}
        set_s_free = {"S": None, "A": None}
        bank_v = {("A", i): gv for i in range(4)}
        for i in range(4):
            bank_v[("S", i)] = None
        relu_last = None
        yv_last = gv

        def wait_set(eng, setn):
            p.wait(eng, "s", set_s_free[setn])
            for i in range(4):
                p.wait(eng, "v", bank_v[(setn, i)])

        for e_ in range(8):
            p.wait("sync", "s", relu_last)
            p.wait("sync", "v", yv_last)
            p.dsem("b12_d")
            vb12 = p.do("sync", lambda e, e_=e_: e.dma_start(out=b1_sb[:], in_=b1[e_]),
                        inc=("b12_d", 16))
            vb12 = p.do("sync", lambda e, e_=e_: e.dma_start(out=b2_sb[:], in_=b2[e_]),
                        inc=("b12_d", 16))
            gtv = None
            wait_set("tensor", "A")
            for tc in range(4):
                gtv = p.do("tensor", lambda e, e_=e_, tc=tc: e.matmul(
                    ps_o[tc][:], sel8_sb[:, e_ * 128:(e_ + 1) * 128],
                    gTm[0:8, tc * 512:(tc + 1) * 512],
                    start=True, stop=True), inc=("t", 1))
            p.wait("vector", "t", gtv)
            gcv = None
            for tc in range(4):
                gcv = p.do("vector", lambda e, tc=tc: e.tensor_copy(
                    out=gbc[:, tc * 512:(tc + 1) * 512], in_=ps_o[tc][:]), inc=("v", 1))
            for i in range(4):
                bank_v[("A", i)] = gcv
            p.wait("scalar", "b12_d", vb12)
            p.wait("vector", "b12_d", vb12)

            for fh in range(2):
                for fl in range(16):
                    f = fh * 16 + fl
                    setn = "S" if fl % 2 == 0 else "A"
                    ps = sets[setn]
                    sl = w1_panel(e_, f)
                    p.wait("tensor", f"w1r{sl}", r_loaded[sl])
                    wait_set("tensor", setn)
                    tval = None
                    for k in range(DT):
                        for qn in range(4):
                            tval = p.do("tensor",
                                lambda e, sl=sl, k=k, qn=qn, ps=ps: e.matmul(
                                    ps[:, qn * 512:(qn + 1) * 512],
                                    w1ring[:, sl * DT * 128 + k * 128: sl * DT * 128 + (k + 1) * 128],
                                    xbf[:, k * T + qn * 512: k * T + qn * 512 + 512],
                                    start=(k == 0), stop=(k == DT - 1)),
                                inc=(None if (k < DT - 1 or qn < 3) else ("t", 1)))
                    p.wait("scalar", "t", tval)
                    relu_last = set_s_free[setn] = p.do("scalar",
                        lambda e, fl=fl, f=f, ps=ps: e.activation(
                            out=hbuf[:, fl * T:(fl + 1) * T], in_=ps[:], func=AF.Relu,
                            bias=b1_sb[:, f:f + 1], scale=1.0), inc=("s", 1))
                    for i in range(4):
                        bank_v[(setn, i)] = None
                    r_free[sl] = p.cnt["t"]
                p.wait("tensor", "s", set_s_free["S"])
                p.wait("tensor", "s", set_s_free["A"])
                pp = 0
                for m in range(DT):
                    sl2 = w2_panel(e_, m, fh)
                    p.wait("tensor", f"w2r{sl2}", r2_loaded[sl2])
                    for tc in range(4):
                        setn = "S" if pp % 2 == 0 else "A"
                        pp += 1
                        ps = sets[setn]
                        p.wait("tensor", "v", bank_v[(setn, tc)])
                        p.wait("tensor", "s", set_s_free[setn])
                        tval = None
                        for fc in range(16):
                            tval = p.do("tensor",
                                lambda e, sl2=sl2, fc=fc, tc=tc, ps=ps: e.matmul(
                                    ps[:, tc * 512:(tc + 1) * 512],
                                    w2ring[:, sl2 * 16 * 128 + fc * 128: sl2 * 16 * 128 + (fc + 1) * 128],
                                    hbuf[:, fc * T + tc * 512: fc * T + tc * 512 + 512],
                                    start=(fc == 0), stop=(fc == 15)),
                                inc=(None if fc < 15 else ("t", 1)))
                        p.wait("vector", "t", tval)
                        dsl = xn[:, m * T + tc * 512: m * T + tc * 512 + 512]
                        if fh == 0:
                            p.do("vector", lambda e, tc=tc, m=m, ps=ps: e.scalar_tensor_tensor(
                                out=tmp[:], in0=ps[:, tc * 512:(tc + 1) * 512],
                                scalar=b2_sb[:, m:m + 1],
                                in1=gbc[:, tc * 512:(tc + 1) * 512], op0=ALU.add, op1=ALU.mult))
                        else:
                            p.do("vector", lambda e, tc=tc, ps=ps: e.tensor_mul(
                                out=tmp[:], in0=ps[:, tc * 512:(tc + 1) * 512],
                                in1=gbc[:, tc * 512:(tc + 1) * 512]))
                        yv_last = bank_v[(setn, tc)] = p.do("vector",
                            lambda e, dsl=dsl: e.tensor_add(
                                out=dsl, in0=dsl, in1=tmp[:]), inc=("v", 1))
                    r2_free[sl2] = p.cnt["t"]
                set_s_free = {"S": None, "A": None}

        p.dsem("out_d")
        for setn in ("S", "A"):
            for i in range(4):
                p.wait("tensor", "v", bank_v[(setn, i)])
        lnv = emit_ln_T(p, xn, ps_o, onesP, sq, mrow, vrow, mb, rb, epsc)
        p.wait("sync", "v", lnv)
        vout = None
        for k in range(DT):
            vout = p.do("sync", lambda e, k=k: e.dma_start(
                out=yT[k * 128:(k + 1) * 128, :], in_=xn[:, k * T:(k + 1) * T]), inc=("out_d", 16))
        p.wait("sync", "out_d", vout)

        p.barrier()


def build_ffn0():
    nc = bass.Bass("TRN2", target_bir_lowering=False)
    xT = nc.dram_tensor("xT", [D, T], F32, kind="ExternalInput").ap()
    wr = nc.dram_tensor("wr", [128, DT * 8], F32, kind="ExternalInput").ap()
    br = nc.dram_tensor("br", [8, 1], F32, kind="ExternalInput").ap()
    w1 = nc.dram_tensor("w1", [8, NF, DT, 128, 128], F32, kind="ExternalInput").ap()
    b1 = nc.dram_tensor("b1", [8, 128, NF], F32, kind="ExternalInput").ap()
    w2 = nc.dram_tensor("w2", [8, DT, NF, 128, 128], F32, kind="ExternalInput").ap()
    b2 = nc.dram_tensor("b2", [8, 128, DT], F32, kind="ExternalInput").ap()
    id8 = nc.dram_tensor("id8", [8, 8], F32, kind="ExternalInput").ap()
    idF = nc.dram_tensor("idF", [128, 128], F32, kind="ExternalInput").ap()
    sel8 = nc.dram_tensor("sel8", [8, 8 * 128], F32, kind="ExternalInput").ap()
    yT = nc.dram_tensor("yT", [D, T], F32, kind="ExternalOutput").ap()

    io = {k: v for k, v in zip(IO_FFN0, [xT, wr, br, w1, b1, w2, b2, id8,
                                         idF, sel8, yT])}
    with ExitStack() as st:
        p = mk_prog(nc, st)
        emit_ffn0(nc, p, io)
        block = st.enter_context(nc.Block())
        p.run(block)
    return nc



IO_FFN1 = ["xT", "xnat", "wr", "w0", "w1", "wo", "id8", "idF", "trii",
           "eio", "tokio", "ynat", "stok", "ydram"]


def emit_ffn1(nc, p, io):
    (xT, xnat, wr, w0, w1, wo, id8, idF, trii, eio, tokio, ynat, stok,
     ydram) = (io[k] for k in IO_FFN1)
    with ExitStack() as st:
        en = st.enter_context
        prb = en(_sb(nc, "prb", [128, NT * 8], F32))
        m1b = en(_sb(nc, "m1b", [128, NT * 8], F32))
        m2b = en(_sb(nc, "m2b", [128, NT * 8], F32))
        cmb = en(_sb(nc, "cmb", [128, NT * 8], F32))
        gcol = en(_sb(nc, "gcol", [128, 4 * NT], F32))
        jcol = en(_sb(nc, "jcol", [128, 2 * NT], F32))
        jint = en(_sb(nc, "jint", [128, 2 * NT], I32))
        pcol = en(_sb(nc, "pcol", [128, 6], F32))
        scr8 = en(_sb(nc, "scr8", [128, 8], F32))
        scr8b = en(_sb(nc, "scr8b", [128, 8], F32))
        cntb = en(_sb(nc, "cntb", [128, 8], F32))
        cntr = en(_sb(nc, "cntr", [1, 8], F32))
        onesP = en(_sb(nc, "onesP", [128, 128], F32))
        id8_sb = en(_sb(nc, "id8_sb", [8, 8], F32))
        idF_sb = en(_sb(nc, "idF_sb", [128, 128], F32))
        trii_sb = en(_sb(nc, "trii_sb", [128, 128], F32))
        eio_sb = en(_sb(nc, "eio_sb", [128, 8], F32))
        toki_sb = en(_sb(nc, "toki_sb", [128, NT], I32))
        stok_sb = en(_sb(nc, "stok_sb", [128, 32], I32))
        wrbuf = en(_sb(nc, "wrbuf", [128, DT * 8], F32))
        xf32 = en(_sb(nc, "xf32", [128, 2 * 512], F32))
        lgT = en(_sb(nc, "lgT", [8, T], F32))

        zbuf = en(_sb(nc, "zbuf", [128, 2 * D], F32))
        ZTe = en(_sb(nc, "ZTe", [128, DT * 512], BF16))
        hid = en(_sb(nc, "hid", [128, HT1 * 512], BF16))
        ybuf = en(_sb(nc, "ybuf", [128, 4 * D], BF16))
        w01ring = en(_sb(nc, "w01ring", [128, 6 * DT * 128], BF16))
        woring = en(_sb(nc, "woring", [128, 4 * D], BF16))
        g12 = en(_sb(nc, "g12", [128, 2 * D], BF16))
        xot = en(_sb(nc, "xot", [128, D], F32))
        xrt = en(_sb(nc, "xrt", [128, D], F32))
        sqn = en(_sb(nc, "sqn", [128, D], F32))
        gt1 = en(_sb(nc, "gt1", [128, D], F32))

        epsc = en(_sb(nc, "epsc", [128, 1], F32))
        ps_o = [en(_ps(nc, f"ps_o{i}", [128, 512], F32)) for i in range(8)]

        vc = p.do("gpsimd", lambda e: e.memset(onesP[:], 1.0), inc=("g", 1))
        vc = p.do("gpsimd", lambda e: e.memset(epsc[:], LNEPS), inc=("g", 1))
        p.dsem("c_d")
        vsm = None
        for dst, src in [(id8_sb[:], id8), (idF_sb[:], idF), (trii_sb[:], trii),
                         (eio_sb[:], eio), (toki_sb[:], tokio), (wrbuf[:], wr)]:
            vsm = p.do("sync", lambda e, dst=dst, src=src: e.dma_start(out=dst, in_=src),
                       inc=("c_d", 16))
        for eng in ("tensor", "vector", "scalar"):
            p.wait(eng, "c_d", vsm)
            p.wait(eng, "g", vc)

        xck = [None, None]
        p.dsem("xf0"); p.dsem("xf1")
        tv = None
        for tc in range(4):
            for k in range(DT):
                sl = (tc * DT + k) % 2
                p.wait("sync", "t", xck[sl])
                vl = p.do("sync", lambda e, sl=sl, k=k, tc=tc: e.dma_start(
                    out=xf32[:, sl * 512:(sl + 1) * 512],
                    in_=xT[k * 128:(k + 1) * 128, tc * 512:(tc + 1) * 512]), inc=(f"xf{sl}", 16))
                p.wait("tensor", f"xf{sl}", vl)
                tv = p.do("tensor", lambda e, sl=sl, k=k, tc=tc: e.matmul(
                    ps_o[tc][0:8, :], wrbuf[:, k * 8:(k + 1) * 8],
                    xf32[:, sl * 512:(sl + 1) * 512],
                    start=(k == 0), stop=(k == DT - 1)), inc=("t", 1))
                xck[sl] = p.cnt["t"]
        p.wait("vector", "t", tv)
        lv = None
        for tc in range(4):
            lv = p.do("vector", lambda e, tc=tc: e.tensor_copy(
                out=lgT[0:8, tc * 512:(tc + 1) * 512], in_=ps_o[tc][0:8, :]), inc=("v", 1))
        p.wait("tensor", "v", lv)
        tp_free = [None] * 4
        for tt in range(NT):
            p.wait("tensor", "v", tp_free[tt % 4])
            tv = p.do("tensor", lambda e, tt=tt: e.matmul(
                ps_o[tt % 4][0:128, 0:8], lgT[0:8, tt * 128:(tt + 1) * 128], id8_sb[:],
                start=True, stop=True), inc=("t", 1))
            p.wait("vector", "t", tv)
            lv = tp_free[tt % 4] = p.do("vector", lambda e, tt=tt: e.tensor_copy(
                out=prb[:, tt * 8:(tt + 1) * 8], in_=ps_o[tt % 4][0:128, 0:8]), inc=("v", 1))

        mv = None
        for tt in range(NT):
            sl_ = prb[:, tt * 8:(tt + 1) * 8]
            _rv = p.do("vector", lambda e, sl_=sl_: e.reduce_max(
                out=pcol[:, 0:1], in_=sl_, axis=AX.X, negate=True), inc=("v", 1))
            p.wait("vector", "v", _rv)
            vv = p.do("vector", lambda e, sl_=sl_: e.tensor_scalar(
                out=sl_, in0=sl_, scalar1=pcol[:, 0:1], scalar2=None, op0=ALU.add), inc=("v", 1))
            p.wait("scalar", "v", vv)
            sv = p.do("scalar", lambda e, sl_=sl_: e.activation(
                out=sl_, in_=sl_, func=AF.Exp, bias=0.0, scale=1.0), inc=("s", 1))
            p.wait("vector", "s", sv)
            _rv = p.do("vector", lambda e, sl_=sl_: e.reduce_sum(out=pcol[:, 1:2], in_=sl_, axis=AX.X), inc=("v", 1))
            p.wait("vector", "v", _rv)
            _rv = p.do("vector", lambda e: e.reciprocal(out=pcol[:, 2:3], in_=pcol[:, 1:2]),
                       inc=("v", 1))
            p.wait("vector", "v", _rv)
            p.do("vector", lambda e, sl_=sl_: e.tensor_scalar(
                out=sl_, in0=sl_, scalar1=pcol[:, 2:3], scalar2=None, op0=ALU.mult))
            m1t = m1b[:, tt * 8:(tt + 1) * 8]
            m2t = m2b[:, tt * 8:(tt + 1) * 8]
            _rv = p.do("vector", lambda e, sl_=sl_, tt=tt: e.reduce_max(
                out=gcol[:, 0 * NT + tt: 0 * NT + tt + 1], in_=sl_, axis=AX.X), inc=("v", 1))
            p.wait("vector", "v", _rv)
            p.do("vector", lambda e, sl_=sl_, m1t=m1t, tt=tt: e.tensor_scalar(
                out=m1t, in0=sl_, scalar1=gcol[:, 0 * NT + tt: 0 * NT + tt + 1],
                scalar2=None, op0=ALU.is_ge))
            p.do("vector", lambda e, sl_=sl_, m1t=m1t: e.tensor_mul(
                out=scr8[:], in0=sl_, in1=m1t))
            p.do("vector", lambda e, sl_=sl_: e.tensor_sub(out=scr8[:], in0=sl_, in1=scr8[:]))
            _rv = p.do("vector", lambda e, tt=tt: e.reduce_max(
                out=gcol[:, 1 * NT + tt: 1 * NT + tt + 1], in_=scr8[:], axis=AX.X), inc=("v", 1))
            p.wait("vector", "v", _rv)
            mv = p.do("vector", lambda e, m2t=m2t, tt=tt: e.tensor_scalar(
                out=m2t, in0=scr8[:], scalar1=gcol[:, 1 * NT + tt: 1 * NT + tt + 1],
                scalar2=None, op0=ALU.is_ge), inc=("v", 1))

        p.wait("tensor", "v", mv)

        def cum_pass(buf):
            nonlocal_mv = [None]
            for mtile in range(NT):
                tv = None
                for ktile in range(mtile + 1):
                    lhs = trii_sb[:] if ktile == mtile else onesP[:]
                    tv = p.do("tensor",
                        lambda e, lhs=lhs, ktile=ktile, buf=buf, mtile=mtile: e.matmul(
                            ps_o[4 + mtile % 2][0:128, 0:8], lhs,
                            buf[:, ktile * 8:(ktile + 1) * 8],
                            start=(ktile == 0), stop=(ktile == mtile)), inc=("t", 1))
                p.wait("vector", "t", tv)
                nonlocal_mv[0] = p.do("vector", lambda e, mtile=mtile: e.tensor_copy(
                    out=cmb[:, mtile * 8:(mtile + 1) * 8],
                    in_=ps_o[4 + mtile % 2][0:128, 0:8]), inc=("v", 1))
                p.wait("tensor", "v", nonlocal_mv[0])
            return nonlocal_mv[0]

        mv = cum_pass(m1b)
        for tt in range(NT):
            m1t = m1b[:, tt * 8:(tt + 1) * 8]
            ct = cmb[:, tt * 8:(tt + 1) * 8]
            p.do("vector", lambda e, ct=ct, m1t=m1t: e.tensor_sub(
                out=ct, in0=ct, in1=m1t))
            p.do("vector", lambda e, ct=ct: e.tensor_scalar(
                out=scr8[:], in0=ct, scalar1=float(CAP), scalar2=None, op0=ALU.is_lt))
            p.do("vector", lambda e, m1t=m1t: e.tensor_mul(
                out=m1t, in0=m1t, in1=scr8[:]))
            p.do("vector", lambda e, ct=ct, m1t=m1t: e.tensor_mul(
                out=scr8[:], in0=ct, in1=m1t))
            _rv = p.do("vector", lambda e: e.reduce_sum(
                out=pcol[:, 3:4], in_=scr8[:], axis=AX.X), inc=("v", 1))
            p.wait("vector", "v", _rv)
            _rv = p.do("vector", lambda e, m1t=m1t: e.reduce_sum(
                out=pcol[:, 4:5], in_=m1t, axis=AX.X), inc=("v", 1))
            p.wait("vector", "v", _rv)
            p.do("vector", lambda e, tt=tt: e.scalar_tensor_tensor(
                out=gcol[:, 2 * NT + tt: 2 * NT + tt + 1], in0=pcol[:, 4:5],
                scalar=1.0, in1=gcol[:, 0 * NT + tt: 0 * NT + tt + 1],
                op0=ALU.mult, op1=ALU.mult))
            p.do("vector", lambda e, m1t=m1t: e.tensor_mul(
                out=scr8[:], in0=m1t, in1=eio_sb[:]))
            _rv = p.do("vector", lambda e: e.reduce_sum(
                out=pcol[:, 5:6], in_=scr8[:], axis=AX.X), inc=("v", 1))
            p.wait("vector", "v", _rv)
            p.do("vector", lambda e: e.tensor_scalar(
                out=pcol[:, 5:6], in0=pcol[:, 5:6], scalar1=512.0, scalar2=None,
                op0=ALU.mult))
            p.do("vector", lambda e: e.tensor_add(
                out=pcol[:, 5:6], in0=pcol[:, 5:6], in1=pcol[:, 3:4]))
            p.do("vector", lambda e: e.tensor_mul(
                out=pcol[:, 5:6], in0=pcol[:, 5:6], in1=pcol[:, 4:5]))
            p.do("vector", lambda e: e.tensor_scalar(
                out=pcol[:, 3:4], in0=pcol[:, 4:5], scalar1=-float(NSLOT),
                scalar2=float(NSLOT), op0=ALU.mult, op1=ALU.add))
            p.do("vector", lambda e, tt=tt: e.tensor_add(
                out=jcol[:, tt:tt + 1], in0=pcol[:, 5:6], in1=pcol[:, 3:4]))
            mv = p.do("vector", lambda e, tt=tt: e.tensor_copy(
                out=jint[:, tt:tt + 1], in_=jcol[:, tt:tt + 1]), inc=("v", 1))
        p.wait("tensor", "v", mv)
        tv = None
        for tt2 in range(NT):
            tv = p.do("tensor", lambda e, tt2=tt2: e.matmul(
                ps_o[6][0:1, 0:8], onesP[:, 0:1], m1b[:, tt2 * 8:(tt2 + 1) * 8],
                start=(tt2 == 0), stop=(tt2 == NT - 1)), inc=("t", 1))
        p.wait("vector", "t", tv)
        mv = p.do("vector", lambda e: e.tensor_copy(
            out=cntr[:], in_=ps_o[6][0:1, 0:8]), inc=("v", 1))
        p.wait("tensor", "v", mv)
        tv = p.do("tensor", lambda e: e.matmul(
            ps_o[6][0:128, 0:8], onesP[0:1, :], cntr[:], start=True, stop=True),
            inc=("t", 1))
        p.wait("vector", "t", tv)
        mv = p.do("vector", lambda e: e.tensor_copy(
            out=cntb[:], in_=ps_o[6][0:128, 0:8]), inc=("v", 1))
        mv = cum_pass(m2b)
        for tt in range(NT):
            m2t = m2b[:, tt * 8:(tt + 1) * 8]
            ct = cmb[:, tt * 8:(tt + 1) * 8]
            p.do("vector", lambda e, ct=ct, m2t=m2t: e.tensor_sub(out=ct, in0=ct, in1=m2t))
            p.do("vector", lambda e, ct=ct: e.tensor_add(out=ct, in0=ct, in1=cntb[:]))
            p.do("vector", lambda e, ct=ct: e.tensor_scalar(
                out=scr8[:], in0=ct, scalar1=float(CAP), scalar2=None, op0=ALU.is_lt))
            p.do("vector", lambda e, m2t=m2t: e.tensor_mul(out=m2t, in0=m2t, in1=scr8[:]))
            p.do("vector", lambda e, ct=ct, m2t=m2t: e.tensor_mul(
                out=scr8[:], in0=ct, in1=m2t))
            _rv = p.do("vector", lambda e: e.reduce_sum(out=pcol[:, 3:4], in_=scr8[:], axis=AX.X), inc=("v", 1))
            p.wait("vector", "v", _rv)
            _rv = p.do("vector", lambda e, m2t=m2t: e.reduce_sum(out=pcol[:, 4:5], in_=m2t, axis=AX.X), inc=("v", 1))
            p.wait("vector", "v", _rv)
            p.do("vector", lambda e, tt=tt: e.scalar_tensor_tensor(
                out=gcol[:, 3 * NT + tt: 3 * NT + tt + 1], in0=pcol[:, 4:5], scalar=1.0,
                in1=gcol[:, 1 * NT + tt: 1 * NT + tt + 1], op0=ALU.mult, op1=ALU.mult))
            p.do("vector", lambda e, m2t=m2t: e.tensor_mul(out=scr8[:], in0=m2t, in1=eio_sb[:]))
            _rv = p.do("vector", lambda e: e.reduce_sum(out=pcol[:, 5:6], in_=scr8[:], axis=AX.X), inc=("v", 1))
            p.wait("vector", "v", _rv)
            p.do("vector", lambda e: e.tensor_scalar(
                out=pcol[:, 5:6], in0=pcol[:, 5:6], scalar1=512.0, scalar2=None, op0=ALU.mult))
            p.do("vector", lambda e: e.tensor_add(
                out=pcol[:, 5:6], in0=pcol[:, 5:6], in1=pcol[:, 3:4]))
            p.do("vector", lambda e: e.tensor_mul(
                out=pcol[:, 5:6], in0=pcol[:, 5:6], in1=pcol[:, 4:5]))
            p.do("vector", lambda e: e.tensor_scalar(
                out=pcol[:, 3:4], in0=pcol[:, 4:5], scalar1=-float(NSLOT),
                scalar2=float(NSLOT), op0=ALU.mult, op1=ALU.add))
            p.do("vector", lambda e, tt=tt: e.tensor_add(
                out=jcol[:, NT + tt:NT + tt + 1], in0=pcol[:, 5:6], in1=pcol[:, 3:4]))
            mv = p.do("vector", lambda e, tt=tt: e.tensor_copy(
                out=jint[:, NT + tt:NT + tt + 1], in_=jcol[:, NT + tt:NT + tt + 1]),
                inc=("v", 1))

        p.dsem("sti_d"); p.dsem("sts_d"); p.dsem("stl_d")
        iv = p.do("vector", lambda e: e.memset(stok_sb[:], T), inc=("v", 1))
        p.wait("gpsimd", "v", iv)
        gv = None
        for jt in range(32):
            gv = p.do("gpsimd", lambda e, jt=jt: e.dma_start(
                out=stok[jt * 128:(jt + 1) * 128, :], in_=stok_sb[:, jt:jt + 1]),
                inc=("sti_d", 16))
        gv = p.do("gpsimd", lambda e: e.dma_start(
            out=stok[NSLOT:NSLOT + 1, :], in_=stok_sb[0:1, 0:1]), inc=("sti_d", 16))
        p.wait("gpsimd", "v", mv)
        p.wait("gpsimd", "sti_d", gv)
        p.wait("gpsimd", "c_d", vsm)
        for half in range(2):
            for tt in range(NT):
                gv = p.do("gpsimd", lambda e, half=half, tt=tt: e.indirect_dma_start(
                    out=stok,
                    out_offset=IndirectOffsetOnAxis(jint[:, half * NT + tt: half * NT + tt + 1], 0),
                    in_=toki_sb[:, tt:tt + 1],
                    in_offset=None,
                    bounds_check=None,
                    oob_is_err=False), inc=("sts_d", 16))
        p.wait("gpsimd", "sts_d", gv)
        sldv = None
        for jt in range(32):
            sldv = p.do("gpsimd", lambda e, jt=jt: e.dma_start(
                out=stok_sb[:, jt:jt + 1], in_=stok[jt * 128:(jt + 1) * 128, :]),
                inc=("stl_d", 16))

        RING = 6
        r_free = [None] * RING
        r_loaded = [None] * RING
        r_pos = [0]

        for _s in range(6):
            p.dsem(f"w01r{_s}")

        def w01_panel(src, e_, ht):
            sl = r_pos[0] % RING
            r_pos[0] += 1
            p.wait("gpsimd", "t", r_free[sl])
            dst = w01ring[:, sl * DT * 128:(sl + 1) * DT * 128].rearrange(
                "p (k c) -> p k c", k=DT)
            r_loaded[sl] = p.do("gpsimd", lambda e, e_=e_, ht=ht, dst=dst, src=src: e.dma_start(
                out=dst, in_=src[e_, ht].rearrange("k p c -> p k c")), inc=(f"w01r{sl}", 16))
            return sl

        r2_free = [None] * 4
        r2_loaded = [None] * 4
        r2_pos = [0]

        for _s in range(4):
            p.dsem(f"wor{_s}")

        def wo_panel(e_, ht):
            sl = r2_pos[0] % 4
            r2_pos[0] += 1
            p.wait("gpsimd", "t", r2_free[sl])
            r2_loaded[sl] = p.do("gpsimd", lambda e, e_=e_, ht=ht, sl=sl: e.dma_start(
                out=woring[:, sl * D:(sl + 1) * D], in_=wo[e_, ht]), inc=(f"wor{sl}", 16))
            return sl

        p.wait("gpsimd", "stl_d", sldv)
        sv_h = None
        sv_hv = None
        zfree_t = None
        ztv = None
        hid_free = None
        ybw = None
        ydma = None
        zslot_free = [None, None]
        tslot_free = [None, None]
        for e_ in range(8):
            for jt in range(4):
                zs = jt % 2
                p.wait("gpsimd", "t", zslot_free[zs])
                zm = p.do("gpsimd", lambda e, zs=zs: e.memset(
                    zbuf[:, zs * D:(zs + 1) * D], 0.0), inc=("g", 1))
                p.wait("gpsimd", "g", zm)
                gz = p.do("gpsimd", lambda e, e_=e_, jt=jt, zs=zs: e.indirect_dma_start(
                    out=zbuf[:, zs * D:(zs + 1) * D],
                    out_offset=None,
                    in_=xnat,
                    in_offset=IndirectOffsetOnAxis(stok_sb[:, e_ * 4 + jt: e_ * 4 + jt + 1], 0),
                    bounds_check=None,
                    oob_is_err=False), inc=(p.dsem(f"zg{zs}"), 16))
                p.wait("tensor", f"zg{zs}", gz)
                p.wait("tensor", "v", ybw)
                for dd in range(DT):
                    tps = 6 + dd % 2
                    p.wait("tensor", "v", tslot_free[dd % 2])
                    t2 = p.do("tensor", lambda e, dd=dd, zs=zs, tps=tps: e.transpose(
                        ps_o[tps][:, 0:128],
                        zbuf[:, zs * D + dd * 128: zs * D + (dd + 1) * 128], idF_sb[:]),
                        inc=("t", 1))
                    p.wait("vector", "t", t2)
                    ztv = tslot_free[dd % 2] = p.do("vector",
                        lambda e, dd=dd, jt=jt, tps=tps: e.tensor_copy(
                            out=ZTe[:, dd * 512 + jt * 128: dd * 512 + jt * 128 + 128],
                            in_=ps_o[tps][:, 0:128]), inc=("v", 1))
                zslot_free[zs] = p.cnt["t"]
            p.wait("tensor", "v", ztv)
            pair_free = {0: None, 1: None}
            for ht in range(HT1):
                pr = ht % 2
                psA, psB = ps_o[pr * 2], ps_o[pr * 2 + 1]
                gsl = slice(pr * 512, pr * 512 + 512)
                sl0 = w01_panel(w0, e_, ht)
                p.wait("tensor", f"w01r{sl0}", r_loaded[sl0])
                p.wait("tensor", "v", pair_free[pr])
                tv0 = None
                for k in range(DT):
                    tv0 = p.do("tensor", lambda e, sl0=sl0, k=k, psA=psA: e.matmul(
                        psA[:],
                        w01ring[:, sl0 * DT * 128 + k * 128: sl0 * DT * 128 + (k + 1) * 128],
                        ZTe[:, k * 512:(k + 1) * 512],
                        start=(k == 0), stop=(k == DT - 1)), inc=("t", 1))
                r_free[sl0] = p.cnt["t"]
                sl1 = w01_panel(w1, e_, ht)
                p.wait("tensor", f"w01r{sl1}", r_loaded[sl1])
                tv1 = None
                for k in range(DT):
                    tv1 = p.do("tensor", lambda e, sl1=sl1, k=k, psB=psB: e.matmul(
                        psB[:],
                        w01ring[:, sl1 * DT * 128 + k * 128: sl1 * DT * 128 + (k + 1) * 128],
                        ZTe[:, k * 512:(k + 1) * 512],
                        start=(k == 0), stop=(k == DT - 1)), inc=("t", 1))
                r_free[sl1] = p.cnt["t"]
                p.wait("scalar", "t", tv0)
                s1 = p.do("scalar", lambda e, psA=psA, gsl=gsl: e.activation(
                    out=sqn[:, gsl], in_=psA[:], func=AF.Square, bias=0.0, scale=1.0),
                    inc=("s", 1))
                p.wait("vector", "s", s1)
                p.do("vector", lambda e, psA=psA, gsl=gsl: e.scalar_tensor_tensor(
                    out=gt1[:, gsl], in0=sqn[:, gsl], scalar=GELU_C, in1=psA[:],
                    op0=ALU.mult, op1=ALU.mult))
                v1 = p.do("vector", lambda e, psA=psA, gsl=gsl: e.scalar_tensor_tensor(
                    out=gt1[:, gsl], in0=gt1[:, gsl], scalar=1.0, in1=psA[:],
                    op0=ALU.mult, op1=ALU.add), inc=("v", 1))
                p.wait("scalar", "v", v1)
                s2 = p.do("scalar", lambda e, gsl=gsl: e.activation(
                    out=gt1[:, gsl], in_=gt1[:, gsl], func=AF.Sigmoid, bias=0.0,
                    scale=GELU_S), inc=("s", 1))
                p.wait("vector", "s", s2)
                p.wait("vector", "t", tv1)
                p.do("vector", lambda e, psA=psA, gsl=gsl: e.tensor_mul(
                    out=gt1[:, gsl], in0=gt1[:, gsl], in1=psA[:]))
                sv_hv = p.do("vector", lambda e, ht=ht, psB=psB, gsl=gsl: e.tensor_mul(
                    out=hid[:, ht * 512:(ht + 1) * 512], in0=gt1[:, gsl], in1=psB[:]),
                    inc=("v", 1))
                pair_free[pr] = sv_hv
                sv_h = sv_hv
            p.wait("tensor", "v", sv_hv)
            p.wait("tensor", "v", ybw)
            tvy = None
            for ht in range(HT1):
                slw = wo_panel(e_, ht)
                p.wait("tensor", f"wor{slw}", r2_loaded[slw])
                for jt in range(4):
                    for mc in range(2):
                        tvy = p.do("tensor",
                            lambda e, slw=slw, ht=ht, jt=jt, mc=mc: e.matmul(
                                ps_o[jt * 2 + mc][:],
                                hid[:, ht * 512 + jt * 128: ht * 512 + jt * 128 + 128],
                                woring[:, slw * D + mc * 512: slw * D + mc * 512 + 512],
                                start=(ht == 0), stop=(ht == HT1 - 1)), inc=("t", 1))
                r2_free[slw] = p.cnt["t"]
            p.wait("vector", "t", tvy)
            p.wait("vector", "yd_d", ydma)
            ybw = None
            for jt in range(4):
                for mc in range(2):
                    ybw = p.do("vector", lambda e, jt=jt, mc=mc: e.tensor_copy(
                        out=ybuf[:, jt * D + mc * 512: jt * D + mc * 512 + 512],
                        in_=ps_o[jt * 2 + mc][:]), inc=("v", 1))
            p.wait("gpsimd", "v", ybw)
            for jt in range(4):
                ydma = p.do("gpsimd", lambda e, e_=e_, jt=jt: e.dma_start(
                    out=ydram[e_ * 512 + jt * 128: e_ * 512 + (jt + 1) * 128, :],
                    in_=ybuf[:, jt * D:(jt + 1) * D]), inc=(p.dsem("yd_d"), 16))

        p.wait("gpsimd", "yd_d", ydma)
        zz = p.do("vector", lambda e: e.memset(ybuf[0:1, 0:D], 0.0), inc=("v", 1))
        p.wait("gpsimd", "v", zz)
        ydma = p.do("gpsimd", lambda e: e.dma_start(
            out=ydram[NSLOT:NSLOT + 1, :], in_=ybuf[0:1, 0:D]), inc=("yd_d", 16))
        p.wait("gpsimd", "yd_d", ydma)
        outv = None
        g1t_free = None
        xrt_free = None
        for tt in range(NT):
            p.wait("gpsimd", "v", g1t_free)
            p.wait("sync", "v", xrt_free)
            zm = p.do("gpsimd", lambda e: e.memset(g12[:], 0.0), inc=("g", 1))
            p.wait("gpsimd", "g", zm)
            gg = None
            for half in range(2):
                gg = p.do("gpsimd", lambda e, half=half, tt=tt: e.indirect_dma_start(
                    out=g12[:, half * D:(half + 1) * D],
                    out_offset=None,
                    in_=ydram,
                    in_offset=IndirectOffsetOnAxis(jint[:, half * NT + tt: half * NT + tt + 1], 0),
                    bounds_check=None,
                    oob_is_err=False), inc=(p.dsem("g12_d"), 16))
            vr = p.do("sync", lambda e, tt=tt: e.dma_start(
                out=xrt[:], in_=xnat[tt * 128:(tt + 1) * 128, :]), inc=(p.dsem("xr_d"), 16))
            p.wait("vector", "g12_d", gg)
            p.wait("vector", "xr_d", vr)
            p.do("vector", lambda e, tt=tt: e.tensor_scalar(
                out=xot[:], in0=g12[:, D:2 * D], scalar1=gcol[:, 3 * NT + tt: 3 * NT + tt + 1],
                scalar2=None, op0=ALU.mult))
            p.do("vector", lambda e, tt=tt: e.scalar_tensor_tensor(
                out=xot[:], in0=g12[:, 0:D], scalar=gcol[:, 2 * NT + tt: 2 * NT + tt + 1],
                in1=xot[:], op0=ALU.mult, op1=ALU.add))
            g1t_free = p.do("vector", lambda e: e.tensor_add(
                out=xot[:], in0=xot[:], in1=xrt[:]), inc=("v", 1))
            xrt_free = g1t_free
            _rv = p.do("vector", lambda e: e.reduce_sum(out=pcol[:, 0:1], in_=xot[:], axis=AX.X), inc=("v", 1))
            p.wait("vector", "v", _rv)
            vv = p.do("vector", lambda e: e.tensor_scalar(
                out=pcol[:, 0:1], in0=pcol[:, 0:1], scalar1=1.0 / D, scalar2=None,
                op0=ALU.mult), inc=("v", 1))
            p.do("vector", lambda e: e.tensor_scalar(
                out=xot[:], in0=xot[:], scalar1=pcol[:, 0:1], scalar2=None, op0=ALU.subtract))
            sv = p.do("vector", lambda e: e.tensor_mul(out=sqn[:], in0=xot[:], in1=xot[:]),
                      inc=("v", 1))
            _rv = p.do("vector", lambda e: e.reduce_sum(out=pcol[:, 1:2], in_=sqn[:], axis=AX.X), inc=("v", 1))
            p.wait("vector", "v", _rv)
            sv = p.do("vector", lambda e: e.tensor_scalar(
                out=pcol[:, 1:2], in0=pcol[:, 1:2], scalar1=1.0 / D, scalar2=None,
                op0=ALU.mult), inc=("v", 1))
            p.wait("scalar", "v", sv)
            p.do("scalar", lambda e: e.activation(
                out=pcol[:, 1:2], in_=pcol[:, 1:2], func=AF.Ln, bias=epsc[:], scale=1.0))
            ssv = p.do("scalar", lambda e: e.activation(
                out=pcol[:, 1:2], in_=pcol[:, 1:2], func=AF.Exp, bias=0.0, scale=-0.5),
                inc=("s", 1))
            p.wait("vector", "s", ssv)
            outv = p.do("vector", lambda e: e.tensor_scalar(
                out=xot[:], in0=xot[:], scalar1=pcol[:, 1:2], scalar2=None, op0=ALU.mult),
                inc=("v", 1))
            p.wait("sync", "v", outv)
            vo = p.do("sync", lambda e, tt=tt: e.dma_start(
                out=ynat[tt * 128:(tt + 1) * 128, :], in_=xot[:]), inc=(p.dsem("out_d"), 16))
            p.wait("vector", "out_d", vo)
        p.wait("sync", "out_d", p.cnt["out_d"])

        p.barrier()


def build_ffn1():
    nc = bass.Bass("TRN2", target_bir_lowering=False)
    xT = nc.dram_tensor("xT", [D, T], F32, kind="ExternalInput").ap()
    xnat = nc.dram_tensor("xnat", [T + 1, D], F32, kind="ExternalInput").ap()
    wr = nc.dram_tensor("wr", [128, DT * 8], F32, kind="ExternalInput").ap()
    w0 = nc.dram_tensor("w0", [8, HT1, DT, 128, 128], F32, kind="ExternalInput").ap()
    w1 = nc.dram_tensor("w1", [8, HT1, DT, 128, 128], F32, kind="ExternalInput").ap()
    wo = nc.dram_tensor("wo", [8, HT1, 128, D], F32, kind="ExternalInput").ap()
    id8 = nc.dram_tensor("id8", [8, 8], F32, kind="ExternalInput").ap()
    idF = nc.dram_tensor("idF", [128, 128], F32, kind="ExternalInput").ap()
    trii = nc.dram_tensor("trii", [128, 128], F32, kind="ExternalInput").ap()
    eio = nc.dram_tensor("eio", [128, 8], F32, kind="ExternalInput").ap()
    tokio = nc.dram_tensor("tokio", [128, NT], I32, kind="ExternalInput").ap()
    ynat = nc.dram_tensor("ynat", [T, D], F32, kind="ExternalOutput").ap()

    stok = nc.dram_tensor("stok", [NSLOT + 1, 1], I32, kind="Internal").ap()
    ydram = nc.dram_tensor("ydram", [NSLOT + 1, D], BF16, kind="Internal").ap()

    io = {k: v for k, v in zip(IO_FFN1, [xT, xnat, wr, w0, w1, wo, id8, idF,
                                         trii, eio, tokio, ynat, stok, ydram])}
    with ExitStack() as st:
        p = mk_prog(nc, st)
        emit_ffn1(nc, p, io)
        block = st.enter_context(nc.Block())
        p.run(block)
    return nc



def _block_lhsT(W):
    din, dout = W.shape
    B = W.reshape(din // 128, 128, dout // 128, 128)
    return np.ascontiguousarray(B.transpose(2, 0, 1, 3))


def prep_attn(l, d):
    if l == 0:
        Wq = np.ascontiguousarray(d["L0_attn_Wq"].transpose(1, 0, 2).reshape(D, D))
        Wk = np.ascontiguousarray(d["L0_attn_Wk"].transpose(1, 0, 2).reshape(D, D))
        Wv = np.ascontiguousarray(d["L0_attn_Wv"].transpose(1, 0, 2).reshape(D, D))
        bq_ = d["L0_attn_bq"].reshape(D)
        bk_ = d["L0_attn_bk"].reshape(D)
        bv_ = d["L0_attn_bv"].reshape(D)
        Wo = d["L0_attn_Wo"]
        bo_ = d["L0_attn_bo"]
    else:
        Wqkv = d["L1_attn_W_qkv"]
        bqkv = d["L1_attn_b_qkv"]
        Wq, Wk, Wv = Wqkv[:, 0:D], Wqkv[:, D:2 * D], Wqkv[:, 2 * D:3 * D]
        bq_, bk_, bv_ = bqkv[0:D], bqkv[D:2 * D], bqkv[2 * D:3 * D]
        Wo = d["L1_attn_W_o"]
        bo_ = d["L1_attn_b_o"]
    return {
        "wq": _block_lhsT(np.ascontiguousarray(Wq)),
        "wk": _block_lhsT(np.ascontiguousarray(Wk)),
        "wv": np.ascontiguousarray(np.ascontiguousarray(Wv).reshape(DT, 128, D)),
        "wo": _block_lhsT(Wo),
        "bq": np.ascontiguousarray(bq_.reshape(DT, 128).T),
        "bk": np.ascontiguousarray(bk_.reshape(DT, 128).T),
        "bvr": np.ascontiguousarray(bv_.reshape(1, D)),
        "bo": np.ascontiguousarray(bo_.reshape(DT, 128).T),
        "triu": np.triu(np.ones((128, 128))).astype(BF16NP),
    }


def prep_ffn0(d):
    W1 = d["L0_ffn_W1"]
    W2 = d["L0_ffn_W2"]
    w1b = np.ascontiguousarray(
        W1.reshape(8, DT, 128, NF, 128).transpose(0, 3, 1, 2, 4))
    w2b = np.ascontiguousarray(
        W2.reshape(8, NF, 128, DT, 128).transpose(0, 3, 1, 2, 4))
    return {
        "wr": np.ascontiguousarray(d["L0_ffn_Wr"].reshape(DT, 128, 8).transpose(1, 0, 2).reshape(128, DT * 8)),
        "br": np.ascontiguousarray(d["L0_ffn_br"].reshape(8, 1)),
        "w1": w1b,
        "b1": np.ascontiguousarray(d["L0_ffn_b1"].reshape(8, NF, 128).transpose(0, 2, 1)),
        "w2": w2b,
        "b2": np.ascontiguousarray(d["L0_ffn_b2"].reshape(8, DT, 128).transpose(0, 2, 1)),
        "id8": np.eye(8, dtype=np.float32),
        "idF": np.eye(128, dtype=np.float32),
        "sel8": np.ascontiguousarray(
            np.eye(8, dtype=np.float32).repeat(128, axis=1).reshape(8, 8 * 128)),
    }


def prep_ffn1(d):
    w0 = d["L1_ffn_w0_kernel"]
    w1 = d["L1_ffn_w1_kernel"]
    wo = d["L1_ffn_wo_kernel"]
    pad = H1P - 5464
    w0p = np.concatenate([w0, np.zeros((8, D, pad), np.float32)], axis=2)
    w1p = np.concatenate([w1, np.zeros((8, D, pad), np.float32)], axis=2)
    wop = np.concatenate([wo, np.zeros((8, pad, D), np.float32)], axis=1)
    w0b = np.ascontiguousarray(w0p.reshape(8, DT, 128, HT1, 128).transpose(0, 3, 1, 2, 4))
    w1b = np.ascontiguousarray(w1p.reshape(8, DT, 128, HT1, 128).transpose(0, 3, 1, 2, 4))
    wob = np.ascontiguousarray(wop.reshape(8, HT1, 128, D))
    toki = (np.arange(T, dtype=np.int32).reshape(NT, 128).T).copy()
    return {
        "wr": np.ascontiguousarray(
            d["L1_ffn_router_w"].reshape(DT, 128, 8).transpose(1, 0, 2).reshape(128, DT * 8)),
        "w0": w0b, "w1": w1b, "wo": wob,
        "id8": np.eye(8, dtype=np.float32),
        "idF": np.eye(128, dtype=np.float32),
        "trii": np.triu(np.ones((128, 128), dtype=np.float32)),
        "eio": np.broadcast_to(np.arange(8, dtype=np.float32), (128, 8)).copy(),
        "tokio": toki,
    }




def build_fused():
    nc = bass.Bass("TRN2", target_bir_lowering=False)
    xin = nc.dram_tensor("xin", [D, T], F32, kind="ExternalInput").ap()
    xpref0 = nc.dram_tensor("xpref0", [D, T], F32, kind="ExternalInput").ap()
    pbias = nc.dram_tensor("pbias", [128, 1], F32, kind="ExternalInput").ap()
    pbneg = nc.dram_tensor("pbneg", [128, 1], F32, kind="ExternalInput").ap()
    attn_io = {}
    for l in range(2):
        for k, shp, dt in [("wq", [DT, DT, 128, 128], F32), ("wk", [DT, DT, 128, 128], F32),
                           ("wv", [DT, 128, D], F32), ("wo", [DT, DT, 128, 128], F32),
                           ("bq", [128, DT], F32), ("bk", [128, DT], F32),
                           ("bvr", [1, D], F32), ("bo", [128, DT], F32)]:
            attn_io[(l, k)] = nc.dram_tensor(f"a{l}_{k}", shp, dt, kind="ExternalInput").ap()
    triu = nc.dram_tensor("triu", [128, 128], BF16, kind="ExternalInput").ap()
    f0 = {}
    for k, shp in [("wr", [128, DT * 8]), ("br", [8, 1]), ("w1", [8, NF, DT, 128, 128]),
                   ("b1", [8, 128, NF]), ("w2", [8, DT, NF, 128, 128]), ("b2", [8, 128, DT]),
                   ("sel8", [8, 8 * 128])]:
        f0[k] = nc.dram_tensor(f"f0_{k}", shp, F32, kind="ExternalInput").ap()
    f1 = {}
    for k, shp in [("wr", [128, DT * 8]), ("w0", [8, HT1, DT, 128, 128]),
                   ("w1", [8, HT1, DT, 128, 128]), ("wo", [8, HT1, 128, D])]:
        f1[k] = nc.dram_tensor(f"f1_{k}", shp, F32, kind="ExternalInput").ap()
    id8 = nc.dram_tensor("id8", [8, 8], F32, kind="ExternalInput").ap()
    idF = nc.dram_tensor("idF", [128, 128], F32, kind="ExternalInput").ap()
    trii = nc.dram_tensor("trii", [128, 128], F32, kind="ExternalInput").ap()
    eio = nc.dram_tensor("eio", [128, 8], F32, kind="ExternalInput").ap()
    tokio = nc.dram_tensor("tokio", [128, NT], I32, kind="ExternalInput").ap()
    ynat = nc.dram_tensor("ynat", [T, D], F32, kind="ExternalOutput").ap()

    x1p = nc.dram_tensor("x1p", [D, T], F32, kind="Internal").ap()
    x2p = nc.dram_tensor("x2p", [D, T], F32, kind="Internal").ap()
    x1s = nc.dram_tensor("x1s", [D, T], F32, kind="Internal").ap()
    x2s = nc.dram_tensor("x2s", [D, T], F32, kind="Internal").ap()
    x3s = nc.dram_tensor("x3s", [D, T], F32, kind="Internal").ap()
    x3nat = nc.dram_tensor("x3nat", [T + 1, D], F32, kind="Internal").ap()
    stok = nc.dram_tensor("stok", [NSLOT + 1, 1], I32, kind="Internal").ap()
    ydram = nc.dram_tensor("ydram", [NSLOT + 1, D], BF16, kind="Internal").ap()

    def aio(l, xT, xprefT, pb, yT):
        d = {k: attn_io[(l, k)] for k in ["wq", "wk", "wv", "wo", "bq", "bk", "bvr", "bo"]}
        d.update({"xT": xT, "xprefT": xprefT, "pbias": pb, "triu": triu, "yT": yT})
        return d

    with ExitStack() as st:
        p = mk_prog(nc, st)
        emit_attn(nc, p, aio(0, xpref0, xpref0, pbneg, x1p), use_prefix=False)
        io0 = dict(f0); io0.update({"xT": x1p, "id8": id8, "idF": idF, "yT": x2p})
        emit_ffn0(nc, p, io0)
        emit_attn(nc, p, aio(0, xin, xpref0, pbias, x1s))
        io0 = dict(f0); io0.update({"xT": x1s, "id8": id8, "idF": idF, "yT": x2s})
        emit_ffn0(nc, p, io0)
        emit_attn(nc, p, aio(1, x2s, x2p, pbias, x3s))
        emit_x3nat(nc, p, x3s, x3nat, idF)
        io1 = dict(f1); io1.update({"xT": x3s, "xnat": x3nat, "id8": id8, "idF": idF,
                                    "trii": trii, "eio": eio, "tokio": tokio,
                                    "ynat": ynat, "stok": stok, "ydram": ydram})
        emit_ffn1(nc, p, io1)
        block = st.enter_context(nc.Block())
        p.run(block)
    return nc


def emit_x3nat(nc, p, x3s, x3nat, idF):
    with ExitStack() as st:
        en = st.enter_context
        xb = en(_sb(nc, "x3b", [128, DT * T], F32))
        natb = en(_sb(nc, "natb", [128, 2 * D], F32))
        idF_sb = en(_sb(nc, "idF3_sb", [128, 128], F32))
        pst = [en(_ps(nc, f"pst{i}", [128, 128], F32)) for i in range(4)]
        p.dsem("x3l_d"); p.dsem("x3n0"); p.dsem("x3n1"); p.dsem("idf3_d")
        vi = p.do("sync", lambda e: e.dma_start(out=idF_sb[:], in_=idF), inc=("idf3_d", 16))
        vl = None
        for k in range(DT):
            vl = p.do("sync", lambda e, k=k: e.dma_start(
                out=xb[:, k * T:(k + 1) * T], in_=x3s[k * 128:(k + 1) * 128, :]),
                inc=("x3l_d", 16))
        p.wait("tensor", "x3l_d", vl)
        p.wait("tensor", "idf3_d", vi)
        vn = None
        cpv = None
        bank_cp = [None] * 4
        nslot_free = [None, None]
        for tt in range(NT):
            ns = tt % 2
            p.wait("vector", f"x3n{ns}", nslot_free[ns])
            for dd in range(DT):
                bk_ = dd % 4
                p.wait("tensor", "v", bank_cp[bk_])
                tv = p.do("tensor", lambda e, tt=tt, dd=dd, bk_=bk_: e.transpose(
                    pst[bk_][:], xb[:, dd * T + tt * 128: dd * T + tt * 128 + 128],
                    idF_sb[:]), inc=("t", 1))
                p.wait("vector", "t", tv)
                cpv = bank_cp[bk_] = p.do("vector",
                    lambda e, dd=dd, bk_=bk_, ns=ns: e.tensor_copy(
                        out=natb[:, ns * D + dd * 128: ns * D + (dd + 1) * 128],
                        in_=pst[bk_][:]), inc=("v", 1))
            p.wait("sync", "v", cpv)
            nslot_free[ns] = p.do("sync", lambda e, tt=tt, ns=ns: e.dma_start(
                out=x3nat[tt * 128:(tt + 1) * 128, :],
                in_=natb[:, ns * D:(ns + 1) * D]), inc=(f"x3n{ns}", 16))
        p.wait("vector", "x3n0", nslot_free[0])
        zv = p.do("vector", lambda e: e.memset(natb[0:1, 0:D], 0.0), inc=("v", 1))
        p.wait("sync", "v", zv)
        vn = p.do("sync", lambda e: e.dma_start(
            out=x3nat[T:T + 1, :], in_=natb[0:1, 0:D]), inc=("x3n0", 16))
        p.wait("sync", "x3n0", vn)
        p.wait("sync", "x3n1", nslot_free[1])
        p.barrier()

_NC_CACHE = {}


def _get_nc(name, builder):
    if name not in _NC_CACHE:
        _NC_CACHE[name] = builder()
    return _NC_CACHE[name]


def _chunks(x):
    xs, xp = [], []
    for c in range(NCORES):
        b, h = c // 2, c % 2
        xs.append(np.ascontiguousarray(x[b, h * T:(h + 1) * T, :].T))
        xp.append(np.ascontiguousarray(x[b, 0:T, :].T))
    return xs, xp


def _run_attn(x, aw):
    nc = _get_nc("attn", build_attn)
    xs, xp = _chunks(x)
    in_maps = []
    for c in range(NCORES):
        pb = np.zeros((128, 1), np.float32) if c % 2 == 1 else np.full((128, 1), -30000.0, np.float32)
        m = {"xT": xs[c], "xprefT": xp[c], "pbias": pb}
        m.update(aw)
        in_maps.append(m)
    res = run_bass_kernel_spmd(nc, in_maps, core_ids=list(range(NCORES)))
    out = np.empty((4, 4096, D), np.float32)
    for c in range(NCORES):
        b, h = c // 2, c % 2
        out[b, h * T:(h + 1) * T, :] = np.asarray(res.results[c]["yT"]).reshape(D, T).T
    return out


def _run_ffn0(x, fw):
    nc = _get_nc("ffn0", build_ffn0)
    xs, _ = _chunks(x)
    in_maps = []
    for c in range(NCORES):
        m = {"xT": xs[c]}
        m.update(fw)
        in_maps.append(m)
    res = run_bass_kernel_spmd(nc, in_maps, core_ids=list(range(NCORES)))
    out = np.empty((4, 4096, D), np.float32)
    for c in range(NCORES):
        b, h = c // 2, c % 2
        out[b, h * T:(h + 1) * T, :] = np.asarray(res.results[c]["yT"]).reshape(D, T).T
    return out


def _run_ffn1(x, fw):
    nc = _get_nc("ffn1", build_ffn1)
    xs, _ = _chunks(x)
    in_maps = []
    for c in range(NCORES):
        b, h = c // 2, c % 2
        xn_pad = np.concatenate(
            [x[b, h * T:(h + 1) * T, :], np.zeros((1, D), np.float32)], axis=0)
        m = {"xT": xs[c], "xnat": np.ascontiguousarray(xn_pad)}
        m.update(fw)
        in_maps.append(m)
    res = run_bass_kernel_spmd(nc, in_maps, core_ids=list(range(NCORES)))
    out = np.empty((4, 4096, D), np.float32)
    for c in range(NCORES):
        b, h = c // 2, c % 2
        out[b, h * T:(h + 1) * T, :] = np.asarray(res.results[c]["ynat"]).reshape(T, D)
    return out



def build_fused_l0():
    nc = bass.Bass("TRN2", target_bir_lowering=False)
    xin = nc.dram_tensor("xin", [D, T], F32, kind="ExternalInput").ap()
    xpref0 = nc.dram_tensor("xpref0", [D, T], F32, kind="ExternalInput").ap()
    pbias = nc.dram_tensor("pbias", [128, 1], F32, kind="ExternalInput").ap()
    a = {}
    for k, shp, dt in [("wq", [DT, DT, 128, 128], F32), ("wk", [DT, DT, 128, 128], F32),
                       ("wv", [DT, 128, D], F32), ("wo", [DT, DT, 128, 128], F32),
                       ("bq", [128, DT], F32), ("bk", [128, DT], F32),
                       ("bvr", [1, D], F32), ("bo", [128, DT], F32)]:
        a[k] = nc.dram_tensor(f"a0_{k}", shp, dt, kind="ExternalInput").ap()
    triu = nc.dram_tensor("triu", [128, 128], BF16, kind="ExternalInput").ap()
    f0 = {}
    for k, shp in [("wr", [128, DT * 8]), ("br", [8, 1]), ("w1", [8, NF, DT, 128, 128]),
                   ("b1", [8, 128, NF]), ("w2", [8, DT, NF, 128, 128]), ("b2", [8, 128, DT]),
                   ("sel8", [8, 8 * 128])]:
        f0[k] = nc.dram_tensor(f"f0_{k}", shp, F32, kind="ExternalInput").ap()
    id8 = nc.dram_tensor("id8", [8, 8], F32, kind="ExternalInput").ap()
    idF = nc.dram_tensor("idF", [128, 128], F32, kind="ExternalInput").ap()
    yT = nc.dram_tensor("yT", [D, T], F32, kind="ExternalOutput").ap()
    x1s = nc.dram_tensor("x1s", [D, T], F32, kind="Internal").ap()
    with ExitStack() as st:
        p = mk_prog(nc, st)
        a.update({"xT": xin, "xprefT": xpref0, "pbias": pbias, "triu": triu, "yT": x1s})
        emit_attn(nc, p, a)
        f0.update({"xT": x1s, "id8": id8, "idF": idF, "yT": yT})
        emit_ffn0(nc, p, f0)
        block = st.enter_context(nc.Block())
        p.run(block)
    return nc


def build_fused_l1():
    nc = bass.Bass("TRN2", target_bir_lowering=False)
    xin = nc.dram_tensor("xin", [D, T], F32, kind="ExternalInput").ap()
    xpref = nc.dram_tensor("xpref", [D, T], F32, kind="ExternalInput").ap()
    pbias = nc.dram_tensor("pbias", [128, 1], F32, kind="ExternalInput").ap()
    a = {}
    for k, shp, dt in [("wq", [DT, DT, 128, 128], F32), ("wk", [DT, DT, 128, 128], F32),
                       ("wv", [DT, 128, D], F32), ("wo", [DT, DT, 128, 128], F32),
                       ("bq", [128, DT], F32), ("bk", [128, DT], F32),
                       ("bvr", [1, D], F32), ("bo", [128, DT], F32)]:
        a[k] = nc.dram_tensor(f"a1_{k}", shp, dt, kind="ExternalInput").ap()
    triu = nc.dram_tensor("triu", [128, 128], BF16, kind="ExternalInput").ap()
    f1 = {}
    for k, shp in [("wr", [128, DT * 8]), ("w0", [8, HT1, DT, 128, 128]),
                   ("w1", [8, HT1, DT, 128, 128]), ("wo", [8, HT1, 128, D])]:
        f1[k] = nc.dram_tensor(f"f1_{k}", shp, F32, kind="ExternalInput").ap()
    id8 = nc.dram_tensor("id8", [8, 8], F32, kind="ExternalInput").ap()
    idF = nc.dram_tensor("idF", [128, 128], F32, kind="ExternalInput").ap()
    trii = nc.dram_tensor("trii", [128, 128], F32, kind="ExternalInput").ap()
    eio = nc.dram_tensor("eio", [128, 8], F32, kind="ExternalInput").ap()
    tokio = nc.dram_tensor("tokio", [128, NT], I32, kind="ExternalInput").ap()
    ynat = nc.dram_tensor("ynat", [T, D], F32, kind="ExternalOutput").ap()
    x3s = nc.dram_tensor("x3s", [D, T], F32, kind="Internal").ap()
    x3nat = nc.dram_tensor("x3nat", [T + 1, D], F32, kind="Internal").ap()
    stok = nc.dram_tensor("stok", [NSLOT + 1, 1], I32, kind="Internal").ap()
    ydram = nc.dram_tensor("ydram", [NSLOT + 1, D], BF16, kind="Internal").ap()
    with ExitStack() as st:
        p = mk_prog(nc, st)
        a.update({"xT": xin, "xprefT": xpref, "pbias": pbias, "triu": triu, "yT": x3s})
        emit_attn(nc, p, a)
        emit_x3nat(nc, p, x3s, x3nat, idF)
        f1.update({"xT": x3s, "xnat": x3nat, "id8": id8, "idF": idF, "trii": trii,
                   "eio": eio, "tokio": tokio, "ynat": ynat, "stok": stok,
                   "ydram": ydram})
        emit_ffn1(nc, p, f1)
        block = st.enter_context(nc.Block())
        p.run(block)
    return nc


def kernel2(**inputs):
    x = inputs["L0_attn_x"].astype(np.float32, copy=False)
    aw = [prep_attn(0, inputs), prep_attn(1, inputs)]
    fw0 = prep_ffn0(inputs)
    fw1 = prep_ffn1(inputs)
    nc1 = _get_nc("fused_l0", build_fused_l0)
    nc2 = _get_nc("fused_l1", build_fused_l1)
    xs, xp = _chunks(x)
    in_maps = []
    for c in range(NCORES):
        pb = (np.zeros((128, 1), np.float32) if c % 2 == 1
              else np.full((128, 1), -30000.0, np.float32))
        m = {"xin": xs[c], "xpref0": xp[c], "pbias": pb, "triu": aw[0]["triu"],
             "id8": fw0["id8"], "idF": fw0["idF"]}
        for k in ["wq", "wk", "wv", "wo", "bq", "bk", "bvr", "bo"]:
            m[f"a0_{k}"] = aw[0][k]
        for k in ["wr", "br", "w1", "b1", "w2", "b2", "sel8"]:
            m[f"f0_{k}"] = fw0[k]
        in_maps.append(m)
    res = run_bass_kernel_spmd(nc1, in_maps, core_ids=list(range(NCORES)))
    x2c = [np.asarray(res.results[c]["yT"]).reshape(D, T) for c in range(NCORES)]
    in_maps = []
    for c in range(NCORES):
        pb = (np.zeros((128, 1), np.float32) if c % 2 == 1
              else np.full((128, 1), -30000.0, np.float32))
        m = {"xin": x2c[c], "xpref": x2c[c - c % 2], "pbias": pb,
             "triu": aw[1]["triu"], "id8": fw0["id8"], "idF": fw0["idF"],
             "trii": fw1["trii"], "eio": fw1["eio"], "tokio": fw1["tokio"]}
        for k in ["wq", "wk", "wv", "wo", "bq", "bk", "bvr", "bo"]:
            m[f"a1_{k}"] = aw[1][k]
        for k in ["wr", "w0", "w1", "wo"]:
            m[f"f1_{k}"] = fw1[k]
        in_maps.append(m)
    res = run_bass_kernel_spmd(nc2, in_maps, core_ids=list(range(NCORES)))
    out = np.empty((4, 4096, D), np.float32)
    for c in range(NCORES):
        b, h = c // 2, c % 2
        out[b, h * T:(h + 1) * T, :] = np.asarray(res.results[c]["ynat"]).reshape(T, D)
    return out

def kernel_unfused(**inputs):
    x = inputs["L0_attn_x"].astype(np.float32, copy=False)
    aw0 = prep_attn(0, inputs)
    fw0 = prep_ffn0(inputs)
    aw1 = prep_attn(1, inputs)
    fw1 = prep_ffn1(inputs)
    x1 = _run_attn(x, aw0)
    x2 = _run_ffn0(x1, fw0)
    x3 = _run_attn(x2, aw1)
    x4 = _run_ffn1(x3, fw1)
    return x4


def kernel(**inputs):
    return kernel_1launch(**inputs)


def kernel_1launch(**inputs):
    x = inputs["L0_attn_x"].astype(np.float32, copy=False)
    aw = [prep_attn(0, inputs), prep_attn(1, inputs)]
    fw0 = prep_ffn0(inputs)
    fw1 = prep_ffn1(inputs)
    nc = _get_nc("fused", build_fused)
    xs, xp = _chunks(x)
    in_maps = []
    for c in range(NCORES):
        pb = (np.zeros((128, 1), np.float32) if c % 2 == 1
              else np.full((128, 1), -30000.0, np.float32))
        m = {"xin": xs[c], "xpref0": xp[c], "pbias": pb,
             "pbneg": np.full((128, 1), -30000.0, np.float32),
             "triu": aw[0]["triu"],
             "id8": fw0["id8"], "idF": fw0["idF"], "trii": fw1["trii"],
             "eio": fw1["eio"], "tokio": fw1["tokio"]}
        for l in range(2):
            for k in ["wq", "wk", "wv", "wo", "bq", "bk", "bvr", "bo"]:
                m[f"a{l}_{k}"] = aw[l][k]
        for k in ["wr", "br", "w1", "b1", "w2", "b2", "sel8"]:
            m[f"f0_{k}"] = fw0[k]
        for k in ["wr", "w0", "w1", "wo"]:
            m[f"f1_{k}"] = fw1[k]
        in_maps.append(m)
    res = run_bass_kernel_spmd(nc, in_maps, core_ids=list(range(NCORES)))
    out = np.empty((4, 4096, D), np.float32)
    for c in range(NCORES):
        b, h = c // 2, c % 2
        out[b, h * T:(h + 1) * T, :] = np.asarray(res.results[c]["ynat"]).reshape(T, D)
    return out


# version 13 (final)
# speedup vs baseline: 1.0622x; 1.0622x over previous
import numpy as np
import ml_dtypes
from contextlib import ExitStack

import concourse.bass as bass
import concourse.mybir as mybir
from concourse.bass_core_rust import IndirectOffsetOnAxis
from concourse.bass_utils import run_bass_kernel_spmd

F32 = mybir.dt.float32
BF16 = mybir.dt.bfloat16
I32 = mybir.dt.int32
AF = mybir.ActivationFunctionType
ALU = mybir.AluOpType
AX = mybir.AxisListType
BF16NP = ml_dtypes.bfloat16

NCORES = 8
T = 2048
D = 1024
DT = D // 128
NT = T // 128
NH = 16
HD = 64
CTX = 4096
F0 = 4096
NF = F0 // 128
H1P = 5504
HT1 = H1P // 128
CAP = 512
NSLOT = 4096
LNEPS = 1e-5
GELU_C = 0.044715
GELU_S = 1.5957691216

_UID = [0]


def _sb(nc, name, shape, dtype):
    _UID[0] += 1
    return nc.sbuf_tensor(f"{name}_u{_UID[0]}", shape, dtype)


def _ps(nc, name, shape, dtype):
    _UID[0] += 1
    return nc.psum_tensor(f"{name}_u{_UID[0]}", shape, dtype)


ENGINES = ("sync", "gpsimd", "tensor", "vector", "scalar")


class Prog:

    def __init__(self, nc):
        self.nc = nc
        self.ops = {e: [] for e in ENGINES}
        self.cnt = {}
        self.sems = {}

    def sem(self, st, name):
        self.sems[name] = st.enter_context(self.nc.semaphore(name))
        self.cnt[name] = 0

    def dsem(self, name):
        if name not in self.sems:
            self.sems[name] = self._st.enter_context(self.nc.semaphore(name))
            self.cnt[name] = 0
        return name

    STRICT = {"vector": "v", "scalar": "s"}

    def do(self, eng, fn, inc=None):
        if inc is None and eng in self.STRICT:
            inc = (self.STRICT[eng], 1)
        if inc is None:
            self.ops[eng].append(fn)
            return None
        sname, n = inc
        val = self.cnt[sname] = self.cnt[sname] + n
        sem = self.sems[sname]
        self.ops[eng].append(lambda e, fn=fn, sem=sem, n=n: fn(e).then_inc(sem, n))
        if eng in self.STRICT and sname == self.STRICT[eng]:
            self.wait(eng, sname, val)
        return val

    def wait(self, eng, sname, val):
        if val is None:
            return
        sem = self.sems[sname]
        self.ops[eng].append(lambda e, sem=sem, val=val: e.wait_ge(sem, val))

    def barrier(self):
        names = {"sync": "b_sy", "gpsimd": "b_g", "tensor": "b_t",
                 "vector": "b_v", "scalar": "b_s"}
        vals = {}
        for eng, sn in names.items():
            vals[eng] = self.do(eng, lambda e: e.nop(), inc=(sn, 1))
        for eng in ENGINES:
            for other, sn in names.items():
                if other != eng:
                    self.wait(eng, sn, vals[other])

    def run(self, block):
        for eng in ENGINES:
            ops = self.ops[eng]

            def body(e, ops=ops):
                for f in ops:
                    f(e)
            getattr(block, eng)(body)


def mk_prog(nc, st):
    p = Prog(nc)
    p._st = st
    for s in ["hdma", "gdma", "t", "v", "s", "g",
              "b_sy", "b_g", "b_t", "b_v", "b_s"]:
        p.sem(st, s)
    return p


def emit_ln_T(p, xn, ps, onesP, sq, mrow, vrow, mb, rb, epsc):
    CH = 512
    lastv = None
    sqslot_t = [None, None]
    p.wait("scalar", "v", p.cnt["v"])
    for tc in range(T // CH):
        tv = None
        for dd in range(DT):
            xs = xn[:, dd * T + tc * CH: dd * T + tc * CH + CH]
            ssl = sq[:, (dd % 2) * CH:(dd % 2) * CH + CH]
            p.wait("scalar", "t", sqslot_t[dd % 2])
            vv = p.do("scalar", lambda e, xs=xs, ssl=ssl: e.activation(
                out=ssl, in_=xs, func=AF.Square, bias=0.0, scale=1.0), inc=("s", 1))
            p.wait("tensor", "s", vv)
            p.do("tensor", lambda e, xs=xs, dd=dd: e.matmul(
                ps[0][0:1, :], onesP[:, 0:1], xs, start=(dd == 0), stop=(dd == DT - 1)))
            tv = p.do("tensor", lambda e, dd=dd, ssl=ssl: e.matmul(
                ps[1][0:1, :], onesP[:, 0:1], ssl, start=(dd == 0), stop=(dd == DT - 1)),
                inc=("t", 1))
            sqslot_t[dd % 2] = tv
        p.wait("vector", "t", tv)
        p.do("vector", lambda e: e.tensor_scalar(
            out=mrow[0:1, :], in0=ps[0][0:1, :], scalar1=1.0 / D, scalar2=None, op0=ALU.mult))
        p.do("vector", lambda e: e.tensor_scalar(
            out=vrow[0:1, :], in0=ps[1][0:1, :], scalar1=1.0 / D, scalar2=None, op0=ALU.mult))
        p.do("vector", lambda e: e.tensor_mul(out=sq[0:1, 0:512], in0=mrow[0:1, :],
                                              in1=mrow[0:1, :]))
        sv = p.do("vector", lambda e: e.tensor_sub(
            out=vrow[0:1, :], in0=vrow[0:1, :], in1=sq[0:1, 0:512]), inc=("v", 1))
        p.wait("scalar", "v", sv)
        p.do("scalar", lambda e: e.activation(out=vrow[0:1, :], in_=vrow[0:1, :],
                                              func=AF.Ln, bias=epsc[0:1, :], scale=1.0))
        ssv = p.do("scalar", lambda e: e.activation(out=vrow[0:1, :], in_=vrow[0:1, :],
                                                    func=AF.Exp, bias=0.0, scale=-0.5),
                   inc=("s", 1))
        p.wait("tensor", "s", ssv)
        p.do("tensor", lambda e: e.matmul(ps[2][:], onesP[0:1, :], mrow[0:1, :],
                                          start=True, stop=True))
        tv = p.do("tensor", lambda e: e.matmul(ps[3][:], onesP[0:1, :], vrow[0:1, :],
                                               start=True, stop=True), inc=("t", 1))
        p.wait("vector", "t", tv)
        p.do("vector", lambda e: e.tensor_copy(out=mb[:], in_=ps[2][:]))
        cv = p.do("vector", lambda e: e.tensor_copy(out=rb[:], in_=ps[3][:]), inc=("v", 1))
        for dd in range(DT):
            xs = xn[:, dd * T + tc * CH: dd * T + tc * CH + CH]
            p.do("vector", lambda e, xs=xs: e.tensor_sub(out=xs, in0=xs, in1=mb[:]))
            lastv = p.do("vector", lambda e, xs=xs: e.tensor_mul(out=xs, in0=xs, in1=rb[:]),
                         inc=("v", 1))
        p.wait("tensor", "v", cv)
    return lastv



IO_ATTN = ["xT", "xprefT", "pbias", "wq", "wk", "wv", "wo", "bq", "bk",
           "bvr", "bo", "triu", "yT"]


def emit_attn(nc, p, io, pref_wait=None, use_prefix=True):
    (xT, xprefT, pbias, wq, wk, wv, wo, bq, bk, bvr, bo, triu, yT) = (
        io[k] for k in IO_ATTN)
    with ExitStack() as st:
        en = st.enter_context
        xbf = en(_sb(nc, "xbf", [128, DT * T], BF16))
        OT = en(_sb(nc, "OT", [128, DT * T], BF16))
        bq_sb = en(_sb(nc, "bq_sb", [128, DT], F32))
        bk_sb = en(_sb(nc, "bk_sb", [128, DT], F32))
        bo_sb = en(_sb(nc, "bo_sb", [128, DT], F32))
        pb_sb = en(_sb(nc, "pb_sb", [128, 2], F32))
        triu_sb = en(_sb(nc, "triu_sb", [128, 128], BF16))
        onesP = en(_sb(nc, "onesP", [128, 128], F32))
        epsc = en(_sb(nc, "epsc", [128, 1], F32))
        scrA = en(_sb(nc, "scrA", [128, 3072], F32))
        bcb = scrA[0:64, 0:2048]
        rr = scrA[64:65, 0:2048]
        bvrow = scrA[64:65, 2048:3072]
        mrow = scrA[0:1, 2048:2560]
        vrow = scrA[0:1, 2560:3072]

        ps_s = en(_ps(nc, "ps_s", [128, 2048], F32))
        ps_o = [en(_ps(nc, f"ps_o{i}", [128, 512], F32)) for i in range(4)]

        vc = p.do("gpsimd", lambda e: e.memset(onesP[:], 1.0), inc=("g", 1))
        vc = p.do("gpsimd", lambda e: e.memset(epsc[:], LNEPS), inc=("g", 1))
        vc = p.do("gpsimd", lambda e: e.memset(pb_sb[:, 1:2], 0.0), inc=("g", 1))
        p.dsem("c_d"); p.dsem("xbf_d")
        vsm = None
        for dst, src in [(bq_sb[:], bq), (bk_sb[:], bk), (bo_sb[:], bo),
                         (triu_sb[:], triu), (pb_sb[:, 0:1], pbias), (bvrow, bvr)]:
            vsm = p.do("sync", lambda e, dst=dst, src=src: e.dma_start(out=dst, in_=src),
                       inc=("c_d", 16))
        vxc = None
        for k in range(DT):
            vxc = p.do("gpsimd", lambda e, k=k: e.dma_start(
                out=xbf[:, k * T:(k + 1) * T], in_=xT[k * 128:(k + 1) * 128, :]),
                inc=("xbf_d", 16))
        for eng in ("tensor", "vector", "scalar"):
            p.wait(eng, "xbf_d", vxc)
            p.wait(eng, "c_d", vsm)
            p.wait(eng, "g", vc)

        with ExitStack() as st2:
            en2 = st2.enter_context
            xpbf = en2(_sb(nc, "xpbf", [128, DT * T], BF16))
            KT = en2(_sb(nc, "KT", [128, 4 * CTX], BF16))
            Vaug = en2(_sb(nc, "Vaug", [128, 32 * 520], BF16))
            QT = en2(_sb(nc, "QT", [128, 4 * T], BF16))
            PT = en2(_sb(nc, "PT", [128, 3 * 1024], BF16))
            wring = en2(_sb(nc, "wring", [128, 2 * DT * 128], BF16))
            wvbuf = en2(_sb(nc, "wvbuf", [128, DT * 512], BF16))
            bvb = PT[:, 0:512]

            p.dsem("xpbf_d")
            if use_prefix:
                if pref_wait is not None:
                    p.wait("gpsimd", pref_wait[0], pref_wait[1])
                vpx = None
                for k in range(DT):
                    vpx = p.do("gpsimd", lambda e, k=k: e.dma_start(
                        out=xpbf[:, k * T:(k + 1) * T], in_=xprefT[k * 128:(k + 1) * 128, :]),
                        inc=("xpbf_d", 16))
                p.wait("tensor", "xpbf_d", vpx)

            RING = 2
            ring_free = [None] * RING
            ring_loaded = [None] * RING
            ring_pos = [0]

            for _s in range(RING):
                p.dsem(f"wr{_s}")

            def qk_panel(src, m):
                sl = ring_pos[0] % RING
                ring_pos[0] += 1
                p.wait("gpsimd", "t", ring_free[sl])
                dst = wring[:, sl * DT * 128:(sl + 1) * DT * 128].rearrange(
                    "p (k c) -> p k c", k=DT)
                ring_loaded[sl] = p.do("gpsimd", lambda e, m=m, dst=dst, src=src: e.dma_start(
                    out=dst, in_=src[m].rearrange("k p c -> p k c")), inc=(f"wr{sl}", 16))
                return sl

            bank_free = {i: None for i in range(4)}
            ps_s_free_s = None
            ps_s_free_v = None

            def ps_s_free_s_asv():
                return ps_s_free_v
            wv_used = None

            for hh in range(2):
                p.wait("tensor", "v", bank_free[0])
                tv = p.do("tensor", lambda e, hh=hh: e.matmul(
                    ps_o[0][:], onesP[64:65, :], bvrow[0:1, hh * 512:(hh + 1) * 512],
                    start=True, stop=True), inc=("t", 1))
                p.wait("vector", "t", tv)
                p.wait("vector", "s", p.cnt["s"])
                vbv = p.do("vector", lambda e: e.tensor_copy(out=bvb, in_=ps_o[0][:]),
                           inc=("v", 1))
                bank_free[0] = vbv

                p.dsem("wv_d")
                p.wait("gpsimd", "t", wv_used)
                wvl = None
                for k in range(DT):
                    wvl = p.do("gpsimd", lambda e, k=k, hh=hh: e.dma_start(
                        out=wvbuf[:, k * 512:(k + 1) * 512],
                        in_=wv[k, :, hh * 512:(hh + 1) * 512]), inc=("wv_d", 16))

                pbank = [ps_o[i][:] for i in range(4)] + \
                    [ps_s[:, i * 512:(i + 1) * 512] for i in range(4)]
                pb_free = {i: bank_free.get(i) for i in range(4)}
                pb_free.update({i: ps_s_free_s_asv() for i in range(4, 8)})
                p.wait("tensor", "s", ps_s_free_s)
                rot = [0]
                for which in ("k", "q"):
                    tcs = (list(range(8)) if use_prefix else [4, 5, 6, 7]) \
                        if which == "k" else list(range(4))
                    wsrc = wk if which == "k" else wq
                    bias_sb = bk_sb if which == "k" else bq_sb
                    for mm in range(4):
                        mg = hh * 4 + mm
                        sl = qk_panel(wsrc, mg)
                        p.wait("tensor", f"wr{sl}", ring_loaded[sl])
                        for tc in tcs:
                            bank = rot[0] % 8
                            rot[0] += 1
                            p.wait("tensor", "v", pb_free[bank])
                            if which == "k":
                                src = xpbf if tc < 4 else xbf
                                stc = tc % 4
                            else:
                                src, stc = xbf, tc
                            tval = None
                            for k in range(DT):
                                tval = p.do("tensor",
                                    lambda e, sl=sl, k=k, bank=bank, src=src, stc=stc: e.matmul(
                                        pbank[bank],
                                        wring[:, sl * DT * 128 + k * 128: sl * DT * 128 + (k + 1) * 128],
                                        src[:, k * T + stc * 512: k * T + stc * 512 + 512],
                                        start=(k == 0), stop=(k == DT - 1)), inc=("t", 1))
                            p.wait("vector", "t", tval)
                            if which == "k":
                                dsl = KT[:, mm * CTX + tc * 512: mm * CTX + tc * 512 + 512]
                            else:
                                dsl = QT[:, mm * T + tc * 512: mm * T + tc * 512 + 512]
                            pb_free[bank] = p.do("vector",
                                lambda e, dsl=dsl, bank=bank, bias_sb=bias_sb, mg=mg:
                                    e.tensor_scalar(out=dsl, in0=pbank[bank],
                                                    scalar1=bias_sb[:, mg:mg + 1],
                                                    scalar2=None, op0=ALU.add), inc=("v", 1))
                        ring_free[sl] = p.cnt["t"]

                p.wait("tensor", "wv_d", wvl)
                for tt in (range(32) if use_prefix else range(16, 32)):
                    bank = rot[0] % 8
                    rot[0] += 1
                    src = xpbf if tt < 16 else xbf
                    ttl = tt % 16
                    p.wait("tensor", "v", pb_free[bank])
                    tval = None
                    for k in range(DT):
                        tval = p.do("tensor",
                            lambda e, k=k, bank=bank, src=src, ttl=ttl: e.matmul(
                                pbank[bank],
                                src[:, k * T + ttl * 128: k * T + ttl * 128 + 128],
                                wvbuf[:, k * 512:(k + 1) * 512],
                                start=(k == 0), stop=(k == DT - 1)), inc=("t", 1))
                    p.wait("vector", "t", tval)
                    p.wait("vector", "v", vbv)
                    dst3 = Vaug[:, tt * 520: tt * 520 + 520].rearrange(
                        "p (h d) -> p h d", h=8)[:, :, 0:64]
                    src3 = pbank[bank].rearrange("p (h d) -> p h d", h=8)
                    bv3 = bvb.rearrange("p (h d) -> p h d", h=8)
                    p.do("vector", lambda e, dst3=dst3, src3=src3, bv3=bv3: e.tensor_tensor(
                        out=dst3, in0=src3, in1=bv3, op=ALU.add))
                    ones3 = Vaug[:, tt * 520: tt * 520 + 520].rearrange(
                        "p (h d) -> p h d", h=8)[:, :, 64:65]
                    pb_free[bank] = p.do("vector", lambda e, ones3=ones3: e.memset(ones3, 1.0),
                                         inc=("v", 2))
                for i in range(4):
                    bank_free[i] = pb_free[i]
                ps_s_free_v = max(v for v in [pb_free[j] for j in range(4, 8)]
                                  if v is not None) if any(
                    pb_free[j] is not None for j in range(4, 8)) else ps_s_free_v
                wv_used = p.cnt["t"]

                p.wait("scalar", "v", p.cnt["v"])
                for qc in range(4):
                    for grp in range(2):
                        kts = (list(range(16)) if use_prefix else []) + \
                            [16 + j for j in range(4 * qc + 4)]
                        for i in range(4):
                            p.wait("tensor", "v", bank_free[i])
                        pvv = None
                        half_free_s = {0: None, 1: None}
                        units = [(ki, kt, hf) for ki, kt in enumerate(kts)
                                 for hf in range(2)]
                        nu = len(units)

                        def emit_pv(u, final):
                            nonlocal pvv
                            (ki, kt, hf), slot, sv, fix = u
                            p.wait("tensor", "s", sv)
                            if fix is not None:
                                p.wait("tensor", "g", fix)
                            for j, hi in enumerate((hf * 2, hf * 2 + 1)):
                                h8 = grp * 4 + hi
                                last = (ki == len(kts) - 1)
                                incv = ("t", 1) if (final and j == 1) else None
                                pv_ = p.do("tensor",
                                    lambda e, hi=hi, h8=h8, kt=kt, ki=ki, last=last,
                                           slot=slot, j=j: e.matmul(
                                        ps_o[hi][0:65, :],
                                        Vaug[:, kt * 520 + h8 * 65: kt * 520 + h8 * 65 + 65],
                                        PT[:, slot * 1024 + j * 512: slot * 1024 + j * 512 + 512],
                                        start=(ki == 0), stop=last), inc=incv)
                                if pv_ is not None:
                                    pvv = pv_

                        from collections import deque
                        pending = deque()
                        for ui, (ki, kt, hf) in enumerate(units):
                            slot = ui % 3
                            bias_ap = pb_sb[:, 0:1] if kt < 16 else pb_sb[:, 1:2]
                            p.wait("tensor", "s", half_free_s[hf])
                            p.wait("tensor", "v", ps_s_free_v)
                            stv = None
                            for hi in (hf * 2, hf * 2 + 1):
                                h8 = grp * 4 + hi
                                mt = h8 // 2
                                r0 = (h8 % 2) * 64
                                stv = p.do("tensor",
                                    lambda e, mt=mt, r0=r0, kt=kt, qc=qc, hi=hi: e.matmul(
                                        ps_s[:, hi * 512:(hi + 1) * 512],
                                        KT[r0:r0 + 64, mt * CTX + kt * 128: mt * CTX + kt * 128 + 128],
                                        QT[r0:r0 + 64, mt * T + qc * 512: mt * T + qc * 512 + 512],
                                        start=True, stop=True), inc=("t", 1))
                            if len(pending) >= 2:
                                emit_pv(pending.popleft(), final=False)
                            p.wait("scalar", "t", stv)
                            sv = p.do("scalar",
                                lambda e, bias_ap=bias_ap, hf=hf, slot=slot: e.activation(
                                    out=PT[:, slot * 1024:(slot + 1) * 1024],
                                    in_=ps_s[:, hf * 1024:(hf + 1) * 1024],
                                    func=AF.Exp, bias=bias_ap, scale=0.125), inc=("s", 1))
                            half_free_s[hf] = sv
                            fix = None
                            fixeng = None
                            if kt >= 16:
                                ko = kt - 16
                                nz = max(0, min(4, ko - 4 * qc))
                                hasdiag = (0 <= ko - 4 * qc < 4)
                                if nz or hasdiag:
                                    p.wait("gpsimd", "s", sv)
                                    fixeng = "g"
                                for j in range(2):
                                    base = slot * 1024 + j * 512
                                    if nz:
                                        fix = p.do("gpsimd", lambda e, base=base, nz=nz:
                                            e.memset(PT[:, base: base + nz * 128], 0.0),
                                            inc=("g", 1))
                                    if hasdiag:
                                        dq = ko - 4 * qc
                                        sl_ = PT[:, base + dq * 128: base + dq * 128 + 128]
                                        fix = p.do("gpsimd", lambda e, sl_=sl_: e.tensor_mul(
                                            out=sl_, in0=sl_, in1=triu_sb[:]), inc=("g", 1))
                            pending.append(((ki, kt, hf), slot, sv, fix))
                        while pending:
                            emit_pv(pending.popleft(), final=(len(pending) == 0))
                        ps_s_free_s = half_free_s[1]
                        p.wait("vector", "t", pvv)
                        rv = None
                        for hi in range(4):
                            rv = p.do("vector", lambda e, hi=hi: e.reciprocal(
                                out=rr[0:1, hi * 512:(hi + 1) * 512], in_=ps_o[hi][64:65, :]),
                                inc=("v", 1))
                        p.wait("tensor", "v", rv)
                        p.wait("tensor", "s", ps_s_free_s)
                        btv = None
                        for hi in range(4):
                            btv = p.do("tensor", lambda e, hi=hi: e.matmul(
                                ps_s[0:64, hi * 512:(hi + 1) * 512], onesP[64:65, 0:64],
                                rr[0:1, hi * 512:(hi + 1) * 512], start=True, stop=True),
                                inc=("t", 1))
                        p.wait("vector", "t", btv)
                        cv = p.do("vector", lambda e: e.tensor_copy(
                            out=bcb[:], in_=ps_s[0:64, :]), inc=("v", 1))
                        ps_s_free_v = cv
                        bvlast = None
                        for hi in range(4):
                            h8 = grp * 4 + hi
                            hg = hh * 8 + h8
                            kc = hg // 2
                            r0 = (hg % 2) * 64
                            dst = OT[r0:r0 + 64, kc * T + qc * 512: kc * T + qc * 512 + 512]
                            bvlast = p.do("vector", lambda e, dst=dst, hi=hi: e.tensor_tensor(
                                out=dst, in0=ps_o[hi][0:64, :],
                                in1=bcb[0:64, hi * 512:(hi + 1) * 512], op=ALU.mult),
                                inc=("v", 1))
                        for i in range(4):
                            bank_free[i] = bvlast

            p.barrier()

        with ExitStack() as st3:
            en3 = st3.enter_context
            xn = en3(_sb(nc, "xn", [128, DT * T], F32))
            woring = en3(_sb(nc, "woring", [128, 4 * DT * 128], BF16))
            sq = en3(_sb(nc, "sq", [128, 1024], F32))
            mb = en3(_sb(nc, "mb", [128, 512], F32))
            rb = en3(_sb(nc, "rb", [128, 512], F32))

            p.dsem("xn_d")
            vxn = None
            for k in range(DT):
                vxn = p.do("sync", lambda e, k=k: e.dma_start(
                    out=xn[:, k * T:(k + 1) * T], in_=xT[k * 128:(k + 1) * 128, :]),
                    inc=("xn_d", 16))
            p.wait("vector", "xn_d", vxn)

            RING2 = 4
            r2_free = [None] * RING2
            r2_loaded = [None] * RING2
            r2_pos = [0]

            for _s in range(RING2):
                p.dsem(f"wo{_s}")

            def wo_panel(m):
                sl = r2_pos[0] % RING2
                r2_pos[0] += 1
                p.wait("gpsimd", "t", r2_free[sl])
                dst = woring[:, sl * DT * 128:(sl + 1) * DT * 128].rearrange(
                    "p (k c) -> p k c", k=DT)
                r2_loaded[sl] = p.do("gpsimd", lambda e, m=m, dst=dst: e.dma_start(
                    out=dst, in_=wo[m].rearrange("k p c -> p k c")), inc=(f"wo{sl}", 16))
                return sl

            bank_free2 = {i: None for i in range(4)}
            for m in range(DT):
                sl = wo_panel(m)
                p.wait("tensor", f"wo{sl}", r2_loaded[sl])
                for tc in range(4):
                    bank = tc % 4
                    p.wait("tensor", "v", bank_free2[bank])
                    tval = None
                    for kc in range(DT):
                        tval = p.do("tensor",
                            lambda e, sl=sl, kc=kc, bank=bank, tc=tc: e.matmul(
                                ps_o[bank][:],
                                woring[:, sl * DT * 128 + kc * 128: sl * DT * 128 + (kc + 1) * 128],
                                OT[:, kc * T + tc * 512: kc * T + tc * 512 + 512],
                                start=(kc == 0), stop=(kc == DT - 1)), inc=("t", 1))
                    p.wait("vector", "t", tval)
                    dsl = xn[:, m * T + tc * 512: m * T + tc * 512 + 512]
                    bank_free2[bank] = p.do("vector",
                        lambda e, dsl=dsl, bank=bank, m=m: e.scalar_tensor_tensor(
                            out=dsl, in0=ps_o[bank][:], scalar=bo_sb[:, m:m + 1], in1=dsl,
                            op0=ALU.add, op1=ALU.add), inc=("v", 1))
                r2_free[sl] = p.cnt["t"]

            for i in range(4):
                p.wait("tensor", "v", bank_free2[i])
            lnv = emit_ln_T(p, xn, ps_o, onesP, sq, mrow, vrow, mb, rb, epsc)

            p.dsem("out_d")
            p.wait("sync", "v", lnv)
            vout = None
            for k in range(DT):
                vout = p.do("sync", lambda e, k=k: e.dma_start(
                    out=yT[k * 128:(k + 1) * 128, :], in_=xn[:, k * T:(k + 1) * T]),
                    inc=("out_d", 16))
            p.wait("sync", "out_d", vout)

            p.barrier()


def build_attn():
    nc = bass.Bass("TRN2", target_bir_lowering=False)
    xT = nc.dram_tensor("xT", [D, T], F32, kind="ExternalInput").ap()
    xprefT = nc.dram_tensor("xprefT", [D, T], F32, kind="ExternalInput").ap()
    pbias = nc.dram_tensor("pbias", [128, 1], F32, kind="ExternalInput").ap()
    wq = nc.dram_tensor("wq", [DT, DT, 128, 128], F32, kind="ExternalInput").ap()
    wk = nc.dram_tensor("wk", [DT, DT, 128, 128], F32, kind="ExternalInput").ap()
    wv = nc.dram_tensor("wv", [DT, 128, D], F32, kind="ExternalInput").ap()
    wo = nc.dram_tensor("wo", [DT, DT, 128, 128], F32, kind="ExternalInput").ap()
    bq = nc.dram_tensor("bq", [128, DT], F32, kind="ExternalInput").ap()
    bk = nc.dram_tensor("bk", [128, DT], F32, kind="ExternalInput").ap()
    bvr = nc.dram_tensor("bvr", [1, D], F32, kind="ExternalInput").ap()
    bo = nc.dram_tensor("bo", [128, DT], F32, kind="ExternalInput").ap()
    triu = nc.dram_tensor("triu", [128, 128], BF16, kind="ExternalInput").ap()
    yT = nc.dram_tensor("yT", [D, T], F32, kind="ExternalOutput").ap()

    io = {k: v for k, v in zip(IO_ATTN, [xT, xprefT, pbias, wq, wk, wv, wo,
                                        bq, bk, bvr, bo, triu, yT])}
    with ExitStack() as st:
        p = mk_prog(nc, st)
        emit_attn(nc, p, io)
        block = st.enter_context(nc.Block())
        p.run(block)
    return nc



IO_FFN0 = ["xT", "wr", "br", "w1", "b1", "w2", "b2", "id8", "idF", "sel8", "yT"]


def emit_ffn0(nc, p, io):
    (xT, wr, br, w1, b1, w2, b2, id8, idF, sel8, yT) = (io[k] for k in IO_FFN0)
    with ExitStack() as st:
        en = st.enter_context
        xn = en(_sb(nc, "xn", [128, DT * T], F32))
        xbf = en(_sb(nc, "xbf", [128, DT * T], BF16))
        hbuf = en(_sb(nc, "hbuf", [128, 16 * T], BF16))
        gTm = en(_sb(nc, "gTm", [8, T], F32))
        gbc = en(_sb(nc, "gbc", [128, T], F32))
        prb = en(_sb(nc, "prb", [128, NT * 8], F32))
        keep = en(_sb(nc, "keep", [128, NT * 8], F32))
        scr8 = en(_sb(nc, "scr8", [128, 8], F32))
        colA = en(_sb(nc, "colA", [128, 4], F32))
        onesP = en(_sb(nc, "onesP", [128, 128], F32))
        id8_sb = en(_sb(nc, "id8_sb", [8, 8], F32))
        idF_sb = en(_sb(nc, "idF_sb", [128, 128], F32))
        sel8_sb = en(_sb(nc, "sel8_sb", [8, 8 * 128], F32))
        br_sb = en(_sb(nc, "br_sb", [8, 1], F32))
        b1_sb = en(_sb(nc, "b1_sb", [128, NF], F32))
        b2_sb = en(_sb(nc, "b2_sb", [128, DT], F32))
        wrbuf = en(_sb(nc, "wrbuf", [128, DT * 8], F32))
        w1ring = en(_sb(nc, "w1ring", [128, 4 * DT * 128], BF16))
        w2ring = en(_sb(nc, "w2ring", [128, 2 * 16 * 128], BF16))
        tmp = en(_sb(nc, "tmp", [128, 512], F32))
        sq = en(_sb(nc, "sq", [128, 1024], F32))
        mrow = tmp[0:1, 0:512]
        vrow = gbc[0:1, 0:512]
        mb = gbc[:, 512:1024]
        rb = gbc[:, 1024:1536]

        epsc = en(_sb(nc, "epsc", [128, 1], F32))
        ps_s = en(_ps(nc, "ps_s", [128, 2048], F32))
        ps_A = en(_ps(nc, "ps_A", [128, 2048], F32))
        ps_o = [ps_A[:, i * 512:(i + 1) * 512] for i in range(4)]

        vc = p.do("gpsimd", lambda e: e.memset(onesP[:], 1.0), inc=("g", 1))
        vc = p.do("gpsimd", lambda e: e.memset(epsc[:], LNEPS), inc=("g", 1))
        p.dsem("c_d"); p.dsem("xn_d"); p.dsem("xbf_d")
        vsm = None
        for dst, src in [(br_sb[:], br), (wrbuf[:], wr), (id8_sb[:], id8), (idF_sb[:], idF),
                         (sel8_sb[:], sel8)]:
            vsm = p.do("sync", lambda e, dst=dst, src=src: e.dma_start(out=dst, in_=src),
                       inc=("c_d", 16))
        vx = None
        for k in range(DT):
            vx = p.do("sync", lambda e, k=k: e.dma_start(
                out=xn[:, k * T:(k + 1) * T], in_=xT[k * 128:(k + 1) * 128, :]), inc=("xn_d", 16))
            p.do("gpsimd", lambda e, k=k: e.dma_start(
                out=xbf[:, k * T:(k + 1) * T], in_=xT[k * 128:(k + 1) * 128, :]), inc=("xbf_d", 16))
        vxb = p.cnt["xbf_d"]
        for eng in ("tensor", "vector", "scalar"):
            p.wait(eng, "c_d", vsm)
            p.wait(eng, "xn_d", vx)
            p.wait(eng, "xbf_d", vxb)
            p.wait(eng, "g", vc)

        tv = None
        for tc in range(4):
            for k in range(DT):
                tv = p.do("tensor", lambda e, k=k, tc=tc: e.matmul(
                    ps_o[tc][0:8, :], wrbuf[:, k * 8:(k + 1) * 8],
                    xn[:, k * T + tc * 512: k * T + tc * 512 + 512],
                    start=(k == 0), stop=(k == DT - 1)), inc=("t", 1))
        p.wait("vector", "t", tv)
        lv = None
        for tc in range(4):
            lv = p.do("vector", lambda e, tc=tc: e.tensor_scalar(
                out=gTm[0:8, tc * 512:(tc + 1) * 512], in0=ps_o[tc][0:8, :],
                scalar1=br_sb[:], scalar2=None, op0=ALU.add), inc=("v", 1))
        p.wait("tensor", "v", lv)
        tp_free = [None] * 4
        for tt in range(NT):
            p.wait("tensor", "v", tp_free[tt % 4])
            tv = p.do("tensor", lambda e, tt=tt: e.matmul(
                ps_o[tt % 4][0:128, 0:8], gTm[0:8, tt * 128:(tt + 1) * 128], id8_sb[:],
                start=True, stop=True), inc=("t", 1))
            p.wait("vector", "t", tv)
            lv = tp_free[tt % 4] = p.do("vector", lambda e, tt=tt: e.tensor_copy(
                out=prb[:, tt * 8:(tt + 1) * 8], in_=ps_o[tt % 4][0:128, 0:8]), inc=("v", 1))
        gv = None
        for tt in range(NT):
            sl_ = prb[:, tt * 8:(tt + 1) * 8]
            kl = keep[:, tt * 8:(tt + 1) * 8]
            _rv = p.do("vector", lambda e, sl_=sl_: e.reduce_max(
                out=colA[:, 0:1], in_=sl_, axis=AX.X, negate=True), inc=("v", 1))
            p.wait("vector", "v", _rv)
            vv = p.do("vector", lambda e, sl_=sl_: e.tensor_scalar(
                out=sl_, in0=sl_, scalar1=colA[:, 0:1], scalar2=None, op0=ALU.add), inc=("v", 1))
            p.wait("scalar", "v", vv)
            sv = p.do("scalar", lambda e, sl_=sl_: e.activation(
                out=sl_, in_=sl_, func=AF.Exp, bias=0.0, scale=1.0), inc=("s", 1))
            p.wait("vector", "s", sv)
            _rv = p.do("vector", lambda e, sl_=sl_: e.reduce_sum(out=colA[:, 1:2], in_=sl_, axis=AX.X), inc=("v", 1))
            p.wait("vector", "v", _rv)
            _rv = p.do("vector", lambda e: e.reciprocal(out=colA[:, 2:3], in_=colA[:, 1:2]),
                       inc=("v", 1))
            p.wait("vector", "v", _rv)
            p.do("vector", lambda e, sl_=sl_: e.tensor_scalar(
                out=sl_, in0=sl_, scalar1=colA[:, 2:3], scalar2=None, op0=ALU.mult))
            for e_ in range(8):
                p.do("vector", lambda e, sl_=sl_, e_=e_: e.tensor_scalar(
                    out=scr8[:], in0=sl_, scalar1=sl_[:, e_:e_ + 1], scalar2=None, op0=ALU.is_gt))
                p.do("vector", lambda e, sl_=sl_: e.tensor_mul(out=scr8[:], in0=scr8[:], in1=sl_))
                _rv = p.do("vector", lambda e, kl=kl, e_=e_: e.reduce_sum(
                    out=kl[:, e_:e_ + 1], in_=scr8[:], axis=AX.X), inc=("v", 1))
                p.wait("vector", "v", _rv)
            p.do("vector", lambda e, kl=kl, sl_=sl_: e.tensor_add(out=kl, in0=kl, in1=sl_))
            p.do("vector", lambda e, kl=kl: e.tensor_scalar(
                out=kl, in0=kl, scalar1=0.8, scalar2=None, op0=ALU.is_lt))
            _rv = p.do("vector", lambda e, sl_=sl_: e.reduce_max(out=colA[:, 3:4], in_=sl_, axis=AX.X), inc=("v", 1))
            p.wait("vector", "v", _rv)
            p.do("vector", lambda e, sl_=sl_: e.tensor_scalar(
                out=scr8[:], in0=sl_, scalar1=colA[:, 3:4], scalar2=None, op0=ALU.is_ge))
            p.do("vector", lambda e, kl=kl: e.tensor_max(out=kl, in0=kl, in1=scr8[:]))
            gv = p.do("vector", lambda e, kl=kl, sl_=sl_: e.tensor_mul(out=kl, in0=kl, in1=sl_),
                      inc=("v", 1))
        p.wait("tensor", "v", gv)
        tp_free = [None] * 4
        for tt in range(NT):
            p.wait("tensor", "v", tp_free[tt % 4])
            tv = p.do("tensor", lambda e, tt=tt: e.matmul(
                ps_o[tt % 4][0:8, 0:128], keep[:, tt * 8:(tt + 1) * 8], idF_sb[:],
                start=True, stop=True), inc=("t", 1))
            p.wait("vector", "t", tv)
            gv = tp_free[tt % 4] = p.do("vector", lambda e, tt=tt: e.tensor_copy(
                out=gTm[0:8, tt * 128:(tt + 1) * 128], in_=ps_o[tt % 4][0:8, 0:128]),
                inc=("v", 1))

        RING = 4
        r_free = [None] * RING
        r_loaded = [None] * RING
        r_pos = [0]

        for _s in range(RING):
            p.dsem(f"w1r{_s}")

        def w1_panel(e_, f):
            sl = r_pos[0] % RING
            r_pos[0] += 1
            p.wait("gpsimd", "t", r_free[sl])
            dst = w1ring[:, sl * DT * 128:(sl + 1) * DT * 128].rearrange(
                "p (k c) -> p k c", k=DT)
            r_loaded[sl] = p.do("gpsimd", lambda e, e_=e_, f=f, dst=dst: e.dma_start(
                out=dst, in_=w1[e_, f].rearrange("k p c -> p k c")), inc=(f"w1r{sl}", 16))
            return sl

        r2_free = [None, None]
        r2_loaded = [None, None]
        r2_pos = [0]

        for _s in range(2):
            p.dsem(f"w2r{_s}")

        def w2_panel(e_, m, fh):
            sl = r2_pos[0] % 2
            r2_pos[0] += 1
            p.wait("gpsimd", "t", r2_free[sl])
            dst = w2ring[:, sl * 16 * 128:(sl + 1) * 16 * 128].rearrange(
                "p (k c) -> p k c", k=16)
            r2_loaded[sl] = p.do("gpsimd", lambda e, e_=e_, m=m, fh=fh, dst=dst: e.dma_start(
                out=dst, in_=w2[e_, m, fh * 16:(fh + 1) * 16].rearrange("k p c -> p k c")),
                inc=(f"w2r{sl}", 16))
            return sl

        vb12 = None
        sets = {"S": ps_s, "A": ps_A}
        set_s_free = {"S": None, "A": None}
        bank_v = {("A", i): gv for i in range(4)}
        for i in range(4):
            bank_v[("S", i)] = None
        relu_last = None
        yv_last = gv

        def wait_set(eng, setn):
            p.wait(eng, "s", set_s_free[setn])
            for i in range(4):
                p.wait(eng, "v", bank_v[(setn, i)])

        for e_ in range(8):
            p.wait("sync", "s", relu_last)
            p.wait("sync", "v", yv_last)
            p.dsem("b12_d")
            vb12 = p.do("sync", lambda e, e_=e_: e.dma_start(out=b1_sb[:], in_=b1[e_]),
                        inc=("b12_d", 16))
            vb12 = p.do("sync", lambda e, e_=e_: e.dma_start(out=b2_sb[:], in_=b2[e_]),
                        inc=("b12_d", 16))
            gtv = None
            wait_set("tensor", "A")
            for tc in range(4):
                gtv = p.do("tensor", lambda e, e_=e_, tc=tc: e.matmul(
                    ps_o[tc][:], sel8_sb[:, e_ * 128:(e_ + 1) * 128],
                    gTm[0:8, tc * 512:(tc + 1) * 512],
                    start=True, stop=True), inc=("t", 1))
            p.wait("vector", "t", gtv)
            gcv = None
            for tc in range(4):
                gcv = p.do("vector", lambda e, tc=tc: e.tensor_copy(
                    out=gbc[:, tc * 512:(tc + 1) * 512], in_=ps_o[tc][:]), inc=("v", 1))
            for i in range(4):
                bank_v[("A", i)] = gcv
            p.wait("scalar", "b12_d", vb12)
            p.wait("vector", "b12_d", vb12)

            for fh in range(2):
                for fl in range(16):
                    f = fh * 16 + fl
                    setn = "S" if fl % 2 == 0 else "A"
                    ps = sets[setn]
                    sl = w1_panel(e_, f)
                    p.wait("tensor", f"w1r{sl}", r_loaded[sl])
                    wait_set("tensor", setn)
                    tval = None
                    for k in range(DT):
                        for qn in range(4):
                            tval = p.do("tensor",
                                lambda e, sl=sl, k=k, qn=qn, ps=ps: e.matmul(
                                    ps[:, qn * 512:(qn + 1) * 512],
                                    w1ring[:, sl * DT * 128 + k * 128: sl * DT * 128 + (k + 1) * 128],
                                    xbf[:, k * T + qn * 512: k * T + qn * 512 + 512],
                                    start=(k == 0), stop=(k == DT - 1)),
                                inc=(None if (k < DT - 1 or qn < 3) else ("t", 1)))
                    p.wait("scalar", "t", tval)
                    relu_last = set_s_free[setn] = p.do("scalar",
                        lambda e, fl=fl, f=f, ps=ps: e.activation(
                            out=hbuf[:, fl * T:(fl + 1) * T], in_=ps[:], func=AF.Relu,
                            bias=b1_sb[:, f:f + 1], scale=1.0), inc=("s", 1))
                    for i in range(4):
                        bank_v[(setn, i)] = None
                    r_free[sl] = p.cnt["t"]
                p.wait("tensor", "s", set_s_free["S"])
                p.wait("tensor", "s", set_s_free["A"])
                pp = 0
                for m in range(DT):
                    sl2 = w2_panel(e_, m, fh)
                    p.wait("tensor", f"w2r{sl2}", r2_loaded[sl2])
                    for tc in range(4):
                        setn = "S" if pp % 2 == 0 else "A"
                        pp += 1
                        ps = sets[setn]
                        p.wait("tensor", "v", bank_v[(setn, tc)])
                        p.wait("tensor", "s", set_s_free[setn])
                        tval = None
                        for fc in range(16):
                            tval = p.do("tensor",
                                lambda e, sl2=sl2, fc=fc, tc=tc, ps=ps: e.matmul(
                                    ps[:, tc * 512:(tc + 1) * 512],
                                    w2ring[:, sl2 * 16 * 128 + fc * 128: sl2 * 16 * 128 + (fc + 1) * 128],
                                    hbuf[:, fc * T + tc * 512: fc * T + tc * 512 + 512],
                                    start=(fc == 0), stop=(fc == 15)),
                                inc=(None if fc < 15 else ("t", 1)))
                        p.wait("vector", "t", tval)
                        dsl = xn[:, m * T + tc * 512: m * T + tc * 512 + 512]
                        if fh == 0:
                            p.do("vector", lambda e, tc=tc, m=m, ps=ps: e.scalar_tensor_tensor(
                                out=tmp[:], in0=ps[:, tc * 512:(tc + 1) * 512],
                                scalar=b2_sb[:, m:m + 1],
                                in1=gbc[:, tc * 512:(tc + 1) * 512], op0=ALU.add, op1=ALU.mult))
                        else:
                            p.do("vector", lambda e, tc=tc, ps=ps: e.tensor_mul(
                                out=tmp[:], in0=ps[:, tc * 512:(tc + 1) * 512],
                                in1=gbc[:, tc * 512:(tc + 1) * 512]))
                        yv_last = bank_v[(setn, tc)] = p.do("vector",
                            lambda e, dsl=dsl: e.tensor_add(
                                out=dsl, in0=dsl, in1=tmp[:]), inc=("v", 1))
                    r2_free[sl2] = p.cnt["t"]
                set_s_free = {"S": None, "A": None}

        p.dsem("out_d")
        for setn in ("S", "A"):
            for i in range(4):
                p.wait("tensor", "v", bank_v[(setn, i)])
        lnv = emit_ln_T(p, xn, ps_o, onesP, sq, mrow, vrow, mb, rb, epsc)
        p.wait("sync", "v", lnv)
        vout = None
        for k in range(DT):
            vout = p.do("sync", lambda e, k=k: e.dma_start(
                out=yT[k * 128:(k + 1) * 128, :], in_=xn[:, k * T:(k + 1) * T]), inc=("out_d", 16))
        p.wait("sync", "out_d", vout)

        p.barrier()


def build_ffn0():
    nc = bass.Bass("TRN2", target_bir_lowering=False)
    xT = nc.dram_tensor("xT", [D, T], F32, kind="ExternalInput").ap()
    wr = nc.dram_tensor("wr", [128, DT * 8], F32, kind="ExternalInput").ap()
    br = nc.dram_tensor("br", [8, 1], F32, kind="ExternalInput").ap()
    w1 = nc.dram_tensor("w1", [8, NF, DT, 128, 128], F32, kind="ExternalInput").ap()
    b1 = nc.dram_tensor("b1", [8, 128, NF], F32, kind="ExternalInput").ap()
    w2 = nc.dram_tensor("w2", [8, DT, NF, 128, 128], F32, kind="ExternalInput").ap()
    b2 = nc.dram_tensor("b2", [8, 128, DT], F32, kind="ExternalInput").ap()
    id8 = nc.dram_tensor("id8", [8, 8], F32, kind="ExternalInput").ap()
    idF = nc.dram_tensor("idF", [128, 128], F32, kind="ExternalInput").ap()
    sel8 = nc.dram_tensor("sel8", [8, 8 * 128], F32, kind="ExternalInput").ap()
    yT = nc.dram_tensor("yT", [D, T], F32, kind="ExternalOutput").ap()

    io = {k: v for k, v in zip(IO_FFN0, [xT, wr, br, w1, b1, w2, b2, id8,
                                         idF, sel8, yT])}
    with ExitStack() as st:
        p = mk_prog(nc, st)
        emit_ffn0(nc, p, io)
        block = st.enter_context(nc.Block())
        p.run(block)
    return nc



IO_FFN1 = ["xT", "xnat", "wr", "w0", "w1", "wo", "id8", "idF", "trii",
           "eio", "tokio", "ynat", "stok", "ydram"]


def emit_ffn1(nc, p, io):
    (xT, xnat, wr, w0, w1, wo, id8, idF, trii, eio, tokio, ynat, stok,
     ydram) = (io[k] for k in IO_FFN1)
    with ExitStack() as st:
        en = st.enter_context
        prb = en(_sb(nc, "prb", [128, NT * 8], F32))
        m1b = en(_sb(nc, "m1b", [128, NT * 8], F32))
        m2b = en(_sb(nc, "m2b", [128, NT * 8], F32))
        cmb = en(_sb(nc, "cmb", [128, NT * 8], F32))
        gcol = en(_sb(nc, "gcol", [128, 4 * NT], F32))
        jcol = en(_sb(nc, "jcol", [128, 2 * NT], F32))
        jint = en(_sb(nc, "jint", [128, 2 * NT], I32))
        pcol = en(_sb(nc, "pcol", [128, 6], F32))
        scr8 = en(_sb(nc, "scr8", [128, 8], F32))
        scr8b = en(_sb(nc, "scr8b", [128, 8], F32))
        cntb = en(_sb(nc, "cntb", [128, 8], F32))
        cntr = en(_sb(nc, "cntr", [1, 8], F32))
        onesP = en(_sb(nc, "onesP", [128, 128], F32))
        id8_sb = en(_sb(nc, "id8_sb", [8, 8], F32))
        idF_sb = en(_sb(nc, "idF_sb", [128, 128], F32))
        trii_sb = en(_sb(nc, "trii_sb", [128, 128], F32))
        eio_sb = en(_sb(nc, "eio_sb", [128, 8], F32))
        toki_sb = en(_sb(nc, "toki_sb", [128, NT], I32))
        stok_sb = en(_sb(nc, "stok_sb", [128, 32], I32))
        wrbuf = en(_sb(nc, "wrbuf", [128, DT * 8], F32))
        xf32 = en(_sb(nc, "xf32", [128, 2 * 512], F32))
        lgT = en(_sb(nc, "lgT", [8, T], F32))

        zbuf = en(_sb(nc, "zbuf", [128, 2 * D], F32))
        ZTe = en(_sb(nc, "ZTe", [128, DT * 512], BF16))
        hid = en(_sb(nc, "hid", [128, HT1 * 512], BF16))
        ybuf = en(_sb(nc, "ybuf", [128, 4 * D], BF16))
        w01ring = en(_sb(nc, "w01ring", [128, 6 * DT * 128], BF16))
        woring = en(_sb(nc, "woring", [128, 4 * D], BF16))
        g12 = en(_sb(nc, "g12", [128, 2 * D], BF16))
        xot = en(_sb(nc, "xot", [128, D], F32))
        xrt = en(_sb(nc, "xrt", [128, D], F32))
        sqn = en(_sb(nc, "sqn", [128, D], F32))
        gt1 = en(_sb(nc, "gt1", [128, D], F32))

        epsc = en(_sb(nc, "epsc", [128, 1], F32))
        ps_o = [en(_ps(nc, f"ps_o{i}", [128, 512], F32)) for i in range(8)]

        vc = p.do("gpsimd", lambda e: e.memset(onesP[:], 1.0), inc=("g", 1))
        vc = p.do("gpsimd", lambda e: e.memset(epsc[:], LNEPS), inc=("g", 1))
        p.dsem("c_d")
        vsm = None
        for dst, src in [(id8_sb[:], id8), (idF_sb[:], idF), (trii_sb[:], trii),
                         (eio_sb[:], eio), (toki_sb[:], tokio), (wrbuf[:], wr)]:
            vsm = p.do("sync", lambda e, dst=dst, src=src: e.dma_start(out=dst, in_=src),
                       inc=("c_d", 16))
        for eng in ("tensor", "vector", "scalar"):
            p.wait(eng, "c_d", vsm)
            p.wait(eng, "g", vc)

        xck = [None, None]
        p.dsem("xf0"); p.dsem("xf1")
        tv = None
        for tc in range(4):
            for k in range(DT):
                sl = (tc * DT + k) % 2
                p.wait("sync", "t", xck[sl])
                vl = p.do("sync", lambda e, sl=sl, k=k, tc=tc: e.dma_start(
                    out=xf32[:, sl * 512:(sl + 1) * 512],
                    in_=xT[k * 128:(k + 1) * 128, tc * 512:(tc + 1) * 512]), inc=(f"xf{sl}", 16))
                p.wait("tensor", f"xf{sl}", vl)
                tv = p.do("tensor", lambda e, sl=sl, k=k, tc=tc: e.matmul(
                    ps_o[tc][0:8, :], wrbuf[:, k * 8:(k + 1) * 8],
                    xf32[:, sl * 512:(sl + 1) * 512],
                    start=(k == 0), stop=(k == DT - 1)), inc=("t", 1))
                xck[sl] = p.cnt["t"]
        p.wait("vector", "t", tv)
        lv = None
        for tc in range(4):
            lv = p.do("vector", lambda e, tc=tc: e.tensor_copy(
                out=lgT[0:8, tc * 512:(tc + 1) * 512], in_=ps_o[tc][0:8, :]), inc=("v", 1))
        p.wait("tensor", "v", lv)
        tp_free = [None] * 4
        for tt in range(NT):
            p.wait("tensor", "v", tp_free[tt % 4])
            tv = p.do("tensor", lambda e, tt=tt: e.matmul(
                ps_o[tt % 4][0:128, 0:8], lgT[0:8, tt * 128:(tt + 1) * 128], id8_sb[:],
                start=True, stop=True), inc=("t", 1))
            p.wait("vector", "t", tv)
            lv = tp_free[tt % 4] = p.do("vector", lambda e, tt=tt: e.tensor_copy(
                out=prb[:, tt * 8:(tt + 1) * 8], in_=ps_o[tt % 4][0:128, 0:8]), inc=("v", 1))

        mv = None
        for tt in range(NT):
            sl_ = prb[:, tt * 8:(tt + 1) * 8]
            _rv = p.do("vector", lambda e, sl_=sl_: e.reduce_max(
                out=pcol[:, 0:1], in_=sl_, axis=AX.X, negate=True), inc=("v", 1))
            p.wait("vector", "v", _rv)
            vv = p.do("vector", lambda e, sl_=sl_: e.tensor_scalar(
                out=sl_, in0=sl_, scalar1=pcol[:, 0:1], scalar2=None, op0=ALU.add), inc=("v", 1))
            p.wait("scalar", "v", vv)
            sv = p.do("scalar", lambda e, sl_=sl_: e.activation(
                out=sl_, in_=sl_, func=AF.Exp, bias=0.0, scale=1.0), inc=("s", 1))
            p.wait("vector", "s", sv)
            _rv = p.do("vector", lambda e, sl_=sl_: e.reduce_sum(out=pcol[:, 1:2], in_=sl_, axis=AX.X), inc=("v", 1))
            p.wait("vector", "v", _rv)
            _rv = p.do("vector", lambda e: e.reciprocal(out=pcol[:, 2:3], in_=pcol[:, 1:2]),
                       inc=("v", 1))
            p.wait("vector", "v", _rv)
            p.do("vector", lambda e, sl_=sl_: e.tensor_scalar(
                out=sl_, in0=sl_, scalar1=pcol[:, 2:3], scalar2=None, op0=ALU.mult))
            m1t = m1b[:, tt * 8:(tt + 1) * 8]
            m2t = m2b[:, tt * 8:(tt + 1) * 8]
            _rv = p.do("vector", lambda e, sl_=sl_, tt=tt: e.reduce_max(
                out=gcol[:, 0 * NT + tt: 0 * NT + tt + 1], in_=sl_, axis=AX.X), inc=("v", 1))
            p.wait("vector", "v", _rv)
            p.do("vector", lambda e, sl_=sl_, m1t=m1t, tt=tt: e.tensor_scalar(
                out=m1t, in0=sl_, scalar1=gcol[:, 0 * NT + tt: 0 * NT + tt + 1],
                scalar2=None, op0=ALU.is_ge))
            p.do("vector", lambda e, sl_=sl_, m1t=m1t: e.tensor_mul(
                out=scr8[:], in0=sl_, in1=m1t))
            p.do("vector", lambda e, sl_=sl_: e.tensor_sub(out=scr8[:], in0=sl_, in1=scr8[:]))
            _rv = p.do("vector", lambda e, tt=tt: e.reduce_max(
                out=gcol[:, 1 * NT + tt: 1 * NT + tt + 1], in_=scr8[:], axis=AX.X), inc=("v", 1))
            p.wait("vector", "v", _rv)
            mv = p.do("vector", lambda e, m2t=m2t, tt=tt: e.tensor_scalar(
                out=m2t, in0=scr8[:], scalar1=gcol[:, 1 * NT + tt: 1 * NT + tt + 1],
                scalar2=None, op0=ALU.is_ge), inc=("v", 1))

        p.wait("tensor", "v", mv)

        def cum_pass(buf):
            nonlocal_mv = [None]
            for mtile in range(NT):
                tv = None
                for ktile in range(mtile + 1):
                    lhs = trii_sb[:] if ktile == mtile else onesP[:]
                    tv = p.do("tensor",
                        lambda e, lhs=lhs, ktile=ktile, buf=buf, mtile=mtile: e.matmul(
                            ps_o[4 + mtile % 2][0:128, 0:8], lhs,
                            buf[:, ktile * 8:(ktile + 1) * 8],
                            start=(ktile == 0), stop=(ktile == mtile)), inc=("t", 1))
                p.wait("vector", "t", tv)
                nonlocal_mv[0] = p.do("vector", lambda e, mtile=mtile: e.tensor_copy(
                    out=cmb[:, mtile * 8:(mtile + 1) * 8],
                    in_=ps_o[4 + mtile % 2][0:128, 0:8]), inc=("v", 1))
                p.wait("tensor", "v", nonlocal_mv[0])
            return nonlocal_mv[0]

        mv = cum_pass(m1b)
        for tt in range(NT):
            m1t = m1b[:, tt * 8:(tt + 1) * 8]
            ct = cmb[:, tt * 8:(tt + 1) * 8]
            p.do("vector", lambda e, ct=ct, m1t=m1t: e.tensor_sub(
                out=ct, in0=ct, in1=m1t))
            p.do("vector", lambda e, ct=ct: e.tensor_scalar(
                out=scr8[:], in0=ct, scalar1=float(CAP), scalar2=None, op0=ALU.is_lt))
            p.do("vector", lambda e, m1t=m1t: e.tensor_mul(
                out=m1t, in0=m1t, in1=scr8[:]))
            p.do("vector", lambda e, ct=ct, m1t=m1t: e.tensor_mul(
                out=scr8[:], in0=ct, in1=m1t))
            _rv = p.do("vector", lambda e: e.reduce_sum(
                out=pcol[:, 3:4], in_=scr8[:], axis=AX.X), inc=("v", 1))
            p.wait("vector", "v", _rv)
            _rv = p.do("vector", lambda e, m1t=m1t: e.reduce_sum(
                out=pcol[:, 4:5], in_=m1t, axis=AX.X), inc=("v", 1))
            p.wait("vector", "v", _rv)
            p.do("vector", lambda e, tt=tt: e.scalar_tensor_tensor(
                out=gcol[:, 2 * NT + tt: 2 * NT + tt + 1], in0=pcol[:, 4:5],
                scalar=1.0, in1=gcol[:, 0 * NT + tt: 0 * NT + tt + 1],
                op0=ALU.mult, op1=ALU.mult))
            p.do("vector", lambda e, m1t=m1t: e.tensor_mul(
                out=scr8[:], in0=m1t, in1=eio_sb[:]))
            _rv = p.do("vector", lambda e: e.reduce_sum(
                out=pcol[:, 5:6], in_=scr8[:], axis=AX.X), inc=("v", 1))
            p.wait("vector", "v", _rv)
            p.do("vector", lambda e: e.tensor_scalar(
                out=pcol[:, 5:6], in0=pcol[:, 5:6], scalar1=512.0, scalar2=None,
                op0=ALU.mult))
            p.do("vector", lambda e: e.tensor_add(
                out=pcol[:, 5:6], in0=pcol[:, 5:6], in1=pcol[:, 3:4]))
            p.do("vector", lambda e: e.tensor_mul(
                out=pcol[:, 5:6], in0=pcol[:, 5:6], in1=pcol[:, 4:5]))
            p.do("vector", lambda e: e.tensor_scalar(
                out=pcol[:, 3:4], in0=pcol[:, 4:5], scalar1=-float(NSLOT),
                scalar2=float(NSLOT), op0=ALU.mult, op1=ALU.add))
            p.do("vector", lambda e, tt=tt: e.tensor_add(
                out=jcol[:, tt:tt + 1], in0=pcol[:, 5:6], in1=pcol[:, 3:4]))
            mv = p.do("vector", lambda e, tt=tt: e.tensor_copy(
                out=jint[:, tt:tt + 1], in_=jcol[:, tt:tt + 1]), inc=("v", 1))
        p.wait("tensor", "v", mv)
        tv = None
        for tt2 in range(NT):
            tv = p.do("tensor", lambda e, tt2=tt2: e.matmul(
                ps_o[6][0:1, 0:8], onesP[:, 0:1], m1b[:, tt2 * 8:(tt2 + 1) * 8],
                start=(tt2 == 0), stop=(tt2 == NT - 1)), inc=("t", 1))
        p.wait("vector", "t", tv)
        mv = p.do("vector", lambda e: e.tensor_copy(
            out=cntr[:], in_=ps_o[6][0:1, 0:8]), inc=("v", 1))
        p.wait("tensor", "v", mv)
        tv = p.do("tensor", lambda e: e.matmul(
            ps_o[6][0:128, 0:8], onesP[0:1, :], cntr[:], start=True, stop=True),
            inc=("t", 1))
        p.wait("vector", "t", tv)
        mv = p.do("vector", lambda e: e.tensor_copy(
            out=cntb[:], in_=ps_o[6][0:128, 0:8]), inc=("v", 1))
        mv = cum_pass(m2b)
        for tt in range(NT):
            m2t = m2b[:, tt * 8:(tt + 1) * 8]
            ct = cmb[:, tt * 8:(tt + 1) * 8]
            p.do("vector", lambda e, ct=ct, m2t=m2t: e.tensor_sub(out=ct, in0=ct, in1=m2t))
            p.do("vector", lambda e, ct=ct: e.tensor_add(out=ct, in0=ct, in1=cntb[:]))
            p.do("vector", lambda e, ct=ct: e.tensor_scalar(
                out=scr8[:], in0=ct, scalar1=float(CAP), scalar2=None, op0=ALU.is_lt))
            p.do("vector", lambda e, m2t=m2t: e.tensor_mul(out=m2t, in0=m2t, in1=scr8[:]))
            p.do("vector", lambda e, ct=ct, m2t=m2t: e.tensor_mul(
                out=scr8[:], in0=ct, in1=m2t))
            _rv = p.do("vector", lambda e: e.reduce_sum(out=pcol[:, 3:4], in_=scr8[:], axis=AX.X), inc=("v", 1))
            p.wait("vector", "v", _rv)
            _rv = p.do("vector", lambda e, m2t=m2t: e.reduce_sum(out=pcol[:, 4:5], in_=m2t, axis=AX.X), inc=("v", 1))
            p.wait("vector", "v", _rv)
            p.do("vector", lambda e, tt=tt: e.scalar_tensor_tensor(
                out=gcol[:, 3 * NT + tt: 3 * NT + tt + 1], in0=pcol[:, 4:5], scalar=1.0,
                in1=gcol[:, 1 * NT + tt: 1 * NT + tt + 1], op0=ALU.mult, op1=ALU.mult))
            p.do("vector", lambda e, m2t=m2t: e.tensor_mul(out=scr8[:], in0=m2t, in1=eio_sb[:]))
            _rv = p.do("vector", lambda e: e.reduce_sum(out=pcol[:, 5:6], in_=scr8[:], axis=AX.X), inc=("v", 1))
            p.wait("vector", "v", _rv)
            p.do("vector", lambda e: e.tensor_scalar(
                out=pcol[:, 5:6], in0=pcol[:, 5:6], scalar1=512.0, scalar2=None, op0=ALU.mult))
            p.do("vector", lambda e: e.tensor_add(
                out=pcol[:, 5:6], in0=pcol[:, 5:6], in1=pcol[:, 3:4]))
            p.do("vector", lambda e: e.tensor_mul(
                out=pcol[:, 5:6], in0=pcol[:, 5:6], in1=pcol[:, 4:5]))
            p.do("vector", lambda e: e.tensor_scalar(
                out=pcol[:, 3:4], in0=pcol[:, 4:5], scalar1=-float(NSLOT),
                scalar2=float(NSLOT), op0=ALU.mult, op1=ALU.add))
            p.do("vector", lambda e, tt=tt: e.tensor_add(
                out=jcol[:, NT + tt:NT + tt + 1], in0=pcol[:, 5:6], in1=pcol[:, 3:4]))
            mv = p.do("vector", lambda e, tt=tt: e.tensor_copy(
                out=jint[:, NT + tt:NT + tt + 1], in_=jcol[:, NT + tt:NT + tt + 1]),
                inc=("v", 1))

        p.dsem("sti_d"); p.dsem("sts_d"); p.dsem("stl_d")
        iv = p.do("vector", lambda e: e.memset(stok_sb[:], T), inc=("v", 1))
        p.wait("gpsimd", "v", iv)
        gv = None
        for jt in range(32):
            gv = p.do("gpsimd", lambda e, jt=jt: e.dma_start(
                out=stok[jt * 128:(jt + 1) * 128, :], in_=stok_sb[:, jt:jt + 1]),
                inc=("sti_d", 16))
        gv = p.do("gpsimd", lambda e: e.dma_start(
            out=stok[NSLOT:NSLOT + 1, :], in_=stok_sb[0:1, 0:1]), inc=("sti_d", 16))
        p.wait("gpsimd", "v", mv)
        p.wait("gpsimd", "sti_d", gv)
        p.wait("gpsimd", "c_d", vsm)
        for half in range(2):
            for tt in range(NT):
                gv = p.do("gpsimd", lambda e, half=half, tt=tt: e.indirect_dma_start(
                    out=stok,
                    out_offset=IndirectOffsetOnAxis(jint[:, half * NT + tt: half * NT + tt + 1], 0),
                    in_=toki_sb[:, tt:tt + 1],
                    in_offset=None,
                    bounds_check=None,
                    oob_is_err=False), inc=("sts_d", 16))
        p.wait("gpsimd", "sts_d", gv)
        sldv = None
        for jt in range(32):
            sldv = p.do("gpsimd", lambda e, jt=jt: e.dma_start(
                out=stok_sb[:, jt:jt + 1], in_=stok[jt * 128:(jt + 1) * 128, :]),
                inc=("stl_d", 16))

        RING = 6
        r_free = [None] * RING
        r_loaded = [None] * RING
        r_pos = [0]

        for _s in range(6):
            p.dsem(f"w01r{_s}")

        def w01_panel(src, e_, ht):
            sl = r_pos[0] % RING
            r_pos[0] += 1
            p.wait("gpsimd", "t", r_free[sl])
            dst = w01ring[:, sl * DT * 128:(sl + 1) * DT * 128].rearrange(
                "p (k c) -> p k c", k=DT)
            r_loaded[sl] = p.do("gpsimd", lambda e, e_=e_, ht=ht, dst=dst, src=src: e.dma_start(
                out=dst, in_=src[e_, ht].rearrange("k p c -> p k c")), inc=(f"w01r{sl}", 16))
            return sl

        r2_free = [None] * 4
        r2_loaded = [None] * 4
        r2_pos = [0]

        for _s in range(4):
            p.dsem(f"wor{_s}")

        def wo_panel(e_, ht):
            sl = r2_pos[0] % 4
            r2_pos[0] += 1
            p.wait("gpsimd", "t", r2_free[sl])
            r2_loaded[sl] = p.do("gpsimd", lambda e, e_=e_, ht=ht, sl=sl: e.dma_start(
                out=woring[:, sl * D:(sl + 1) * D], in_=wo[e_, ht]), inc=(f"wor{sl}", 16))
            return sl

        p.wait("gpsimd", "stl_d", sldv)
        sv_h = None
        sv_hv = None
        zfree_t = None
        ztv = None
        hid_free = None
        ybw = None
        ydma = None
        zslot_free = [None, None]
        tslot_free = [None, None]
        for e_ in range(8):
            for jt in range(4):
                zs = jt % 2
                p.wait("gpsimd", "t", zslot_free[zs])
                zm = p.do("gpsimd", lambda e, zs=zs: e.memset(
                    zbuf[:, zs * D:(zs + 1) * D], 0.0), inc=("g", 1))
                p.wait("gpsimd", "g", zm)
                gz = p.do("gpsimd", lambda e, e_=e_, jt=jt, zs=zs: e.indirect_dma_start(
                    out=zbuf[:, zs * D:(zs + 1) * D],
                    out_offset=None,
                    in_=xnat,
                    in_offset=IndirectOffsetOnAxis(stok_sb[:, e_ * 4 + jt: e_ * 4 + jt + 1], 0),
                    bounds_check=None,
                    oob_is_err=False), inc=(p.dsem(f"zg{zs}"), 16))
                p.wait("tensor", f"zg{zs}", gz)
                p.wait("tensor", "v", ybw)
                for dd in range(DT):
                    tps = 6 + dd % 2
                    p.wait("tensor", "v", tslot_free[dd % 2])
                    t2 = p.do("tensor", lambda e, dd=dd, zs=zs, tps=tps: e.transpose(
                        ps_o[tps][:, 0:128],
                        zbuf[:, zs * D + dd * 128: zs * D + (dd + 1) * 128], idF_sb[:]),
                        inc=("t", 1))
                    p.wait("vector", "t", t2)
                    ztv = tslot_free[dd % 2] = p.do("vector",
                        lambda e, dd=dd, jt=jt, tps=tps: e.tensor_copy(
                            out=ZTe[:, dd * 512 + jt * 128: dd * 512 + jt * 128 + 128],
                            in_=ps_o[tps][:, 0:128]), inc=("v", 1))
                zslot_free[zs] = p.cnt["t"]
            p.wait("tensor", "v", ztv)
            pair_free = {0: None, 1: None}
            for ht in range(HT1):
                pr = ht % 2
                psA, psB = ps_o[pr * 2], ps_o[pr * 2 + 1]
                gsl = slice(pr * 512, pr * 512 + 512)
                sl0 = w01_panel(w0, e_, ht)
                p.wait("tensor", f"w01r{sl0}", r_loaded[sl0])
                p.wait("tensor", "v", pair_free[pr])
                tv0 = None
                for k in range(DT):
                    tv0 = p.do("tensor", lambda e, sl0=sl0, k=k, psA=psA: e.matmul(
                        psA[:],
                        w01ring[:, sl0 * DT * 128 + k * 128: sl0 * DT * 128 + (k + 1) * 128],
                        ZTe[:, k * 512:(k + 1) * 512],
                        start=(k == 0), stop=(k == DT - 1)), inc=("t", 1))
                r_free[sl0] = p.cnt["t"]
                sl1 = w01_panel(w1, e_, ht)
                p.wait("tensor", f"w01r{sl1}", r_loaded[sl1])
                tv1 = None
                for k in range(DT):
                    tv1 = p.do("tensor", lambda e, sl1=sl1, k=k, psB=psB: e.matmul(
                        psB[:],
                        w01ring[:, sl1 * DT * 128 + k * 128: sl1 * DT * 128 + (k + 1) * 128],
                        ZTe[:, k * 512:(k + 1) * 512],
                        start=(k == 0), stop=(k == DT - 1)), inc=("t", 1))
                r_free[sl1] = p.cnt["t"]
                p.wait("scalar", "t", tv0)
                s1 = p.do("scalar", lambda e, psA=psA, gsl=gsl: e.activation(
                    out=sqn[:, gsl], in_=psA[:], func=AF.Square, bias=0.0, scale=1.0),
                    inc=("s", 1))
                p.wait("vector", "s", s1)
                p.do("vector", lambda e, psA=psA, gsl=gsl: e.scalar_tensor_tensor(
                    out=gt1[:, gsl], in0=sqn[:, gsl], scalar=GELU_C, in1=psA[:],
                    op0=ALU.mult, op1=ALU.mult))
                v1 = p.do("vector", lambda e, psA=psA, gsl=gsl: e.scalar_tensor_tensor(
                    out=gt1[:, gsl], in0=gt1[:, gsl], scalar=1.0, in1=psA[:],
                    op0=ALU.mult, op1=ALU.add), inc=("v", 1))
                p.wait("scalar", "v", v1)
                s2 = p.do("scalar", lambda e, gsl=gsl: e.activation(
                    out=gt1[:, gsl], in_=gt1[:, gsl], func=AF.Sigmoid, bias=0.0,
                    scale=GELU_S), inc=("s", 1))
                p.wait("vector", "s", s2)
                p.wait("vector", "t", tv1)
                p.do("vector", lambda e, psA=psA, gsl=gsl: e.tensor_mul(
                    out=gt1[:, gsl], in0=gt1[:, gsl], in1=psA[:]))
                sv_hv = p.do("vector", lambda e, ht=ht, psB=psB, gsl=gsl: e.tensor_mul(
                    out=hid[:, ht * 512:(ht + 1) * 512], in0=gt1[:, gsl], in1=psB[:]),
                    inc=("v", 1))
                pair_free[pr] = sv_hv
                sv_h = sv_hv
            p.wait("tensor", "v", sv_hv)
            p.wait("tensor", "v", ybw)
            tvy = None
            for ht in range(HT1):
                slw = wo_panel(e_, ht)
                p.wait("tensor", f"wor{slw}", r2_loaded[slw])
                for jt in range(4):
                    for mc in range(2):
                        tvy = p.do("tensor",
                            lambda e, slw=slw, ht=ht, jt=jt, mc=mc: e.matmul(
                                ps_o[jt * 2 + mc][:],
                                hid[:, ht * 512 + jt * 128: ht * 512 + jt * 128 + 128],
                                woring[:, slw * D + mc * 512: slw * D + mc * 512 + 512],
                                start=(ht == 0), stop=(ht == HT1 - 1)), inc=("t", 1))
                r2_free[slw] = p.cnt["t"]
            p.wait("vector", "t", tvy)
            p.wait("vector", "yd_d", ydma)
            ybw = None
            for jt in range(4):
                for mc in range(2):
                    ybw = p.do("vector", lambda e, jt=jt, mc=mc: e.tensor_copy(
                        out=ybuf[:, jt * D + mc * 512: jt * D + mc * 512 + 512],
                        in_=ps_o[jt * 2 + mc][:]), inc=("v", 1))
            p.wait("gpsimd", "v", ybw)
            for jt in range(4):
                ydma = p.do("gpsimd", lambda e, e_=e_, jt=jt: e.dma_start(
                    out=ydram[e_ * 512 + jt * 128: e_ * 512 + (jt + 1) * 128, :],
                    in_=ybuf[:, jt * D:(jt + 1) * D]), inc=(p.dsem("yd_d"), 16))

        p.wait("gpsimd", "yd_d", ydma)
        zz = p.do("vector", lambda e: e.memset(ybuf[0:1, 0:D], 0.0), inc=("v", 1))
        p.wait("gpsimd", "v", zz)
        ydma = p.do("gpsimd", lambda e: e.dma_start(
            out=ydram[NSLOT:NSLOT + 1, :], in_=ybuf[0:1, 0:D]), inc=("yd_d", 16))
        p.wait("gpsimd", "yd_d", ydma)
        outv = None
        g1t_free = None
        xrt_free = None
        for tt in range(NT):
            p.wait("gpsimd", "v", g1t_free)
            p.wait("sync", "v", xrt_free)
            zm = p.do("gpsimd", lambda e: e.memset(g12[:], 0.0), inc=("g", 1))
            p.wait("gpsimd", "g", zm)
            gg = None
            for half in range(2):
                gg = p.do("gpsimd", lambda e, half=half, tt=tt: e.indirect_dma_start(
                    out=g12[:, half * D:(half + 1) * D],
                    out_offset=None,
                    in_=ydram,
                    in_offset=IndirectOffsetOnAxis(jint[:, half * NT + tt: half * NT + tt + 1], 0),
                    bounds_check=None,
                    oob_is_err=False), inc=(p.dsem("g12_d"), 16))
            vr = p.do("sync", lambda e, tt=tt: e.dma_start(
                out=xrt[:], in_=xnat[tt * 128:(tt + 1) * 128, :]), inc=(p.dsem("xr_d"), 16))
            p.wait("vector", "g12_d", gg)
            p.wait("vector", "xr_d", vr)
            p.do("vector", lambda e, tt=tt: e.tensor_scalar(
                out=xot[:], in0=g12[:, D:2 * D], scalar1=gcol[:, 3 * NT + tt: 3 * NT + tt + 1],
                scalar2=None, op0=ALU.mult))
            p.do("vector", lambda e, tt=tt: e.scalar_tensor_tensor(
                out=xot[:], in0=g12[:, 0:D], scalar=gcol[:, 2 * NT + tt: 2 * NT + tt + 1],
                in1=xot[:], op0=ALU.mult, op1=ALU.add))
            g1t_free = p.do("vector", lambda e: e.tensor_add(
                out=xot[:], in0=xot[:], in1=xrt[:]), inc=("v", 1))
            xrt_free = g1t_free
            _rv = p.do("vector", lambda e: e.reduce_sum(out=pcol[:, 0:1], in_=xot[:], axis=AX.X), inc=("v", 1))
            p.wait("vector", "v", _rv)
            vv = p.do("vector", lambda e: e.tensor_scalar(
                out=pcol[:, 0:1], in0=pcol[:, 0:1], scalar1=1.0 / D, scalar2=None,
                op0=ALU.mult), inc=("v", 1))
            p.do("vector", lambda e: e.tensor_scalar(
                out=xot[:], in0=xot[:], scalar1=pcol[:, 0:1], scalar2=None, op0=ALU.subtract))
            sv = p.do("vector", lambda e: e.tensor_mul(out=sqn[:], in0=xot[:], in1=xot[:]),
                      inc=("v", 1))
            _rv = p.do("vector", lambda e: e.reduce_sum(out=pcol[:, 1:2], in_=sqn[:], axis=AX.X), inc=("v", 1))
            p.wait("vector", "v", _rv)
            sv = p.do("vector", lambda e: e.tensor_scalar(
                out=pcol[:, 1:2], in0=pcol[:, 1:2], scalar1=1.0 / D, scalar2=None,
                op0=ALU.mult), inc=("v", 1))
            p.wait("scalar", "v", sv)
            p.do("scalar", lambda e: e.activation(
                out=pcol[:, 1:2], in_=pcol[:, 1:2], func=AF.Ln, bias=epsc[:], scale=1.0))
            ssv = p.do("scalar", lambda e: e.activation(
                out=pcol[:, 1:2], in_=pcol[:, 1:2], func=AF.Exp, bias=0.0, scale=-0.5),
                inc=("s", 1))
            p.wait("vector", "s", ssv)
            outv = p.do("vector", lambda e: e.tensor_scalar(
                out=xot[:], in0=xot[:], scalar1=pcol[:, 1:2], scalar2=None, op0=ALU.mult),
                inc=("v", 1))
            p.wait("sync", "v", outv)
            vo = p.do("sync", lambda e, tt=tt: e.dma_start(
                out=ynat[tt * 128:(tt + 1) * 128, :], in_=xot[:]), inc=(p.dsem("out_d"), 16))
            p.wait("vector", "out_d", vo)
        p.wait("sync", "out_d", p.cnt["out_d"])

        p.barrier()


def build_ffn1():
    nc = bass.Bass("TRN2", target_bir_lowering=False)
    xT = nc.dram_tensor("xT", [D, T], F32, kind="ExternalInput").ap()
    xnat = nc.dram_tensor("xnat", [T + 1, D], F32, kind="ExternalInput").ap()
    wr = nc.dram_tensor("wr", [128, DT * 8], F32, kind="ExternalInput").ap()
    w0 = nc.dram_tensor("w0", [8, HT1, DT, 128, 128], F32, kind="ExternalInput").ap()
    w1 = nc.dram_tensor("w1", [8, HT1, DT, 128, 128], F32, kind="ExternalInput").ap()
    wo = nc.dram_tensor("wo", [8, HT1, 128, D], F32, kind="ExternalInput").ap()
    id8 = nc.dram_tensor("id8", [8, 8], F32, kind="ExternalInput").ap()
    idF = nc.dram_tensor("idF", [128, 128], F32, kind="ExternalInput").ap()
    trii = nc.dram_tensor("trii", [128, 128], F32, kind="ExternalInput").ap()
    eio = nc.dram_tensor("eio", [128, 8], F32, kind="ExternalInput").ap()
    tokio = nc.dram_tensor("tokio", [128, NT], I32, kind="ExternalInput").ap()
    ynat = nc.dram_tensor("ynat", [T, D], F32, kind="ExternalOutput").ap()

    stok = nc.dram_tensor("stok", [NSLOT + 1, 1], I32, kind="Internal").ap()
    ydram = nc.dram_tensor("ydram", [NSLOT + 1, D], BF16, kind="Internal").ap()

    io = {k: v for k, v in zip(IO_FFN1, [xT, xnat, wr, w0, w1, wo, id8, idF,
                                         trii, eio, tokio, ynat, stok, ydram])}
    with ExitStack() as st:
        p = mk_prog(nc, st)
        emit_ffn1(nc, p, io)
        block = st.enter_context(nc.Block())
        p.run(block)
    return nc



def _block_lhsT(W):
    din, dout = W.shape
    B = W.reshape(din // 128, 128, dout // 128, 128)
    return np.ascontiguousarray(B.transpose(2, 0, 1, 3))


def prep_attn(l, d):
    if l == 0:
        Wq = np.ascontiguousarray(d["L0_attn_Wq"].transpose(1, 0, 2).reshape(D, D))
        Wk = np.ascontiguousarray(d["L0_attn_Wk"].transpose(1, 0, 2).reshape(D, D))
        Wv = np.ascontiguousarray(d["L0_attn_Wv"].transpose(1, 0, 2).reshape(D, D))
        bq_ = d["L0_attn_bq"].reshape(D)
        bk_ = d["L0_attn_bk"].reshape(D)
        bv_ = d["L0_attn_bv"].reshape(D)
        Wo = d["L0_attn_Wo"]
        bo_ = d["L0_attn_bo"]
    else:
        Wqkv = d["L1_attn_W_qkv"]
        bqkv = d["L1_attn_b_qkv"]
        Wq, Wk, Wv = Wqkv[:, 0:D], Wqkv[:, D:2 * D], Wqkv[:, 2 * D:3 * D]
        bq_, bk_, bv_ = bqkv[0:D], bqkv[D:2 * D], bqkv[2 * D:3 * D]
        Wo = d["L1_attn_W_o"]
        bo_ = d["L1_attn_b_o"]
    return {
        "wq": _block_lhsT(np.ascontiguousarray(Wq)),
        "wk": _block_lhsT(np.ascontiguousarray(Wk)),
        "wv": np.ascontiguousarray(np.ascontiguousarray(Wv).reshape(DT, 128, D)),
        "wo": _block_lhsT(Wo),
        "bq": np.ascontiguousarray(bq_.reshape(DT, 128).T),
        "bk": np.ascontiguousarray(bk_.reshape(DT, 128).T),
        "bvr": np.ascontiguousarray(bv_.reshape(1, D)),
        "bo": np.ascontiguousarray(bo_.reshape(DT, 128).T),
        "triu": np.triu(np.ones((128, 128))).astype(BF16NP),
    }


def prep_ffn0(d):
    W1 = d["L0_ffn_W1"]
    W2 = d["L0_ffn_W2"]
    w1b = np.ascontiguousarray(
        W1.reshape(8, DT, 128, NF, 128).transpose(0, 3, 1, 2, 4))
    w2b = np.ascontiguousarray(
        W2.reshape(8, NF, 128, DT, 128).transpose(0, 3, 1, 2, 4))
    return {
        "wr": np.ascontiguousarray(d["L0_ffn_Wr"].reshape(DT, 128, 8).transpose(1, 0, 2).reshape(128, DT * 8)),
        "br": np.ascontiguousarray(d["L0_ffn_br"].reshape(8, 1)),
        "w1": w1b,
        "b1": np.ascontiguousarray(d["L0_ffn_b1"].reshape(8, NF, 128).transpose(0, 2, 1)),
        "w2": w2b,
        "b2": np.ascontiguousarray(d["L0_ffn_b2"].reshape(8, DT, 128).transpose(0, 2, 1)),
        "id8": np.eye(8, dtype=np.float32),
        "idF": np.eye(128, dtype=np.float32),
        "sel8": np.ascontiguousarray(
            np.eye(8, dtype=np.float32).repeat(128, axis=1).reshape(8, 8 * 128)),
    }


def prep_ffn1(d):
    w0 = d["L1_ffn_w0_kernel"]
    w1 = d["L1_ffn_w1_kernel"]
    wo = d["L1_ffn_wo_kernel"]
    pad = H1P - 5464
    w0p = np.concatenate([w0, np.zeros((8, D, pad), np.float32)], axis=2)
    w1p = np.concatenate([w1, np.zeros((8, D, pad), np.float32)], axis=2)
    wop = np.concatenate([wo, np.zeros((8, pad, D), np.float32)], axis=1)
    w0b = np.ascontiguousarray(w0p.reshape(8, DT, 128, HT1, 128).transpose(0, 3, 1, 2, 4))
    w1b = np.ascontiguousarray(w1p.reshape(8, DT, 128, HT1, 128).transpose(0, 3, 1, 2, 4))
    wob = np.ascontiguousarray(wop.reshape(8, HT1, 128, D))
    toki = (np.arange(T, dtype=np.int32).reshape(NT, 128).T).copy()
    return {
        "wr": np.ascontiguousarray(
            d["L1_ffn_router_w"].reshape(DT, 128, 8).transpose(1, 0, 2).reshape(128, DT * 8)),
        "w0": w0b, "w1": w1b, "wo": wob,
        "id8": np.eye(8, dtype=np.float32),
        "idF": np.eye(128, dtype=np.float32),
        "trii": np.triu(np.ones((128, 128), dtype=np.float32)),
        "eio": np.broadcast_to(np.arange(8, dtype=np.float32), (128, 8)).copy(),
        "tokio": toki,
    }




def build_fused():
    nc = bass.Bass("TRN2", target_bir_lowering=False)
    xin = nc.dram_tensor("xin", [D, T], F32, kind="ExternalInput").ap()
    xpref0 = nc.dram_tensor("xpref0", [D, T], F32, kind="ExternalInput").ap()
    pbias = nc.dram_tensor("pbias", [128, 1], F32, kind="ExternalInput").ap()
    pbneg = nc.dram_tensor("pbneg", [128, 1], F32, kind="ExternalInput").ap()
    attn_io = {}
    for l in range(2):
        for k, shp, dt in [("wq", [DT, DT, 128, 128], F32), ("wk", [DT, DT, 128, 128], F32),
                           ("wv", [DT, 128, D], F32), ("wo", [DT, DT, 128, 128], F32),
                           ("bq", [128, DT], F32), ("bk", [128, DT], F32),
                           ("bvr", [1, D], F32), ("bo", [128, DT], F32)]:
            attn_io[(l, k)] = nc.dram_tensor(f"a{l}_{k}", shp, dt, kind="ExternalInput").ap()
    triu = nc.dram_tensor("triu", [128, 128], BF16, kind="ExternalInput").ap()
    f0 = {}
    for k, shp in [("wr", [128, DT * 8]), ("br", [8, 1]), ("w1", [8, NF, DT, 128, 128]),
                   ("b1", [8, 128, NF]), ("w2", [8, DT, NF, 128, 128]), ("b2", [8, 128, DT]),
                   ("sel8", [8, 8 * 128])]:
        f0[k] = nc.dram_tensor(f"f0_{k}", shp, F32, kind="ExternalInput").ap()
    f1 = {}
    for k, shp in [("wr", [128, DT * 8]), ("w0", [8, HT1, DT, 128, 128]),
                   ("w1", [8, HT1, DT, 128, 128]), ("wo", [8, HT1, 128, D])]:
        f1[k] = nc.dram_tensor(f"f1_{k}", shp, F32, kind="ExternalInput").ap()
    id8 = nc.dram_tensor("id8", [8, 8], F32, kind="ExternalInput").ap()
    idF = nc.dram_tensor("idF", [128, 128], F32, kind="ExternalInput").ap()
    trii = nc.dram_tensor("trii", [128, 128], F32, kind="ExternalInput").ap()
    eio = nc.dram_tensor("eio", [128, 8], F32, kind="ExternalInput").ap()
    tokio = nc.dram_tensor("tokio", [128, NT], I32, kind="ExternalInput").ap()
    ynat = nc.dram_tensor("ynat", [T, D], F32, kind="ExternalOutput").ap()

    x1p = nc.dram_tensor("x1p", [D, T], F32, kind="Internal").ap()
    x2p = nc.dram_tensor("x2p", [D, T], F32, kind="Internal").ap()
    x1s = nc.dram_tensor("x1s", [D, T], F32, kind="Internal").ap()
    x2s = nc.dram_tensor("x2s", [D, T], F32, kind="Internal").ap()
    x3s = nc.dram_tensor("x3s", [D, T], F32, kind="Internal").ap()
    x3nat = nc.dram_tensor("x3nat", [T + 1, D], F32, kind="Internal").ap()
    stok = nc.dram_tensor("stok", [NSLOT + 1, 1], I32, kind="Internal").ap()
    ydram = nc.dram_tensor("ydram", [NSLOT + 1, D], BF16, kind="Internal").ap()

    def aio(l, xT, xprefT, pb, yT):
        d = {k: attn_io[(l, k)] for k in ["wq", "wk", "wv", "wo", "bq", "bk", "bvr", "bo"]}
        d.update({"xT": xT, "xprefT": xprefT, "pbias": pb, "triu": triu, "yT": yT})
        return d

    with ExitStack() as st:
        p = mk_prog(nc, st)
        emit_attn(nc, p, aio(0, xpref0, xpref0, pbneg, x1p), use_prefix=False)
        io0 = dict(f0); io0.update({"xT": x1p, "id8": id8, "idF": idF, "yT": x2p})
        emit_ffn0(nc, p, io0)
        emit_attn(nc, p, aio(0, xin, xpref0, pbias, x1s))
        io0 = dict(f0); io0.update({"xT": x1s, "id8": id8, "idF": idF, "yT": x2s})
        emit_ffn0(nc, p, io0)
        emit_attn(nc, p, aio(1, x2s, x2p, pbias, x3s))
        emit_x3nat(nc, p, x3s, x3nat, idF)
        io1 = dict(f1); io1.update({"xT": x3s, "xnat": x3nat, "id8": id8, "idF": idF,
                                    "trii": trii, "eio": eio, "tokio": tokio,
                                    "ynat": ynat, "stok": stok, "ydram": ydram})
        emit_ffn1(nc, p, io1)
        block = st.enter_context(nc.Block())
        p.run(block)
    return nc


def emit_x3nat(nc, p, x3s, x3nat, idF):
    with ExitStack() as st:
        en = st.enter_context
        xb = en(_sb(nc, "x3b", [128, DT * T], F32))
        natb = en(_sb(nc, "natb", [128, 2 * D], F32))
        idF_sb = en(_sb(nc, "idF3_sb", [128, 128], F32))
        pst = [en(_ps(nc, f"pst{i}", [128, 128], F32)) for i in range(4)]
        p.dsem("x3l_d"); p.dsem("x3n0"); p.dsem("x3n1"); p.dsem("idf3_d")
        vi = p.do("sync", lambda e: e.dma_start(out=idF_sb[:], in_=idF), inc=("idf3_d", 16))
        vl = None
        for k in range(DT):
            vl = p.do("sync", lambda e, k=k: e.dma_start(
                out=xb[:, k * T:(k + 1) * T], in_=x3s[k * 128:(k + 1) * 128, :]),
                inc=("x3l_d", 16))
        p.wait("tensor", "x3l_d", vl)
        p.wait("tensor", "idf3_d", vi)
        vn = None
        cpv = None
        bank_cp = [None] * 4
        nslot_free = [None, None]
        for tt in range(NT):
            ns = tt % 2
            p.wait("vector", f"x3n{ns}", nslot_free[ns])
            for dd in range(DT):
                bk_ = dd % 4
                p.wait("tensor", "v", bank_cp[bk_])
                tv = p.do("tensor", lambda e, tt=tt, dd=dd, bk_=bk_: e.transpose(
                    pst[bk_][:], xb[:, dd * T + tt * 128: dd * T + tt * 128 + 128],
                    idF_sb[:]), inc=("t", 1))
                p.wait("vector", "t", tv)
                cpv = bank_cp[bk_] = p.do("vector",
                    lambda e, dd=dd, bk_=bk_, ns=ns: e.tensor_copy(
                        out=natb[:, ns * D + dd * 128: ns * D + (dd + 1) * 128],
                        in_=pst[bk_][:]), inc=("v", 1))
            p.wait("sync", "v", cpv)
            nslot_free[ns] = p.do("sync", lambda e, tt=tt, ns=ns: e.dma_start(
                out=x3nat[tt * 128:(tt + 1) * 128, :],
                in_=natb[:, ns * D:(ns + 1) * D]), inc=(f"x3n{ns}", 16))
        p.wait("vector", "x3n0", nslot_free[0])
        zv = p.do("vector", lambda e: e.memset(natb[0:1, 0:D], 0.0), inc=("v", 1))
        p.wait("sync", "v", zv)
        vn = p.do("sync", lambda e: e.dma_start(
            out=x3nat[T:T + 1, :], in_=natb[0:1, 0:D]), inc=("x3n0", 16))
        p.wait("sync", "x3n0", vn)
        p.wait("sync", "x3n1", nslot_free[1])
        p.barrier()

_NC_CACHE = {}


def _get_nc(name, builder):
    if name not in _NC_CACHE:
        _NC_CACHE[name] = builder()
    return _NC_CACHE[name]


def _chunks(x):
    xs, xp = [], []
    for c in range(NCORES):
        b, h = c // 2, c % 2
        xs.append(np.ascontiguousarray(x[b, h * T:(h + 1) * T, :].T))
        xp.append(np.ascontiguousarray(x[b, 0:T, :].T))
    return xs, xp


def _run_attn(x, aw):
    nc = _get_nc("attn", build_attn)
    xs, xp = _chunks(x)
    in_maps = []
    for c in range(NCORES):
        pb = np.zeros((128, 1), np.float32) if c % 2 == 1 else np.full((128, 1), -30000.0, np.float32)
        m = {"xT": xs[c], "xprefT": xp[c], "pbias": pb}
        m.update(aw)
        in_maps.append(m)
    res = run_bass_kernel_spmd(nc, in_maps, core_ids=list(range(NCORES)))
    out = np.empty((4, 4096, D), np.float32)
    for c in range(NCORES):
        b, h = c // 2, c % 2
        out[b, h * T:(h + 1) * T, :] = np.asarray(res.results[c]["yT"]).reshape(D, T).T
    return out


def _run_ffn0(x, fw):
    nc = _get_nc("ffn0", build_ffn0)
    xs, _ = _chunks(x)
    in_maps = []
    for c in range(NCORES):
        m = {"xT": xs[c]}
        m.update(fw)
        in_maps.append(m)
    res = run_bass_kernel_spmd(nc, in_maps, core_ids=list(range(NCORES)))
    out = np.empty((4, 4096, D), np.float32)
    for c in range(NCORES):
        b, h = c // 2, c % 2
        out[b, h * T:(h + 1) * T, :] = np.asarray(res.results[c]["yT"]).reshape(D, T).T
    return out


def _run_ffn1(x, fw):
    nc = _get_nc("ffn1", build_ffn1)
    xs, _ = _chunks(x)
    in_maps = []
    for c in range(NCORES):
        b, h = c // 2, c % 2
        xn_pad = np.concatenate(
            [x[b, h * T:(h + 1) * T, :], np.zeros((1, D), np.float32)], axis=0)
        m = {"xT": xs[c], "xnat": np.ascontiguousarray(xn_pad)}
        m.update(fw)
        in_maps.append(m)
    res = run_bass_kernel_spmd(nc, in_maps, core_ids=list(range(NCORES)))
    out = np.empty((4, 4096, D), np.float32)
    for c in range(NCORES):
        b, h = c // 2, c % 2
        out[b, h * T:(h + 1) * T, :] = np.asarray(res.results[c]["ynat"]).reshape(T, D)
    return out



def build_fused_l0():
    nc = bass.Bass("TRN2", target_bir_lowering=False)
    xin = nc.dram_tensor("xin", [D, T], F32, kind="ExternalInput").ap()
    xpref0 = nc.dram_tensor("xpref0", [D, T], F32, kind="ExternalInput").ap()
    pbias = nc.dram_tensor("pbias", [128, 1], F32, kind="ExternalInput").ap()
    a = {}
    for k, shp, dt in [("wq", [DT, DT, 128, 128], F32), ("wk", [DT, DT, 128, 128], F32),
                       ("wv", [DT, 128, D], F32), ("wo", [DT, DT, 128, 128], F32),
                       ("bq", [128, DT], F32), ("bk", [128, DT], F32),
                       ("bvr", [1, D], F32), ("bo", [128, DT], F32)]:
        a[k] = nc.dram_tensor(f"a0_{k}", shp, dt, kind="ExternalInput").ap()
    triu = nc.dram_tensor("triu", [128, 128], BF16, kind="ExternalInput").ap()
    f0 = {}
    for k, shp in [("wr", [128, DT * 8]), ("br", [8, 1]), ("w1", [8, NF, DT, 128, 128]),
                   ("b1", [8, 128, NF]), ("w2", [8, DT, NF, 128, 128]), ("b2", [8, 128, DT]),
                   ("sel8", [8, 8 * 128])]:
        f0[k] = nc.dram_tensor(f"f0_{k}", shp, F32, kind="ExternalInput").ap()
    id8 = nc.dram_tensor("id8", [8, 8], F32, kind="ExternalInput").ap()
    idF = nc.dram_tensor("idF", [128, 128], F32, kind="ExternalInput").ap()
    yT = nc.dram_tensor("yT", [D, T], F32, kind="ExternalOutput").ap()
    x1s = nc.dram_tensor("x1s", [D, T], F32, kind="Internal").ap()
    with ExitStack() as st:
        p = mk_prog(nc, st)
        a.update({"xT": xin, "xprefT": xpref0, "pbias": pbias, "triu": triu, "yT": x1s})
        emit_attn(nc, p, a)
        f0.update({"xT": x1s, "id8": id8, "idF": idF, "yT": yT})
        emit_ffn0(nc, p, f0)
        block = st.enter_context(nc.Block())
        p.run(block)
    return nc


def build_fused_l1():
    nc = bass.Bass("TRN2", target_bir_lowering=False)
    xin = nc.dram_tensor("xin", [D, T], F32, kind="ExternalInput").ap()
    xpref = nc.dram_tensor("xpref", [D, T], F32, kind="ExternalInput").ap()
    pbias = nc.dram_tensor("pbias", [128, 1], F32, kind="ExternalInput").ap()
    a = {}
    for k, shp, dt in [("wq", [DT, DT, 128, 128], F32), ("wk", [DT, DT, 128, 128], F32),
                       ("wv", [DT, 128, D], F32), ("wo", [DT, DT, 128, 128], F32),
                       ("bq", [128, DT], F32), ("bk", [128, DT], F32),
                       ("bvr", [1, D], F32), ("bo", [128, DT], F32)]:
        a[k] = nc.dram_tensor(f"a1_{k}", shp, dt, kind="ExternalInput").ap()
    triu = nc.dram_tensor("triu", [128, 128], BF16, kind="ExternalInput").ap()
    f1 = {}
    for k, shp in [("wr", [128, DT * 8]), ("w0", [8, HT1, DT, 128, 128]),
                   ("w1", [8, HT1, DT, 128, 128]), ("wo", [8, HT1, 128, D])]:
        f1[k] = nc.dram_tensor(f"f1_{k}", shp, F32, kind="ExternalInput").ap()
    id8 = nc.dram_tensor("id8", [8, 8], F32, kind="ExternalInput").ap()
    idF = nc.dram_tensor("idF", [128, 128], F32, kind="ExternalInput").ap()
    trii = nc.dram_tensor("trii", [128, 128], F32, kind="ExternalInput").ap()
    eio = nc.dram_tensor("eio", [128, 8], F32, kind="ExternalInput").ap()
    tokio = nc.dram_tensor("tokio", [128, NT], I32, kind="ExternalInput").ap()
    ynat = nc.dram_tensor("ynat", [T, D], F32, kind="ExternalOutput").ap()
    x3s = nc.dram_tensor("x3s", [D, T], F32, kind="Internal").ap()
    x3nat = nc.dram_tensor("x3nat", [T + 1, D], F32, kind="Internal").ap()
    stok = nc.dram_tensor("stok", [NSLOT + 1, 1], I32, kind="Internal").ap()
    ydram = nc.dram_tensor("ydram", [NSLOT + 1, D], BF16, kind="Internal").ap()
    with ExitStack() as st:
        p = mk_prog(nc, st)
        a.update({"xT": xin, "xprefT": xpref, "pbias": pbias, "triu": triu, "yT": x3s})
        emit_attn(nc, p, a)
        emit_x3nat(nc, p, x3s, x3nat, idF)
        f1.update({"xT": x3s, "xnat": x3nat, "id8": id8, "idF": idF, "trii": trii,
                   "eio": eio, "tokio": tokio, "ynat": ynat, "stok": stok,
                   "ydram": ydram})
        emit_ffn1(nc, p, f1)
        block = st.enter_context(nc.Block())
        p.run(block)
    return nc


def kernel2(**inputs):
    x = inputs["L0_attn_x"].astype(np.float32, copy=False)
    aw = [prep_attn(0, inputs), prep_attn(1, inputs)]
    fw0 = prep_ffn0(inputs)
    fw1 = prep_ffn1(inputs)
    nc1 = _get_nc("fused_l0", build_fused_l0)
    nc2 = _get_nc("fused_l1", build_fused_l1)
    xs, xp = _chunks(x)
    in_maps = []
    for c in range(NCORES):
        pb = (np.zeros((128, 1), np.float32) if c % 2 == 1
              else np.full((128, 1), -30000.0, np.float32))
        m = {"xin": xs[c], "xpref0": xp[c], "pbias": pb, "triu": aw[0]["triu"],
             "id8": fw0["id8"], "idF": fw0["idF"]}
        for k in ["wq", "wk", "wv", "wo", "bq", "bk", "bvr", "bo"]:
            m[f"a0_{k}"] = aw[0][k]
        for k in ["wr", "br", "w1", "b1", "w2", "b2", "sel8"]:
            m[f"f0_{k}"] = fw0[k]
        in_maps.append(m)
    res = run_bass_kernel_spmd(nc1, in_maps, core_ids=list(range(NCORES)))
    x2c = [np.asarray(res.results[c]["yT"]).reshape(D, T) for c in range(NCORES)]
    in_maps = []
    for c in range(NCORES):
        pb = (np.zeros((128, 1), np.float32) if c % 2 == 1
              else np.full((128, 1), -30000.0, np.float32))
        m = {"xin": x2c[c], "xpref": x2c[c - c % 2], "pbias": pb,
             "triu": aw[1]["triu"], "id8": fw0["id8"], "idF": fw0["idF"],
             "trii": fw1["trii"], "eio": fw1["eio"], "tokio": fw1["tokio"]}
        for k in ["wq", "wk", "wv", "wo", "bq", "bk", "bvr", "bo"]:
            m[f"a1_{k}"] = aw[1][k]
        for k in ["wr", "w0", "w1", "wo"]:
            m[f"f1_{k}"] = fw1[k]
        in_maps.append(m)
    res = run_bass_kernel_spmd(nc2, in_maps, core_ids=list(range(NCORES)))
    out = np.empty((4, 4096, D), np.float32)
    for c in range(NCORES):
        b, h = c // 2, c % 2
        out[b, h * T:(h + 1) * T, :] = np.asarray(res.results[c]["ynat"]).reshape(T, D)
    return out

def kernel_unfused(**inputs):
    x = inputs["L0_attn_x"].astype(np.float32, copy=False)
    aw0 = prep_attn(0, inputs)
    fw0 = prep_ffn0(inputs)
    aw1 = prep_attn(1, inputs)
    fw1 = prep_ffn1(inputs)
    x1 = _run_attn(x, aw0)
    x2 = _run_ffn0(x1, fw0)
    x3 = _run_attn(x2, aw1)
    x4 = _run_ffn1(x3, fw1)
    return x4


def kernel(**inputs):
    return kernel_1launch(**inputs)


def kernel_1launch(**inputs):
    x = inputs["L0_attn_x"].astype(np.float32, copy=False)
    aw = [prep_attn(0, inputs), prep_attn(1, inputs)]
    fw0 = prep_ffn0(inputs)
    fw1 = prep_ffn1(inputs)
    nc = _get_nc("fused", build_fused)
    xs, xp = _chunks(x)
    in_maps = []
    for c in range(NCORES):
        pb = (np.zeros((128, 1), np.float32) if c % 2 == 1
              else np.full((128, 1), -30000.0, np.float32))
        m = {"xin": xs[c], "xpref0": xp[c], "pbias": pb,
             "pbneg": np.full((128, 1), -30000.0, np.float32),
             "triu": aw[0]["triu"],
             "id8": fw0["id8"], "idF": fw0["idF"], "trii": fw1["trii"],
             "eio": fw1["eio"], "tokio": fw1["tokio"]}
        for l in range(2):
            for k in ["wq", "wk", "wv", "wo", "bq", "bk", "bvr", "bo"]:
                m[f"a{l}_{k}"] = aw[l][k]
        for k in ["wr", "br", "w1", "b1", "w2", "b2", "sel8"]:
            m[f"f0_{k}"] = fw0[k]
        for k in ["wr", "w0", "w1", "wo"]:
            m[f"f1_{k}"] = fw1[k]
        in_maps.append(m)
    res = run_bass_kernel_spmd(nc, in_maps, core_ids=list(range(NCORES)))
    out = np.empty((4, 4096, D), np.float32)
    for c in range(NCORES):
        b, h = c // 2, c % 2
        out[b, h * T:(h + 1) * T, :] = np.asarray(res.results[c]["ynat"]).reshape(T, D)
    return out
